# version 30
# baseline (speedup 1.0000x reference)
import numpy as np
import ml_dtypes
import concourse.bass as bass
import concourse.mybir as mybir
from concourse.bass_utils import run_bass_kernel_spmd

F32 = mybir.dt.float32
BF16 = mybir.dt.bfloat16
I8 = mybir.dt.int8
AF = mybir.ActivationFunctionType
ALU = mybir.AluOpType
AX = mybir.AxisListType
NPBF = ml_dtypes.bfloat16
f32 = np.float32

SHIFT = 16.0
EPS = 1e-5
SB_LIMIT = 229344


class Sched:

    def __init__(self):
        self.steps = []
        self.lw = {}
        self.readers = {}

    def step(self, eng, fn, reads=(), writes=(), dma_sem=None, name="",
             bulk=False, dur=0.12, lat=None, table=None):
        idx = len(self.steps)
        deps = set()
        for r in reads:
            deps.update(self.lw.get(r, {}).values())
        for w in writes:
            deps.update(self.lw.get(w, {}).values())
            deps.update(self.readers.get(w, ()))
        deps.discard(idx)
        if dma_sem is not None and lat is None:
            dur, lat = 0.1, 2.5
        st = dict(idx=idx, eng=eng, fn=fn, deps=deps, dma_sem=dma_sem,
                  name=name, needed=False, inc_val=None, bulk=bulk,
                  dur=dur, lat=lat if lat is not None else 0.0, table=table)
        self.steps.append(st)
        key = dma_sem if dma_sem is not None else eng
        for w in writes:
            self.lw[w] = {key: idx}
            self.readers[w] = []
        for r in reads:
            self.readers.setdefault(r, []).append(idx)
        return idx

    def reorder(self, fence):
        n = len(self.steps)
        succ = [[] for _ in range(n)]
        indeg = [0] * n
        LAT = 0.4
        for st in self.steps[fence:]:
            for d in st["deps"]:
                if d >= fence:
                    succ[d].append(st["idx"])
                    indeg[st["idx"]] += 1
        rank = [0.0] * n
        for st in reversed(self.steps[fence:]):
            i = st["idx"]
            r = 0.0
            for s2 in succ[i]:
                x = rank[s2] + (LAT if self.steps[s2]["eng"] != st["eng"]
                                else 0.0)
                if x > r:
                    r = x
            rank[i] = r + st["dur"] + st["lat"]
        est = [0.0] * n
        fin = [0.0] * n
        ready = {}
        engt = {}
        for st in self.steps[fence:]:
            if indeg[st["idx"]] == 0:
                ready.setdefault(st["eng"], []).append(st["idx"])
        order = []
        total = n - fence
        cur_table = {}
        TBL_COST = 0.0
        while len(order) < total:
            best = None
            for eng, lst in ready.items():
                if not lst:
                    continue
                t0 = engt.get(eng, 0.0)
                s0 = min(max(t0, est[i]) for i in lst)
                if best is None or s0 < best[0]:
                    best = (s0, eng)
            s0, eng = best
            lst = ready[eng]
            cand = [i for i in lst if max(engt.get(eng, 0.0), est[i]) <= s0 + 1e-9]
            ctab = cur_table.get(eng)
            def prio(j):
                t = self.steps[j]["table"]
                pen = TBL_COST if (t is not None and ctab is not None
                                   and t != ctab) else 0.0
                return (rank[j] - pen, -j)
            i = max(cand, key=prio)
            lst.remove(i)
            st = self.steps[i]
            start = max(engt.get(eng, 0.0), est[i])
            sw = 0.0
            if st["table"] is not None:
                if cur_table.get(eng) not in (None, st["table"]):
                    sw = TBL_COST
                cur_table[eng] = st["table"]
            f = start + st["dur"] + sw
            engt[eng] = f
            fin[i] = f + st["lat"]
            order.append(st)
            for s2 in succ[i]:
                st2 = self.steps[s2]
                x = fin[i] + (LAT if (st2["eng"] != eng
                                      or st["dma_sem"] is not None) else 0.0)
                if x > est[s2]:
                    est[s2] = x
                indeg[s2] -= 1
                if indeg[s2] == 0:
                    ready.setdefault(st2["eng"], []).append(s2)
        self.steps = self.steps[:fence] + order
        self.span_est = max(engt.values()) if engt else 0.0

    def finalize(self):
        self.by_idx = {st["idx"]: st for st in self.steps}
        for st in self.steps:
            for d in st["deps"]:
                dep = self.by_idx[d]
                if (dep["eng"] == st["eng"] == "tensor"
                        and dep["dma_sem"] is None and st["dma_sem"] is None):
                    continue
                dep["needed"] = True
        counters = {}
        for st in self.steps:
            key = st["dma_sem"] if st["dma_sem"] is not None else st["eng"]
            if st["dma_sem"] is not None:
                counters[key] = counters.get(key, 0) + 16
                st["inc_val"] = counters[key]
            elif st["needed"]:
                counters[key] = counters.get(key, 0) + 1
                st["inc_val"] = counters[key]
        self.sem_totals = dict(counters)
        self.bulk_sems = {st["dma_sem"] for st in self.steps
                         if st["dma_sem"] is not None and st["bulk"]}

    def emit(self, eng_name, eng, sems):
        watermark = {}
        for st in self.steps:
            if st["eng"] != eng_name:
                continue
            waits = {}
            for d in sorted(st["deps"]):
                dep = self.by_idx[d]
                if (dep["eng"] == eng_name == "tensor"
                        and dep["dma_sem"] is None and st["dma_sem"] is None):
                    continue
                key = dep["dma_sem"] if dep["dma_sem"] is not None else dep["eng"]
                assert dep["inc_val"] is not None, \
                    f"{st['name']} deps on {dep['name']} without inc"
                val = self.sem_totals[key] if key in self.bulk_sems else dep["inc_val"]
                waits[key] = max(waits.get(key, 0), val)
            for key in sorted(waits):
                if watermark.get(key, -1) < waits[key]:
                    eng.wait_ge(sems[key], waits[key])
                    watermark[key] = waits[key]
            inst = st["fn"](eng)
            if st["inc_val"] is not None:
                key = st["dma_sem"] if st["dma_sem"] is not None else st["eng"]
                assert inst is not None, f"{st['name']} needs inc, fn returned None"
                inst.then_inc(sems[key], 16 if st["dma_sem"] is not None else 1)


class Bump:
    def __init__(self, base):
        self.o = base

    def take(self, nbytes, name=""):
        r = self.o
        self.o = (r + nbytes + 31) & ~31
        assert self.o <= SB_LIMIT, f"SBUF overflow at {name}: {self.o}"
        return r


def build_program(S=4096, dbg=()):
    E, HID, H, D = 1024, 4096, 16, 64
    NP = H // 2
    TOK = S // 2
    NKB = S // 128
    NQB = TOK // 512
    NTIL = TOK // 128
    EC = E // 128
    HC = HID // 128
    NTG = S // 512
    WN = HID * E

    nc = bass.Bass("TRN2", target_bir_lowering=False)
    global SB_LIMIT
    SB_BASE = (nc._sbuf_addr_for_side("left") + 31) & ~31
    SB_LIMIT = nc._sbuf_addr_for_side("right")

    xkv_d = nc.dram_tensor("xkv", [S, E], F32, kind="ExternalInput")
    xq_d = nc.dram_tensor("xq", [TOK, E], F32, kind="ExternalInput")
    qkvw_d = nc.dram_tensor("qkv_w", [E, 3 * E], F32, kind="ExternalInput")
    qkvb_d = nc.dram_tensor("qkvb", [3, E], F32, kind="ExternalInput")
    outw_d = nc.dram_tensor("out_w", [E, E], F32, kind="ExternalInput")
    outb_d = nc.dram_tensor("out_b", [1, E], F32, kind="ExternalInput")
    w1_d = nc.dram_tensor("w1", [HID, E], F32, kind="ExternalInput")
    b1_d = nc.dram_tensor("b1", [1, HID], F32, kind="ExternalInput")
    w2_d = nc.dram_tensor("w2", [E, HID], F32, kind="ExternalInput")
    b2_d = nc.dram_tensor("b2", [1, E], F32, kind="ExternalInput")
    ln1w_d = nc.dram_tensor("ln1w", [1, E], F32, kind="ExternalInput")
    ln1b_d = nc.dram_tensor("ln1b", [1, E], F32, kind="ExternalInput")
    ln2w_d = nc.dram_tensor("ln2w", [1, HID], F32, kind="ExternalInput")
    ln2b_d = nc.dram_tensor("ln2b", [1, HID], F32, kind="ExternalInput")
    ident_d = nc.dram_tensor("ident", [128, 128], BF16, kind="ExternalInput")
    onesc_d = nc.dram_tensor("onesc", [128, 64], BF16, kind="ExternalInput")
    onesr_d = nc.dram_tensor("onesr", [1, 128], BF16, kind="ExternalInput")
    onesf_d = nc.dram_tensor("onesf", [128, 1], F32, kind="ExternalInput")
    dmask_d = nc.dram_tensor("dmask", [128, 64], BF16, kind="ExternalInput")
    out_d = nc.dram_tensor("out", [TOK, E], F32, kind="ExternalOutput")
    w2sp_d = nc.dram_tensor("w2spill", [HID, E], BF16)

    dbg_d = {}
    for nm, shape in dbg:
        dbg_d[nm] = nc.dram_tensor("dbg_" + nm, list(shape), F32,
                                   kind="ExternalOutput")

    def sbat(name, shape, dtype, offset):
        nbytes = int(np.prod(shape[1:])) * mybir.dt.size(dtype)
        assert offset + nbytes <= SB_LIMIT, f"{name}: {offset}+{nbytes}"
        return nc.alloc_sbuf_tensor_at(name, list(shape), dtype, offset=offset)

    KVX_BASE = SB_BASE
    kvx = sbat("kvx", [128, NKB * 2048], BF16, KVX_BASE)
    KVX_END = KVX_BASE + NKB * 4096

    def kt_ap(kb, pair, r0, r1):
        c = kb * 2048 + pair * 128
        return kvx[r0:r1, c: c + 128]

    def v_ap(kb, h):
        c = kb * 2048 + 1024 + h * 64
        return kvx[:, c: c + 64]

    def x1_ap(qb, tt, c0=0, c1=1024):
        c = 8 * qb * 2048 + tt * 1024
        return kvx[:, c + c0: c + c1]

    x1_slots = set()
    for j in range(NQB):
        x1_slots.update((8 * j, 8 * j + 1))
    free_slots = [p for p in range(NKB) if p not in x1_slots]
    w1t_pairs = []
    i = 0
    while i + 1 < len(free_slots) and len(w1t_pairs) < EC:
        if free_slots[i + 1] == free_slots[i] + 1:
            w1t_pairs.append(free_slots[i])
            i += 2
        else:
            i += 1
    used = set()
    for p in w1t_pairs:
        used.update((p, p + 1))
    w2_kvx_slots = [p for p in free_slots if p not in used]
    H2OFF = None
    for ii in range(len(w2_kvx_slots) - 1):
        if w2_kvx_slots[ii + 1] == w2_kvx_slots[ii] + 1:
            H2OFF = w2_kvx_slots[ii] * 2048
            del w2_kvx_slots[ii:ii + 2]
            break
    WST_EXTRA = []
    if len(w2_kvx_slots) >= 4:
        for _ in range(2):
            sl2 = w2_kvx_slots.pop()
            WST_EXTRA += [sl2 * 2048, sl2 * 2048 + 1024]
    NSTSL = 2 + len(WST_EXTRA)

    PASSA_EARLY = NQB >= 2
    wstA = None
    if PASSA_EARLY:
        PA_BSL = 8 * (NQB - 1) + 2
        PA_TGS = (2 * (NQB - 1), 2 * NQB - 1)
        wstA = sbat("wstA", [128, 4096], F32, KVX_BASE + PA_BSL * 4096)

    cb = Bump(KVX_END)
    ident = sbat("identb", [128, 128], BF16, cb.take(256, "id"))
    onesc = sbat("onescb", [128, 64], BF16, cb.take(128, "oc"))
    onesr = sbat("onesrb", [1, 128], BF16, cb.take(256, "or"))
    onesf = sbat("onesfb", [128, 1], F32, cb.take(32, "of"))
    dmask = sbat("dmaskb", [128, 64], BF16, cb.take(128, "dm"))
    qkvb_s = sbat("qkvb_s", [128, 3 * EC], F32, cb.take(3 * EC * 4, "qb"))
    cvals = sbat("cvals", [128, 128], F32, cb.take(512, "cv"))
    cvx = sbat("cvx", [128, 64], F32, cb.take(64 * 4, "cvx"))
    lnst = sbat("lnst", [128, 48], F32, cb.take(48 * 4, "lnst"))
    LBASE = cb.o

    a = Bump(LBASE)
    qkvw_sb = sbat("qkvw_sb", [128, EC * 3 * E], BF16, a.take(EC * 3 * E * 2, "qw"))
    xkvbf = sbat("xkvbf", [128, 2 * 4 * E], BF16, a.take(2 * 4 * E * 2, "xk"))
    xT1 = sbat("xT1", [128, EC * 512], BF16, a.take(EC * 512 * 2, "xT1"))
    vtmp = sbat("vtmp", [128, 512], BF16, a.take(512 * 2, "vt"))

    a = Bump(LBASE)
    qkvwQ = sbat("qkvwQ", [128, EC * E], BF16, a.take(EC * E * 2, "qQ"))
    outw_sb = sbat("outw_sb", [128, EC * E], BF16, a.take(EC * E * 2, "ow"))
    xqbf_off = a.take(4 * E * 2, "xq")
    xqbf = sbat("xqbf", [128, 4 * E], BF16, xqbf_off)
    xres = sbat("xres", [128, 2 * E], F32, xqbf_off)
    xT2_off = a.take(EC * 512 * 2, "xT2")
    xT2 = sbat("xT2", [128, EC * 512], BF16, xT2_off)
    trash2 = sbat("trash2", [128, E], F32, xT2_off)
    QT = sbat("QT", [128, NP * 512], BF16, a.take(NP * 512 * 2, "QT"))
    ptb = sbat("ptb", [128, 2 * 1024], BF16, a.take(2 * 1024 * 2, "pt"))
    attnT = sbat("attnT", [128, NP * 512], BF16, a.take(NP * 512 * 2, "aT"))
    recip = sbat("recip", [128, 512], F32, a.take(512 * 4, "rc"))
    outbr = sbat("outbr", [1, E], BF16, a.take(E * 2, "ob"))
    L2_END = a.o

    a = Bump(LBASE)
    wbuf = sbat("wbuf", [128, 2 * HID], F32, a.take(2 * HID * 4, "wb"))
    ybuf = sbat("ybuf", [128, 2048], F32, a.take(2048 * 4, "yb"))
    gbuf = sbat("gbuf", [128, 2048], BF16, a.take(2048 * 2, "gb"))
    ternb = sbat("ternb", [128, 2 * HID], BF16, a.take(2 * HID * 2, "tb"))
    wstage = sbat("wstage", [128, 2 * 1024], BF16, a.take(2 * 1024 * 2, "ws"))
    L3_END = a.o

    a = Bump(LBASE)
    ln2w_r = sbat("ln2w_r", [128, HID], BF16, a.take(HID * 2, "l2w"))
    ln2b_r = sbat("ln2b_r", [128, HID], BF16, a.take(HID * 2, "l2b"))
    ln1w_r = sbat("ln1w_r", [128, E], BF16, a.take(E * 2, "l1w"))
    ln1b_r = sbat("ln1b_r", [128, E], BF16, a.take(E * 2, "l1b"))
    h2b = sbat("h2b", [128, HID], BF16, a.take(HID * 2, "h2"))
    h2c = None
    abuf = sbat("abuf", [128, 4096], BF16, a.take(4096 * 2, "ab"))
    bbuf = sbat("bbuf", [128, 4096], BF16, a.take(4096 * 2, "bb"))
    ubuf = sbat("ubuf", [128, 4096], I8, a.take(4096, "ub"))
    q1t = sbat("q1t", [128, 2 * E], BF16, a.take(2 * E * 2, "q1t"))
    q2t = sbat("q2t", [128, 2 * E], BF16, a.take(2 * E * 2, "q2t"))
    rbuf = sbat("rbuf", [128, E], F32, a.take(E * 4, "rb4"))
    obuf = sbat("obuf", [128, E], F32, a.take(E * 4, "ob4"))
    wsbuf = sbat("wsbuf", [128, 2 * E], BF16, a.take(2 * E * 2, "wsb"))
    if H2OFF is None:
        h2c = sbat("h2c", [128, HID], BF16, a.take(HID * 2, "h2c"))
    n_w1_ext0 = EC - len(w1t_pairs)
    w1ext = None
    if n_w1_ext0 > 0:
        w1ext = sbat("w1ext", [128, n_w1_ext0 * HID], BF16,
                     a.take(n_w1_ext0 * HID * 2, "w1e"))
    a = Bump(max(L3_END, a.o))
    b1c = sbat("b1c", [1, HID], BF16, a.take(HID * 2, "b1c"))
    b2c = sbat("b2c", [1, E], BF16, a.take(E * 2, "b2c"))
    n_res_rb = max(0, min(HC - len(w2_kvx_slots) * 2,
                          (SB_LIMIT - a.o - 64) // (E * 2))) & ~7
    assert (len(w2_kvx_slots) * 2 + n_res_rb) % 8 == 0
    w2rb = None
    if n_res_rb > 0:
        w2rb = sbat("w2rb", [128, n_res_rb * E], BF16,
                    a.take(n_res_rb * E * 2, "w2rb"))
    L4_END = a.o

    w2place = {}
    half = 0
    for sl in w2_kvx_slots:
        for hh in range(2):
            if half < HC:
                w2place[half] = ("kvx", sl * 2048 + hh * 1024)
                half += 1
    for i2 in range(n_res_rb):
        if half < HC:
            w2place[half] = ("rb", i2)
            half += 1
    while half < HC:
        w2place[half] = ("stream", None)
        half += 1
    w2_streamed = [j for j in range(HC) if w2place[j][0] == "stream"]

    def w2_home_ap(j, c0, c1):
        kind, arg = w2place[j]
        if kind == "kvx":
            return kvx[:, arg + c0: arg + c1]
        if kind == "rb":
            return w2rb[:, arg * E + c0: arg * E + c1]
        return None

    def wst_ap(ssl, c0=0, c1=E):
        if ssl < 2:
            return wsbuf[:, ssl * E + c0: ssl * E + c1]
        off = WST_EXTRA[ssl - 2]
        return kvx[:, off + c0: off + c1]

    def w1t_ap(k, c0, c1):
        if k < len(w1t_pairs):
            base = w1t_pairs[k] * 2048
            return kvx[:, base + c0: base + c1]
        kk = k - len(w1t_pairs)
        return w1ext[:, kk * HID + c0: kk * HID + c1]

    ps = [nc.alloc_psum_tensor(f"ps{i}", [128, 512], F32) for i in range(6)]
    pt = [nc.alloc_psum_tensor(f"pt{i}", [128, 1024], BF16) for i in range(2)]

    sd = Sched()
    c0125 = f32(0.125)
    cSH = f32(-SHIFT)

    def dma(eng_name, dst, src, sem, *, reads=(), writes=(), name="",
            bulk=False, slow=False, lat=None):
        def fn(eng, dst=dst, src=src, slow=slow):
            if slow:
                return eng.dma_start(dst, src, allow_slow_non_contiguous=True)
            return eng.dma_start(dst, src)
        return sd.step(eng_name, fn, reads=reads, writes=writes, dma_sem=sem,
                       name=name, bulk=bulk, lat=lat)

    def kv_names(tg):
        return ([f"KV{tg}K{p}" for p in range(NP)] +
                [f"KV{tg}V{p}" for p in range(NP)])

    for t, d, nm in ((ident, ident_d, "identb"), (onesc, onesc_d, "onescb"),
                     (onesr, onesr_d, "onesrb"), (onesf, onesf_d, "onesfb"),
                     (dmask, dmask_d, "dmaskb")):
        dma("sync", t[:, :], d[:, :], "dc", writes=[nm],
            name="ld_" + nm, bulk=True)
    dma("sync", qkvb_s[:, :],
        qkvb_d[:, :].rearrange("t (c p) -> p (t c)", p=128),
        "dc", writes=["qkvb_s"], name="ld_qkvb", bulk=True, slow=True)
    for ec in range(EC):
        dma("gpsimd", qkvw_sb[:, ec * 3 * E:(ec + 1) * 3 * E],
            qkvw_d[ec * 128:(ec + 1) * 128, :], "dw",
            writes=[f"qkvw{ec}"], name=f"ld_qkvw{ec}", bulk=True, lat=8.0)

    for tg in range(NTG):
        sl = tg % 2
        dst_ap = bass.AP(xkvbf, sl * 4 * E, [[2 * 4 * E, 128], [E, 4], [1, E]])
        src_ap = bass.AP(xkv_d, tg * 512 * E, [[E, 128], [128 * E, 4], [1, E]])
        dma("gpsimd", dst_ap, src_ap, f"dx{sl}",
            reads=[f"xkvbf{sl}"], writes=[f"xkvbf{sl}"], name=f"p1ldx{tg}", lat=12.0)
        for ecn in range(EC):
            bk = ecn % 2
            def ftr(eng, sl=sl, ecn=ecn, bk=bk):
                r = None
                for t4 in range(4):
                    r = eng.transpose(
                        pt[bk][:, t4 * 128:(t4 + 1) * 128],
                        xkvbf[:, sl * 4 * E + t4 * E + ecn * 128:
                                 sl * 4 * E + t4 * E + (ecn + 1) * 128],
                        ident[:, :])
                return r
            sd.step("tensor", ftr,
                    reads=[f"xkvbf{sl}", "identb", f"pt{bk}"],
                    writes=[f"pt{bk}"], name=f"p1xt{tg}_{ecn}", dur=0.9)
            def fev(eng, ecn=ecn, bk=bk):
                return eng.tensor_copy(xT1[:, ecn * 512:(ecn + 1) * 512],
                                       pt[bk][:, 0:512])
            sd.step("vector", fev, reads=[f"pt{bk}"], writes=[f"xT1_{ecn}"],
                    name=f"p1xe{tg}_{ecn}", dur=0.35)
        for p in range(NP):
            for typ in (1, 2):
                bk = (p * 2 + typ - 1) % 2
                def fmm(eng, p=p, typ=typ, bk=bk):
                    r = None
                    for ec in range(EC):
                        r = eng.matmul(
                            ps[bk][:, 0:512],
                            qkvw_sb[:, ec * 3 * E + typ * E + p * 128:
                                       ec * 3 * E + typ * E + (p + 1) * 128],
                            xT1[:, ec * 512:(ec + 1) * 512],
                            start=(ec == 0), stop=(ec == EC - 1))
                    return r
                sd.step("tensor", fmm,
                        reads=[f"qkvw{e2}" for e2 in range(EC)] +
                              [f"xT1_{e2}" for e2 in range(EC)] + [f"ps{bk}"],
                        writes=[f"ps{bk}"], name=f"p1mm{tg}_{p}_{typ}", dur=2.0)
                if typ == 1:
                    def fke(eng, tg=tg, p=p, bk=bk):
                        dst = bass.AP(kvx, (tg * 4) * 2048 + p * 128,
                                      [[NKB * 2048, 128], [2048, 4], [1, 128]])
                        return eng.activation(dst, ps[bk][:, 0:512], AF.Identity,
                                              bias=qkvb_s[:, EC + p:EC + p + 1],
                                              scale=f32(1.0))
                    sd.step("scalar", fke, reads=[f"ps{bk}", "qkvb_s"],
                            writes=[f"KV{tg}K{p}"], name=f"p1ke{tg}_{p}", dur=0.45)
                else:
                    def fve(eng, p=p, bk=bk):
                        return eng.activation(
                            vtmp[:, 0:512], ps[bk][:, 0:512], AF.Identity,
                            bias=qkvb_s[:, 2 * EC + p:2 * EC + p + 1],
                            scale=f32(1.0))
                    sd.step("scalar", fve, reads=[f"ps{bk}", "qkvb_s", "vtmp"],
                            writes=["vtmp"], name=f"p1ve{tg}_{p}", dur=0.45)
                    vbk = p % 2
                    def fvt(eng, vbk=vbk):
                        r = None
                        for kq in range(4):
                            r = eng.transpose(
                                pt[vbk][:, kq * 128:(kq + 1) * 128],
                                vtmp[:, kq * 128:(kq + 1) * 128], ident[:, :])
                        return r
                    sd.step("tensor", fvt, reads=["vtmp", "identb", f"pt{vbk}"],
                            writes=[f"pt{vbk}"], name=f"p1vt{tg}_{p}", dur=0.5)
                    def fvv(eng, tg=tg, p=p, vbk=vbk):
                        dst = bass.AP(kvx, (tg * 4) * 2048 + 1024 + p * 128,
                                      [[NKB * 2048, 128], [2048, 4], [1, 128]])
                        return eng.tensor_copy(dst, pt[vbk][:, 0:512])
                    sd.step("vector", fvv, reads=[f"pt{vbk}"],
                            writes=[f"KV{tg}V{p}"], name=f"p1vv{tg}_{p}", dur=0.35)

    L1_NAMES = (["xkvbf0", "xkvbf1", "vtmp"] +
                [f"qkvw{e2}" for e2 in range(EC)] +
                [f"xT1_{e2}" for e2 in range(EC)])
    L2_NAMES = (["xqbf", "xres0", "xres1", "trash2", "ptb0", "ptb1", "recip",
                 "outbrow"] +
                [f"xT2_{e2}" for e2 in range(EC)] +
                [f"qwQ{e2}" for e2 in range(EC)] +
                [f"ow{e2}" for e2 in range(EC)] +
                [f"QT{p}" for p in range(NP)] +
                [f"attnT{p}" for p in range(NP)])
    def fbar12(eng):
        return eng.memset(cvals[:, 23:24], 0.0)
    sd.step("vector", fbar12, writes=L1_NAMES + L2_NAMES + ["cv23"], name="bar12")

    for ec in range(EC):
        dma("gpsimd", qkvwQ[:, ec * E:(ec + 1) * E],
            qkvw_d[ec * 128:(ec + 1) * 128, 0:E], "dw2",
            reads=["cv23"], writes=[f"qwQ{ec}"], name=f"ld_qwQ{ec}", bulk=True, lat=5.0)
    for ch in range(EC):
        dma("gpsimd", outw_sb[:, ch * E:(ch + 1) * E],
            outw_d[ch * 128:(ch + 1) * 128, :], "dw2",
            reads=["cv23"], writes=[f"ow{ch}"], name=f"ld_ow{ch}", bulk=True, lat=5.0)
    dma("gpsimd", outbr[0:1, :], outb_d[0:1, :], "dw2",
        reads=["cv23"], writes=["outbrow"], name="ld_outb", bulk=True)

    for qb in range(NQB - 1, -1, -1):
        nkb = 8 * (qb + 1)
        qraw = qb * 1024
        dst_ap = bass.AP(xqbf, 0, [[4 * E, 128], [E, 4], [1, E]])
        src_ap = bass.AP(xq_d, qb * 512 * E, [[E, 128], [128 * E, 4], [1, E]])
        dma("gpsimd", dst_ap, src_ap, "dq",
            reads=[], writes=["xqbf", "xres0", "xres1"], name=f"q{qb}ldx", lat=12.0)
        for ecn in range(EC):
            bk = ecn % 2
            def ftr(eng, ecn=ecn, bk=bk):
                r = None
                for t4 in range(4):
                    r = eng.transpose(
                        pt[bk][:, t4 * 128:(t4 + 1) * 128],
                        xqbf[:, t4 * E + ecn * 128: t4 * E + (ecn + 1) * 128],
                        ident[:, :])
                return r
            sd.step("tensor", ftr, reads=["xqbf", "identb", f"pt{bk}"],
                    writes=[f"pt{bk}"], name=f"q{qb}xt{ecn}", dur=0.9)
            def fev(eng, ecn=ecn, bk=bk):
                return eng.tensor_copy(xT2[:, ecn * 512:(ecn + 1) * 512],
                                       pt[bk][:, 0:512])
            sd.step("vector", fev, reads=[f"pt{bk}", "trash2"],
                    writes=[f"xT2_{ecn}"], name=f"q{qb}xe{ecn}", dur=0.35)
        for p in range(NP):
            bk = p % 2
            def fmm(eng, p=p, bk=bk):
                r = None
                for ec in range(EC):
                    r = eng.matmul(
                        ps[bk][:, 0:512],
                        qkvwQ[:, ec * E + p * 128: ec * E + (p + 1) * 128],
                        xT2[:, ec * 512:(ec + 1) * 512],
                        start=(ec == 0), stop=(ec == EC - 1))
                return r
            sd.step("tensor", fmm,
                    reads=[f"qwQ{e2}" for e2 in range(EC)] +
                          [f"xT2_{e2}" for e2 in range(EC)] + [f"ps{bk}"],
                    writes=[f"ps{bk}"], name=f"q{qb}qp{p}", dur=2.0)
            def fqe(eng, p=p, bk=bk):
                return eng.activation(QT[:, p * 512:(p + 1) * 512],
                                      ps[bk][:, 0:512], AF.Identity,
                                      bias=qkvb_s[:, p:p + 1], scale=f32(1.0))
            sd.step("scalar", fqe, reads=[f"ps{bk}", "qkvb_s", f"QT{p}"],
                    writes=[f"QT{p}"], name=f"q{qb}qe{p}", dur=0.45)
        for p in range(NP):
            def mk_scores(kb, p=p):
                off = kb * 128 - qraw
                noff = max(0, off) // 2
                sA, sB = 2 * ((kb + 1) % 2), 2 * ((kb + 1) % 2) + 1
                tgk = kb // 4
                def fsc(eng, p=p, kb=kb, noff=noff, sA=sA, sB=sB):
                    eng.matmul(ps[sA][:, noff:512], kt_ap(kb, p, 0, 64),
                               QT[0:64, p * 512 + noff:(p + 1) * 512],
                               start=True, stop=True, tile_position=(0, 0))
                    return eng.matmul(ps[sB][:, noff:512], kt_ap(kb, p, 64, 128),
                                      QT[64:128, p * 512 + noff:(p + 1) * 512],
                                      start=True, stop=True,
                                      tile_position=(64, 0))
                sd.step("tensor", fsc,
                        reads=[f"KV{tgk}K{p}", f"QT{p}", f"ps{sA}", f"ps{sB}"],
                        writes=[f"ps{sA}", f"ps{sB}"], name=f"q{qb}s{p}_{kb}", dur=0.42)
            mk_scores(0)
            for kb in range(nkb):
                off = kb * 128 - qraw
                noff = max(0, off) // 2
                sA, sB = 2 * ((kb + 1) % 2), 2 * ((kb + 1) % 2) + 1
                ptA = (kb % 2) * 1024
                ptB = ptA + 512
                tgk = kb // 4
                def fex(eng, noff=noff, sA=sA, sB=sB, ptA=ptA, ptB=ptB):
                    eng.activation(ptb[:, ptA + noff: ptA + 512],
                                   ps[sA][:, noff:512], AF.Exp,
                                   bias=cSH, scale=c0125)
                    return eng.activation(ptb[:, ptB + noff: ptB + 512],
                                          ps[sB][:, noff:512], AF.Exp,
                                          bias=cSH, scale=c0125)
                sd.step("scalar", fex,
                        reads=[f"ps{sA}", f"ps{sB}", f"ptb{kb % 2}"],
                        writes=[f"ptb{kb % 2}"], name=f"q{qb}e{p}_{kb}", dur=0.92, table="log")
                if off >= 0:
                    def fmk(eng, noff=noff, ptA=ptA, ptB=ptB):
                        eng.tensor_mul(ptb[:, ptA + noff: ptA + noff + 64],
                                       ptb[:, ptA + noff: ptA + noff + 64],
                                       dmask[:, :])
                        return eng.tensor_mul(
                            ptb[:, ptB + noff: ptB + noff + 64],
                            ptb[:, ptB + noff: ptB + noff + 64], dmask[:, :])
                    sd.step("vector", fmk, reads=["dmaskb", f"ptb{kb % 2}"],
                            writes=[f"ptb{kb % 2}"], name=f"q{qb}m{p}_{kb}", dur=0.2)
                if kb + 1 < nkb:
                    mk_scores(kb + 1)
                def fl(eng, kb=kb, nkb=nkb, noff=noff, ptA=ptA, ptB=ptB):
                    eng.matmul(ps[5][0:64, noff:512], onesc[:, :],
                               ptb[:, ptA + noff: ptA + 512],
                               start=(kb == 0), stop=(kb == nkb - 1),
                               tile_position=(0, 0), skip_group_check=True)
                    return eng.matmul(ps[5][64:128, noff:512], onesc[:, :],
                                      ptb[:, ptB + noff: ptB + 512],
                                      start=(kb == 0), stop=(kb == nkb - 1),
                                      tile_position=(0, 64),
                                      skip_group_check=True)
                sd.step("tensor", fl, reads=["onescb", f"ptb{kb % 2}", "ps5"],
                        writes=["ps5"], name=f"q{qb}l{p}_{kb}", dur=0.4)
                def fav(eng, p=p, kb=kb, nkb=nkb, noff=noff, ptA=ptA, ptB=ptB):
                    eng.matmul(ps[4][0:64, noff:512], v_ap(kb, 2 * p),
                               ptb[:, ptA + noff: ptA + 512],
                               start=(kb == 0), stop=(kb == nkb - 1),
                               tile_position=(0, 0), skip_group_check=True)
                    return eng.matmul(ps[4][64:128, noff:512],
                                      v_ap(kb, 2 * p + 1),
                                      ptb[:, ptB + noff: ptB + 512],
                                      start=(kb == 0), stop=(kb == nkb - 1),
                                      tile_position=(0, 64),
                                      skip_group_check=True)
                sd.step("tensor", fav,
                        reads=[f"KV{tgk}V{p}", f"ptb{kb % 2}", "ps4"],
                        writes=["ps4"], name=f"q{qb}av{p}_{kb}", dur=0.4)
            def fln(eng):
                return eng.activation(recip[:, :], ps[5][:, 0:512], AF.Ln,
                                      bias=f32(0.0), scale=f32(1.0))
            sd.step("scalar", fln, reads=["ps5", "recip"], writes=["recip"],
                    name=f"q{qb}rl{p}", dur=0.45, table="log")
            def fexr(eng):
                return eng.activation(recip[:, :], recip[:, :], AF.Exp,
                                      bias=f32(0.0), scale=f32(-1.0))
            sd.step("scalar", fexr, reads=["recip"], writes=["recip"],
                    name=f"q{qb}re{p}", dur=0.45, table="log")
            def fat(eng, p=p):
                return eng.tensor_mul(attnT[:, p * 512:(p + 1) * 512],
                                      ps[4][:, 0:512], recip[:, :])
            sd.step("vector", fat, reads=["ps4", "recip", f"attnT{p}"],
                    writes=[f"attnT{p}"], name=f"q{qb}at{p}", dur=0.35)
        for tt in range(4):
            sl = tt % 2
            dma("sync", xres[:, sl * E:(sl + 1) * E],
                xq_d[qb * 512 + tt * 128: qb * 512 + (tt + 1) * 128, :],
                f"dr{sl}", reads=[f"xres{sl}"],
                writes=[f"xres{sl}", "xqbf"], name=f"q{qb}xr{tt}", lat=5.0)
            for nn in range(2):
                pb = 4 + nn
                def fop(eng, tt=tt, nn=nn, pb=pb):
                    for p in range(NP):
                        eng.matmul(ps[pb][:, 0:512],
                                   attnT[:, p * 512 + tt * 128:
                                            p * 512 + (tt + 1) * 128],
                                   outw_sb[:, p * E + nn * 512:
                                              p * E + nn * 512 + 512],
                                   start=(p == 0), stop=False)
                    return eng.matmul(ps[pb][:, 0:512], onesr[0:1, :],
                                      outbr[0:1, nn * 512:(nn + 1) * 512],
                                      start=False, stop=True)
                sd.step("tensor", fop,
                        reads=[f"attnT{p}" for p in range(NP)] +
                              [f"ow{c2}" for c2 in range(EC)] +
                              ["onesrb", "outbrow", f"ps{pb}"],
                        writes=[f"ps{pb}"], name=f"q{qb}op{tt}_{nn}", dur=2.0)
            def fres(eng, sl=sl):
                eng.scalar_tensor_tensor(xres[:, sl * E: sl * E + 512],
                                         ps[4][:, 0:512], 1.0,
                                         xres[:, sl * E: sl * E + 512],
                                         ALU.bypass, ALU.add)
                return eng.scalar_tensor_tensor(
                    xres[:, sl * E + 512:(sl + 1) * E], ps[5][:, 0:512], 1.0,
                    xres[:, sl * E + 512:(sl + 1) * E], ALU.bypass, ALU.add)
            sd.step("vector", fres, reads=["ps4", "ps5", f"xres{sl}"],
                    writes=[f"xres{sl}"], name=f"q{qb}rs{tt}", dur=0.7)
            KC = (0, 118)[tt % 2]
            def fsum(eng, sl=sl, KC=KC):
                return eng.reduce_sum(cvals[:, KC:KC + 1],
                                      xres[:, sl * E:(sl + 1) * E], AX.X)
            sd.step("vector", fsum, reads=[f"xres{sl}", f"cv{KC}"],
                    writes=[f"cv{KC}"], name=f"q{qb}st{tt}a", dur=0.6)
            def fsq(eng, sl=sl, KC=KC):
                return eng.scalar_tensor_tensor(
                    trash2[:, 0:E], xres[:, sl * E:(sl + 1) * E], 1.0,
                    xres[:, sl * E:(sl + 1) * E], ALU.bypass, ALU.mult,
                    accum_out=cvals[:, KC + 1:KC + 2])
            sd.step("vector", fsq, reads=[f"xres{sl}", f"cv{KC + 1}"],
                    writes=[f"cv{KC + 1}", "trash2"] +
                           [f"xT2_{e2}" for e2 in range(EC)],
                    name=f"q{qb}st{tt}b", dur=0.7)
            def fmv1(eng, KC=KC):
                eng.tensor_scalar(cvals[:, KC + 2:KC + 3], cvals[:, KC:KC + 1],
                                  float(1.0 / E), None, ALU.mult)
                return eng.tensor_scalar(cvals[:, KC + 3:KC + 4],
                                         cvals[:, KC + 1:KC + 2],
                                         float(1.0 / E), None, ALU.mult)
            sd.step("vector", fmv1,
                    reads=[f"cv{KC}", f"cv{KC + 1}", f"cv{KC + 2}",
                           f"cv{KC + 3}"],
                    writes=[f"cv{KC + 2}", f"cv{KC + 3}"],
                    name=f"q{qb}st{tt}c1")
            def fmv2(eng, KC=KC):
                return eng.tensor_mul(cvals[:, KC + 4:KC + 5],
                                      cvals[:, KC + 2:KC + 3],
                                      cvals[:, KC + 2:KC + 3])
            sd.step("vector", fmv2, reads=[f"cv{KC + 2}", f"cv{KC + 4}"],
                    writes=[f"cv{KC + 4}"], name=f"q{qb}st{tt}c2")
            def fmv3(eng, KC=KC):
                return eng.tensor_sub(cvals[:, KC + 5:KC + 6],
                                      cvals[:, KC + 3:KC + 4],
                                      cvals[:, KC + 4:KC + 5])
            sd.step("vector", fmv3,
                    reads=[f"cv{KC + 3}", f"cv{KC + 4}", f"cv{KC + 5}"],
                    writes=[f"cv{KC + 5}"], name=f"q{qb}st{tt}c3")
            def frs1(eng, KC=KC):
                return eng.activation(cvals[:, KC + 6:KC + 7],
                                      cvals[:, KC + 5:KC + 6], AF.Ln,
                                      bias=f32(EPS), scale=f32(1.0))
            sd.step("scalar", frs1, reads=[f"cv{KC + 5}", f"cv{KC + 6}"],
                    writes=[f"cv{KC + 6}"], name=f"q{qb}st{tt}d", table="log")
            def frs2(eng, KC=KC):
                return eng.activation(cvals[:, KC + 7:KC + 8],
                                      cvals[:, KC + 6:KC + 7], AF.Exp,
                                      bias=f32(0.0), scale=f32(-0.5))
            sd.step("scalar", frs2, reads=[f"cv{KC + 6}", f"cv{KC + 7}"],
                    writes=[f"cv{KC + 7}"], name=f"q{qb}st{tt}e", table="log")
            def fx1(eng, qb=qb, tt=tt, sl=sl, KC=KC):
                rsb = bass.AP(cvals, KC + 7, [[128, 128], [0, E]])
                return eng.scalar_tensor_tensor(
                    x1_ap(qb, tt), xres[:, sl * E:(sl + 1) * E],
                    cvals[:, KC + 2:KC + 3], rsb,
                    ALU.subtract, ALU.mult,
                    accum_out=cvx[:, 4 * qb + tt: 4 * qb + tt + 1])
            sd.step("vector", fx1,
                    reads=[f"xres{sl}", f"cv{KC + 2}", f"cv{KC + 7}"] +
                          kv_names(2 * qb),
                    writes=[f"x1_{qb}_{tt}", f"cvx{4 * qb + tt}"] +
                           kv_names(2 * qb),
                    name=f"q{qb}x1_{tt}", dur=0.7)
            if "x1" in dbg_d:
                dma("gpsimd",
                    dbg_d["x1"][qb * 512 + tt * 128: qb * 512 + (tt + 1) * 128, :],
                    x1_ap(qb, tt), "dbg", reads=[f"x1_{qb}_{tt}"],
                    name=f"dbgx1_{qb}_{tt}")

    if PASSA_EARLY:
        for wn, (wd, nrows, ncols) in enumerate(((w1_d, HID, E), (w2_d, E, HID))):
            acc, p0 = (52, 53) if wn == 0 else (54, 55)
            ntile = nrows // 128
            nch = max(1, ncols // 2048)
            wch_a = ncols // nch
            ldlat = 5.0 if ncols == E else 12.0
            def fz(eng, acc=acc):
                return eng.memset(cvx[:, acc:acc + 1], 0.0)
            sd.step("vector", fz, reads=[f"cvx{acc}"], writes=[f"cvx{acc}"],
                    name=f"e{wn}z")
            for ti in range(ntile):
                if wn == 0:
                    j = ti % 4
                    dstap = wstA[:, j * 1024:(j + 1) * 1024]
                    wr = [f"wsl{j}"]
                    sem = f"dvE{j}"
                else:
                    dstap = wstA[:, :]
                    wr = [f"wsl{k}" for k in range(4)]
                    sem = "dvE0"
                kvw = []
                if wn == 0 and ti < 4:
                    kvw = kv_names(PA_TGS[0]) + kv_names(PA_TGS[1])
                dma("sync", dstap, wd[ti * 128:(ti + 1) * 128, :], sem,
                    reads=wr, writes=wr + kvw, name=f"e{wn}l{ti}", lat=ldlat)
                for aa in range(nch):
                    if wn == 0:
                        seg = wstA[:, (ti % 4) * 1024:(ti % 4 + 1) * 1024]
                    else:
                        seg = wstA[:, aa * wch_a:(aa + 1) * wch_a]
                    def fabs(eng, seg=seg, aa=aa, p0=p0):
                        return eng.activation(
                            seg, seg, AF.Abs, bias=f32(0.0), scale=f32(1.0),
                            accum_out=cvx[:, p0 + aa:p0 + aa + 1])
                    sd.step("scalar", fabs,
                            reads=wr + [f"cvx{p0 + aa}"],
                            writes=wr + [f"cvx{p0 + aa}"],
                            name=f"e{wn}abs{ti}_{aa}", dur=0.9 * wch_a / 1024)
                for aa in range(nch):
                    def facc(eng, aa=aa, acc=acc, p0=p0):
                        return eng.tensor_add(cvx[:, acc:acc + 1],
                                              cvx[:, acc:acc + 1],
                                              cvx[:, p0 + aa:p0 + aa + 1])
                    sd.step("vector", facc,
                            reads=[f"cvx{acc}", f"cvx{p0 + aa}"],
                            writes=[f"cvx{acc}"], name=f"e{wn}acc{ti}_{aa}")

    L3_NAMES = ([f"wb{j}" for j in range(8)] +
                ["ybuf", "gbuf", "ternb0", "ternb1",
                 "wstage0", "wstage1", "w1t", "w2t", "b1crow", "b2crow"] +
                [f"cv{i3}" for i3 in (8, 9, 10, 12, 13, 14, 15,
                                      114, 115, 116, 117)])
    def fbar23(eng):
        return eng.memset(cvals[:, 23:24], 0.0)
    allkv = []
    for tg2 in range(NTG):
        allkv += kv_names(tg2)
    sd.step("vector", fbar23,
            writes=L2_NAMES + L3_NAMES + allkv + ["cv23"], name="bar23")
    PH3_FENCE = len(sd.steps)

    for wn, (wd, nrows, ncols) in enumerate(((w1_d, HID, E), (w2_d, E, HID))):
        ntile = nrows // 128
        nsl = (2 * HID) // ncols
        ldlat = 5.0 if ncols == E else 12.0
        nch = max(1, ncols // 2048)
        wch_a = ncols // nch
        if not PASSA_EARLY:
            def fz(eng):
                return eng.memset(cvals[:, 8:9], 0.0)
            sd.step("vector", fz, reads=["cv8"], writes=["cv8"], name=f"w{wn}z")
            for ti in range(ntile):
                sl = ti % nsl
                grA = [f"wb{j}" for j in range(sl * ncols // 1024,
                                               (sl + 1) * ncols // 1024)]
                dma("sync", wbuf[:, sl * ncols: (sl + 1) * ncols],
                    wd[ti * 128:(ti + 1) * 128, :], f"dvA{sl}",
                    reads=grA + ["cv23"], writes=grA,
                    name=f"w{wn}lA{ti}", lat=ldlat)
                for aa in range(nch):
                    def fabs(eng, sl=sl, aa=aa, wch=wch_a, ncols=ncols):
                        return eng.activation(
                            ybuf[:, 0:wch],
                            wbuf[:, sl * ncols + aa * wch:
                                    sl * ncols + (aa + 1) * wch],
                            AF.Abs, bias=f32(0.0), scale=f32(1.0),
                            accum_out=cvals[:, 9 + aa:10 + aa])
                    grC = [f"wb{j}" for j in
                           range((sl * ncols + aa * wch_a) // 1024,
                                 (sl * ncols + (aa + 1) * wch_a) // 1024)]
                    sd.step("scalar", fabs,
                            reads=grC + [f"cv{9 + aa}", "ybuf"],
                            writes=[f"cv{9 + aa}", "ybuf"],
                            name=f"w{wn}abs{ti}_{aa}", dur=0.9 * wch_a / 1024)
                for aa in range(nch):
                    def facc(eng, aa=aa):
                        return eng.tensor_add(cvals[:, 8:9], cvals[:, 8:9],
                                              cvals[:, 9 + aa:10 + aa])
                    sd.step("vector", facc, reads=["cv8", f"cv{9 + aa}"],
                            writes=["cv8"], name=f"w{wn}acc{ti}_{aa}")
        if PASSA_EARLY:
            acc2 = 52 if wn == 0 else 54
            totsrc, totread = bass.AP(cvx, acc2, [[64, 128], [0, 128]]), f"cvx{acc2}"
        else:
            totsrc, totread = bass.AP(cvals, 8, [[128, 128], [0, 128]]), "cv8"
        def ftot(eng, totsrc=totsrc):
            return eng.matmul(ps[0][:, 0:1], totsrc, onesf[:, :],
                              start=True, stop=True)
        sd.step("tensor", ftot, reads=[totread, "onesfb", "ps0"], writes=["ps0"],
                name=f"w{wn}tot")
        def fc(eng, wn=wn):
            return eng.tensor_scalar(cvals[:, 12 + wn:13 + wn], ps[0][:, 0:1],
                                     float(1.0 / WN), float(EPS), ALU.mult, ALU.max)
        sd.step("vector", fc, reads=["ps0", f"cv1{2 + wn}"],
                writes=[f"cv1{2 + wn}"], name=f"w{wn}c")
        def frc(eng, wn=wn):
            return eng.reciprocal(cvals[:, 14 + wn:15 + wn],
                                  cvals[:, 12 + wn:13 + wn])
        sd.step("vector", frc, reads=[f"cv1{2 + wn}", f"cv1{4 + wn}"],
                writes=[f"cv1{4 + wn}"], name=f"w{wn}rc")
        def fph(eng, wn=wn):
            return eng.tensor_scalar(cvals[:, 114 + wn:115 + wn],
                                     cvals[:, 12 + wn:13 + wn], 0.5,
                                     None, ALU.mult)
        sd.step("vector", fph, reads=[f"cv1{2 + wn}", f"cv11{4 + wn}"],
                writes=[f"cv11{4 + wn}"], name=f"w{wn}ph")
        def fmh(eng, wn=wn):
            return eng.tensor_scalar(cvals[:, 116 + wn:117 + wn],
                                     cvals[:, 12 + wn:13 + wn], -0.5,
                                     None, ALU.mult)
        sd.step("vector", fmh, reads=[f"cv1{2 + wn}", f"cv11{6 + wn}"],
                writes=[f"cv11{6 + wn}"], name=f"w{wn}mh")
        for ti in range(ntile):
            sl = ti % nsl
            tn = f"ternb{ti % 2}"
            tno = (ti % 2) * HID
            grB = [f"wb{j}" for j in range(sl * ncols // 1024,
                                           (sl + 1) * ncols // 1024)]
            qeng, qsem = (("scalar", f"dvB{sl}") if ti % 2 == 0
                          else ("sync", f"dvC{sl}"))
            dma(qeng, wbuf[:, sl * ncols: (sl + 1) * ncols],
                wd[ti * 128:(ti + 1) * 128, :], qsem,
                reads=grB, writes=grB,
                name=f"w{wn}lB{ti}", lat=ldlat)
            wch = min(2048, ncols)
            for hh in range(ncols // wch):
                def fg(eng, sl=sl, hh=hh, wch=wch, wn=wn, ncols=ncols):
                    return eng.tensor_scalar(
                        gbuf[:, 0:wch],
                        wbuf[:, sl * ncols + hh * wch:
                                sl * ncols + (hh + 1) * wch],
                        cvals[:, 116 + wn:117 + wn], None, ALU.is_lt)
                grD = [f"wb{j}" for j in
                       range((sl * ncols + hh * wch) // 1024,
                             (sl * ncols + (hh + 1) * wch) // 1024)]
                sd.step("vector", fg,
                        reads=grD + [f"cv11{6 + wn}", "gbuf"],
                        writes=["gbuf"], name=f"w{wn}g{ti}_{hh}",
                        dur=0.65 * wch / 1024)
                def ftn(eng, sl=sl, hh=hh, wch=wch, wn=wn, ncols=ncols,
                        tno=tno):
                    return eng.scalar_tensor_tensor(
                        ternb[:, tno + hh * wch: tno + (hh + 1) * wch],
                        wbuf[:, sl * ncols + hh * wch:
                                sl * ncols + (hh + 1) * wch],
                        cvals[:, 114 + wn:115 + wn],
                        gbuf[:, 0:wch], ALU.is_gt, ALU.subtract)
                sd.step("vector", ftn,
                        reads=grD + [f"cv11{4 + wn}", "gbuf", tn],
                        writes=[tn], name=f"w{wn}t{ti}_{hh}",
                        dur=0.65 * wch / 1024)
            ngr = ncols // 1024
            for g in range(ngr):
                bk = (ti * ngr + g) % 2
                is_stream = (wn == 1 and w2place[g * 8][0] == "stream")
                def ftr(eng, g=g, bk=bk, tno=tno):
                    r = None
                    for k8 in range(8):
                        r = eng.transpose(
                            pt[bk][:, k8 * 128:(k8 + 1) * 128],
                            ternb[:, tno + g * 1024 + k8 * 128:
                                     tno + g * 1024 + (k8 + 1) * 128],
                            ident[:, :])
                    return r
                sd.step("tensor", ftr, reads=[tn, "identb", f"pt{bk}"],
                        writes=[f"pt{bk}"], name=f"w{wn}tr{ti}_{g}", dur=0.9)
                if is_stream:
                    wsl = (ti * ngr + g) % 2
                    def fev(eng, bk=bk, wsl=wsl):
                        return eng.activation(
                            wstage[:, wsl * 1024:(wsl + 1) * 1024],
                            pt[bk][:, 0:1024], AF.Identity,
                            bias=f32(0.0), scale=f32(1.0))
                    sd.step("scalar", fev,
                            reads=[f"pt{bk}", f"wstage{wsl}"],
                            writes=[f"wstage{wsl}"],
                            name=f"w2evs{ti}_{g}", dur=0.9)
                    goff = (g * 8 * 128) * E + ti * 128
                    dstap = bass.AP(w2sp_d, goff,
                                    [[E, 128], [128 * E, 8], [1, 128]])
                    dma("gpsimd", dstap,
                        wstage[:, wsl * 1024:(wsl + 1) * 1024], f"dsp{wsl}",
                        reads=[f"wstage{wsl}"],
                        writes=[f"wsp{g * 8 + k8}" for k8 in range(8)],
                        name=f"w2sp{ti}_{g}", lat=4.0)
                else:
                    for k8 in range(8):
                        cc = g * 8 + k8
                        if wn == 0:
                            dst = w1t_ap(cc, ti * 128, (ti + 1) * 128)
                            wres = "w1t"
                        else:
                            dst = w2_home_ap(cc, ti * 128, (ti + 1) * 128)
                            wres = "w2t"
                        eng_ev = ("vector", "scalar")[k8 % 2]
                        if eng_ev == "vector":
                            def fev(eng, dst=dst, bk=bk, k8=k8):
                                return eng.tensor_copy(
                                    dst, pt[bk][:, k8 * 128:(k8 + 1) * 128])
                        else:
                            def fev(eng, dst=dst, bk=bk, k8=k8):
                                return eng.activation(
                                    dst, pt[bk][:, k8 * 128:(k8 + 1) * 128],
                                    AF.Identity, bias=f32(0.0), scale=f32(1.0))
                        paf = (["cvx54", "wsl0", "wsl1"]
                               if PASSA_EARLY else [])
                        sd.step(eng_ev, fev, reads=[f"pt{bk}", wres] + paf,
                                writes=[wres], name=f"w{wn}ev{ti}_{cc}",
                                dur=0.16)
    for wn, (bd, ncols, dstt) in enumerate(((b1_d, HID, b1c), (b2_d, E, b2c))):
        grE = [f"wb{j}" for j in range(ncols // 1024)]
        dma("sync", wbuf[0:1, 0:ncols], bd[0:1, :], "dv0",
            reads=grE, writes=grE, name=f"b{wn}ld")
        def fb(eng, wn=wn, ncols=ncols, dstt=dstt):
            return eng.tensor_scalar(dstt[0:1, 0:ncols], wbuf[0:1, 0:ncols],
                                     cvals[0:1, 14 + wn:15 + wn], None, ALU.mult)
        sd.step("vector", fb, reads=["wb0", f"cv1{4 + wn}", f"b{wn + 1}crow"],
                writes=[f"b{wn + 1}crow"], name=f"b{wn}c")

    def fspf(eng):
        return eng.memset(cvals[0:1, 11:12], 0.0)
    sd.step("gpsimd", fspf,
            reads=[f"wsp{cc2}" for cc2 in w2_streamed], writes=["spfence"],
            name="spfence")

    L4_NAMES = (["ln2wrx", "ln2brx", "ln1wrx", "ln1brx", "q1t0", "q1t1",
                 "q2t0", "q2t1", "rb0", "rb1", "obuf"] +
                [f"{pfx}{p5}{q5}" for pfx in ("ab", "bb", "ub")
                 for p5 in range(2) for q5 in range(2)] +
                [f"h2b{p5}h{h5}" for p5 in range(2) for h5 in range(2)] +
                [f"h2b{p5}q{q5}" for p5 in range(2) for q5 in range(4)] +
                [f"m2p{k5}" for k5 in range(8)] +
                [f"wst{s5}" for s5 in range(NSTSL)] +
                [f"cv{i4}" for i4 in (0, 1, 2, 5, 6, 7, 16, 17, 18, 19, 20,
                                      21, 22)])
    def fbar34(eng):
        return eng.memset(cvals[:, 23:24], 0.0)
    sd.step("vector", fbar34,
            writes=L3_NAMES + L4_NAMES + ["cv23"] +
                   [f"xT2_{e2}" for e2 in range(EC)], name="bar34")

    for rowd, dstt, nn2, wnm in ((ln2w_d, ln2w_r, HID, "ln2wrx"),
                                 (ln2b_d, ln2b_r, HID, "ln2brx"),
                                 (ln1w_d, ln1w_r, E, "ln1wrx"),
                                 (ln1b_d, ln1b_r, E, "ln1brx")):
        srcap = bass.AP(rowd, 0, [[0, 128], [1, nn2]])
        dma("gpsimd", dstt[:, :], srcap, "dl",
            reads=["cv23"], writes=[wnm], name="ld_" + wnm, bulk=True)

    PH4_FENCE = len(sd.steps)

    LN1_OK = PASSA_EARLY and NTIL <= 16 and len(WST_EXTRA) >= 1
    if LN1_OK:
        for t6 in range(NTIL):
            qb6, tt6 = t6 // 4, t6 % 4
            kvw6 = kv_names(PA_TGS[1]) if t6 == 0 else []
            def fsb6(eng, qb6=qb6, tt6=tt6, t6=t6):
                return eng.activation(wst_ap(NSTSL - 2), x1_ap(qb6, tt6),
                                      AF.Square, bias=f32(0.0), scale=f32(1.0),
                                      accum_out=lnst[:, t6:t6 + 1])
            sd.step("scalar", fsb6,
                    reads=[f"x1_{qb6}_{tt6}", f"wst{NSTSL - 2}",
                           f"lnst{t6}"],
                    writes=[f"wst{NSTSL - 2}", f"lnst{t6}"] + kvw6,
                    name=f"pp{t6}sb", dur=0.9)
        NT = NTIL
        def fmvA(eng, NT=NT):
            eng.tensor_scalar(lnst[:, 0:NT], lnst[:, 0:NT],
                              float(1.0 / E), None, ALU.mult)
            return eng.tensor_scalar(lnst[:, 16:16 + NT],
                                     cvx[:, 0:NT],
                                     float(1.0 / E), None, ALU.mult)
        sd.step("vector", fmvA,
                reads=[f"lnst{t7}" for t7 in range(NT)] +
                      [f"cvx{t7}" for t7 in range(NT)] + ["lnb0", "lnb1"],
                writes=["lnb0", "lnb1"], name="ppmv1", dur=0.1)
        def fmvB(eng, NT=NT):
            return eng.scalar_tensor_tensor(
                lnst[:, 32:32 + NT], lnst[:, 16:16 + NT], 1.0,
                lnst[:, 16:16 + NT], ALU.bypass, ALU.mult)
        sd.step("vector", fmvB, reads=["lnb1", "lnb2"],
                writes=["lnb2"], name="ppmv2", dur=0.1)
        def fmvC(eng, NT=NT):
            return eng.tensor_sub(lnst[:, 0:NT], lnst[:, 0:NT],
                                  lnst[:, 32:32 + NT])
        sd.step("vector", fmvC, reads=["lnb0", "lnb2"],
                writes=["lnb0"], name="ppmv3", dur=0.1)
        def frsA(eng, NT=NT):
            return eng.activation(lnst[:, 32:32 + NT], lnst[:, 0:NT],
                                  AF.Ln, bias=f32(EPS), scale=f32(1.0))
        sd.step("scalar", frsA, reads=["lnb0", "lnb2"], writes=["lnb2"],
                name="pprsA", dur=0.1, table="log")
        def frsB(eng, NT=NT):
            return eng.activation(lnst[:, 0:NT], lnst[:, 32:32 + NT],
                                  AF.Exp, bias=f32(0.0), scale=f32(-0.5))
        sd.step("scalar", frsB, reads=["lnb2", "lnb0"], writes=["lnb0"],
                name="pprsB", dur=0.1, table="log")
        def fnbA(eng, NT=NT):
            return eng.scalar_tensor_tensor(
                lnst[:, 32:32 + NT], lnst[:, 16:16 + NT], -1.0,
                lnst[:, 0:NT], ALU.mult, ALU.mult)
        sd.step("vector", fnbA, reads=["lnb1", "lnb0", "lnb2"],
                writes=["lnb2"], name="ppnb", dur=0.1)

    def _tile_ctx(ti):
        qb, tt = ti // 4, ti % 4
        par = ti % 2
        pp = par * 2048
        pq = par * E
        AB, BB, UB, QN = f"abuf{par}", f"bbuf{par}", f"ubuf{par}", f"q1t{par}"
        if par == 0:
            Csum, Csq, Cmu, Cex, Cms, Cvr, Cln, Crs, Cnb = 0, 1, 2, 3, 4, 5, 6, 7, 22
            Cm, Cmp, Crm, Cs, Cis = 16, 17, 18, 19, 20
        else:
            Csum, Csq, Cmu, Cex, Cms, Cvr, Cln, Crs, Cnb = 50, 51, 52, 53, 54, 55, 56, 57, 58
            Cm, Cmp, Crm, Cs, Cis = 59, 60, 61, 62, 63
        Q = 64 * par
        HN = f"h2b{par}"
        HNh = [f"h2b{par}h0", f"h2b{par}h1"]
        HNq = [f"h2b{par}q{qq}" for qq in range(4)]
        A_ = [f"ab{par}{qq}" for qq in range(2)]
        B_ = [f"bb{par}{qq}" for qq in range(2)]
        U_ = [f"ub{par}{qq}" for qq in range(2)]
        def cv(c):
            return cvals[:, c:c + 1]
        def h2(c0, c1):
            if par == 0:
                return h2b[:, c0:c1]
            if H2OFF is not None:
                return kvx[:, H2OFF + c0: H2OFF + c1]
            return h2c[:, c0:c1]
        x1r = [f"x1_{qb}_{tt}"]
        return dict(locals())

    def emit_ffn_front(ti):
        C = _tile_ctx(ti)
        qb, tt, par, pp, pq = C["qb"], C["tt"], C["par"], C["pp"], C["pq"]
        AB, BB, UB, QN = C["AB"], C["BB"], C["UB"], C["QN"]
        A_, B_, U_ = C["A_"], C["B_"], C["U_"]
        Csum, Csq, Cmu, Cex, Cms = C["Csum"], C["Csq"], C["Cmu"], C["Cex"], C["Cms"]
        Cvr, Cln, Crs, Cnb = C["Cvr"], C["Cln"], C["Crs"], C["Cnb"]
        Cm, Cmp, Crm, Cs, Cis = C["Cm"], C["Cmp"], C["Crm"], C["Cs"], C["Cis"]
        cv = C["cv"]
        x1r = C["x1r"]
        def fsq(eng):
            return eng.activation(abuf[:, pp: pp + E], x1_ap(qb, tt),
                                  AF.Square, bias=f32(0.0), scale=f32(1.0),
                                  accum_out=cv(Csq))
        if not LN1_OK:
            sd.step("scalar", fsq, reads=x1r + [A_[0], f"cv{Csq}"],
                    writes=[f"cv{Csq}", A_[0]], name=f"f{ti}sb", dur=0.9)
        def fmv1(eng):
            eng.tensor_scalar(cv(Cmu), cvx[:, ti:ti + 1], float(1.0 / E),
                              None, ALU.mult)
            return eng.tensor_scalar(cv(Cex), cv(Csq), float(1.0 / E),
                                     None, ALU.mult)
        if not LN1_OK:
            sd.step("vector", fmv1, reads=[f"cvx{ti}", f"cv{Csq}", f"cv{Cmu}",
                                           f"cv{Cex}"],
                    writes=[f"cv{Cmu}", f"cv{Cex}"], name=f"f{ti}sc1")
        def fmv2(eng):
            return eng.tensor_mul(cv(Cms), cv(Cmu), cv(Cmu))
        if not LN1_OK:
            sd.step("vector", fmv2, reads=[f"cv{Cmu}", f"cv{Cms}"],
                    writes=[f"cv{Cms}"], name=f"f{ti}sc2")
        def fmv3(eng):
            return eng.tensor_sub(cv(Cvr), cv(Cex), cv(Cms))
        if not LN1_OK:
            sd.step("vector", fmv3, reads=[f"cv{Cex}", f"cv{Cms}", f"cv{Cvr}"],
                    writes=[f"cv{Cvr}"], name=f"f{ti}sc3")
        def frs1(eng):
            return eng.activation(cv(Cln), cv(Cvr), AF.Ln,
                                  bias=f32(EPS), scale=f32(1.0))
        if not LN1_OK:
            sd.step("scalar", frs1, reads=[f"cv{Cvr}", f"cv{Cln}"],
                    writes=[f"cv{Cln}"], name=f"f{ti}sd", table="log")
        def frs2(eng):
            return eng.activation(cv(Crs), cv(Cln), AF.Exp,
                                  bias=f32(0.0), scale=f32(-0.5))
        if not LN1_OK:
            sd.step("scalar", frs2, reads=[f"cv{Cln}", f"cv{Crs}"],
                    writes=[f"cv{Crs}"], name=f"f{ti}se", table="log")
        def fnb(eng):
            return eng.scalar_tensor_tensor(cv(Cnb), cv(Cmu), -1.0, cv(Crs),
                                            ALU.mult, ALU.mult)
        if not LN1_OK:
            sd.step("vector", fnb, reads=[f"cv{Cmu}", f"cv{Crs}", f"cv{Cnb}"],
                    writes=[f"cv{Cnb}"], name=f"f{ti}sf")
        if LN1_OK:
            nbap, rsap = lnst[:, 32 + ti:33 + ti], lnst[:, ti:ti + 1]
            nbrd = ["lnb2", "lnb0"]
        else:
            nbap, rsap = cv(Cnb), cv(Crs)
            nbrd = [f"cv{Cnb}", f"cv{Crs}"]
        def fa1(eng, nbap=nbap, rsap=rsap):
            return eng.activation(abuf[:, pp: pp + E], x1_ap(qb, tt),
                                  AF.Identity, bias=nbap, scale=rsap)
        sd.step("scalar", fa1, reads=x1r + nbrd + [A_[0]],
                writes=[A_[0]], name=f"f{ti}a1", dur=0.9)
        def fb1(eng):
            return eng.tensor_mul(bbuf[:, pp: pp + E], abuf[:, pp: pp + E],
                                  ln1w_r[:, :])
        sd.step("vector", fb1, reads=[A_[0], "ln1wrx", B_[0]],
                writes=[B_[0]], name=f"f{ti}b1", dur=0.6)
        def fh1(eng):
            return eng.tensor_add(abuf[:, pp + E: pp + 2 * E],
                                  bbuf[:, pp: pp + E], ln1b_r[:, :])
        sd.step("vector", fh1, reads=[B_[0], "ln1brx", A_[1]],
                writes=[A_[1]], name=f"f{ti}h1", dur=0.6)
        def fm1(eng):
            return eng.reduce_max(cv(Cm), abuf[:, pp + E: pp + 2 * E], AX.X,
                                  apply_absolute_value=True)
        sd.step("vector", fm1, reads=[A_[1], f"cv{Cm}"], writes=[f"cv{Cm}"],
                name=f"f{ti}m1", dur=0.6)
        def fc1a(eng):
            return eng.tensor_scalar(cv(Cmp), cv(Cm), float(EPS), None, ALU.max)
        sd.step("vector", fc1a, reads=[f"cv{Cm}", f"cv{Cmp}"],
                writes=[f"cv{Cmp}"], name=f"f{ti}c1a")
        def fc1b(eng):
            return eng.reciprocal(cv(Crm), cv(Cmp))
        sd.step("vector", fc1b, reads=[f"cv{Cmp}", f"cv{Crm}"],
                writes=[f"cv{Crm}"], name=f"f{ti}c1b")
        def fc1c(eng):
            return eng.tensor_scalar(cv(Cs), cv(Crm), float(127.0),
                                     None, ALU.mult)
        sd.step("vector", fc1c, reads=[f"cv{Crm}", f"cv{Cs}"],
                writes=[f"cv{Cs}"], name=f"f{ti}c1c")
        def fi1(eng):
            return eng.reciprocal(cv(Cis), cv(Cs))
        sd.step("vector", fi1, reads=[f"cv{Cs}", f"cv{Cis}"],
                writes=[f"cv{Cis}"], name=f"f{ti}i1")
        def fq1a(eng):
            return eng.activation(ubuf[:, pp: pp + E],
                                  abuf[:, pp + E: pp + 2 * E], AF.Identity,
                                  bias=f32(0.5), scale=cv(Cs))
        sd.step("scalar", fq1a, reads=[A_[1], f"cv{Cs}", U_[0]],
                writes=[U_[0]], name=f"f{ti}q1a", dur=0.9)
        def fq1b(eng):
            return eng.activation(bbuf[:, pp: pp + E], ubuf[:, pp: pp + E],
                                  AF.Identity, bias=f32(-0.5), scale=f32(1.0))
        sd.step("scalar", fq1b, reads=[U_[0], B_[0]], writes=[B_[0]],
                name=f"f{ti}q1b", dur=0.9)
        def fq1c(eng):
            return eng.scalar_tensor_tensor(bbuf[:, pp + E: pp + 2 * E],
                                            abuf[:, pp + E: pp + 2 * E],
                                            cv(Cs), bbuf[:, pp: pp + E],
                                            ALU.mult, ALU.is_gt)
        sd.step("vector", fq1c, reads=[A_[1], f"cv{Cs}", B_[0], B_[1]],
                writes=[B_[1]], name=f"f{ti}q1c", dur=0.6)
        def fq1d(eng):
            return eng.scalar_tensor_tensor(abuf[:, pp: pp + E],
                                            bbuf[:, pp: pp + E], float(0.5),
                                            bbuf[:, pp + E: pp + 2 * E],
                                            ALU.subtract, ALU.add)
        sd.step("vector", fq1d, reads=[B_[0], B_[1], A_[0]],
                writes=[A_[0]], name=f"f{ti}q1d", dur=0.6)
        def fq1e(eng):
            return eng.tensor_scalar(abuf[:, pp + E: pp + 2 * E],
                                     abuf[:, pp: pp + E], cv(Cis),
                                     None, ALU.mult)
        sd.step("vector", fq1e, reads=[A_[0], f"cv{Cis}", A_[1]],
                writes=[A_[1]], name=f"f{ti}q1e", dur=0.6)
        gbq = par
        def ftq1(eng):
            r = None
            for k in range(EC):
                r = eng.transpose(pt[gbq][:, k * 128:(k + 1) * 128],
                                  abuf[:, pp + E + k * 128: pp + E + (k + 1) * 128],
                                  ident[:, :])
            return r
        sd.step("tensor", ftq1, reads=[A_[1], "identb", f"pt{gbq}"],
                writes=[f"pt{gbq}"], name=f"f{ti}tq", dur=0.9)
        def feq1(eng):
            return eng.tensor_copy(q1t[:, pq: pq + E], pt[gbq][:, 0:1024])
        sd.step("vector", feq1, reads=[f"pt{gbq}", QN], writes=[QN],
                name=f"f{ti}eq", dur=0.6)

    def emit_ffn_mm1(ti, qx):
        C = _tile_ctx(ti)
        pq, QN = C["pq"], C["QN"]
        bp = (qx % 2) * 2
        for nn in range(2):
            pn = bp + nn
            def fmm1(eng, qx=qx, nn=nn, pq=pq, pn=pn):
                c0 = qx * 1024 + nn * 512
                for k in range(EC):
                    eng.matmul(ps[pn][:, 0:512],
                               q1t[:, pq + k * 128: pq + (k + 1) * 128],
                               w1t_ap(k, c0, c0 + 512),
                               start=(k == 0), stop=False)
                return eng.matmul(ps[pn][:, 0:512], onesr[0:1, :],
                                  b1c[0:1, qx * 1024 + nn * 512:
                                          qx * 1024 + nn * 512 + 512],
                                  start=False, stop=True)
            sd.step("tensor", fmm1,
                    reads=[QN, "w1t", "onesrb", "b1crow", f"ps{pn}"],
                    writes=[f"ps{pn}"], name=f"f{ti}mm1_{qx}_{nn}", dur=2.3)

    def emit_ffn_gelu(ti, qx):
        C = _tile_ctx(ti)
        pp, AB = C["pp"], C["AB"]
        Q, HN, h2 = C["Q"], C["HN"], C["h2"]
        HNq, A_ = C["HNq"], C["A_"]
        par = C["par"]
        bp = (qx % 2) * 2
        gco = 16 + par * 16 + qx * 2
        def fgel(eng, qx=qx, gco=gco, bp=bp):
            r = None
            for nn in range(2):
                r = eng.activation(
                    h2(qx * 1024 + nn * 512,  qx * 1024 + nn * 512 + 512),
                    ps[bp + nn][:, 0:512], AF.Gelu, bias=f32(0.0),
                    scale=cvals[:, 12:13],
                    accum_out=cvx[:, gco + nn:gco + nn + 1])
            return r
        sd.step("scalar", fgel,
                reads=[f"ps{bp}", f"ps{bp + 1}", "cv12", HNq[qx]] +
                      [f"gco{gco + nn}" for nn in range(2)],
                writes=[HNq[qx]] + [f"gco{gco + nn}" for nn in range(2)],
                name=f"f{ti}gel{qx}", dur=1.0, table="gelu")
        dstq2 = 24 + par * 4 + qx
        def f2qq(eng, qx=qx, dstq2=dstq2):
            return eng.activation(abuf[:, pp + (qx % 2) * 1024:
                                       pp + (qx % 2) * 1024 + 1024],
                                  h2(qx * 1024, (qx + 1) * 1024),
                                  AF.Square, bias=f32(0.0), scale=f32(1.0),
                                  accum_out=cvx[:, dstq2:dstq2 + 1])
        sd.step("scalar", f2qq,
                reads=[HNq[qx], A_[qx % 2], f"sq2{dstq2}"],
                writes=[f"sq2{dstq2}", A_[qx % 2]], name=f"f{ti}2q{qx}",
                dur=0.9)
        if qx % 2 == 1:
            hf = qx // 2
            dstc = Q + 25 + hf
            def f2s(eng, gco=gco, dstc=dstc):
                return eng.reduce_sum(cvals[:, dstc:dstc + 1],
                                      cvx[:, gco - 2:gco + 2], AX.X)
            sd.step("vector", f2s,
                    reads=[f"gco{gco - 2 + nn}" for nn in range(4)] +
                          [f"cv{dstc}"],
                    writes=[f"cv{dstc}"], name=f"f{ti}2s{hf}")
            dstq = Q + 9 + hf
            def f2q(eng, dstq=dstq, dstq2=dstq2):
                return eng.tensor_add(cvals[:, dstq:dstq + 1],
                                      cvx[:, dstq2 - 1:dstq2],
                                      cvx[:, dstq2:dstq2 + 1])
            sd.step("vector", f2q,
                    reads=[f"sq2{dstq2 - 1}", f"sq2{dstq2}", f"cv{dstq}"],
                    writes=[f"cv{dstq}"], name=f"f{ti}2qs{hf}")

    def emit_ffn_tail(ti):
        C = _tile_ctx(ti)
        qb, tt, par, pp, pq = C["qb"], C["tt"], C["par"], C["pp"], C["pq"]
        AB, BB, UB, QN = C["AB"], C["BB"], C["UB"], C["QN"]
        Q, HN, h2 = C["Q"], C["HN"], C["h2"]
        HNh = C["HNh"]
        HNq, A_, B_, U_ = C["HNq"], C["A_"], C["B_"], C["U_"]
        cv = C["cv"]
        x1r = C["x1r"]
        def emit_stream_load(si):
            if si % 2 or si >= len(w2_streamed):
                return
            cc2 = w2_streamed[si]
            ssl2 = si % NSTSL
            if ssl2 < 2:
                dstap = bass.AP(wsbuf, 0, [[2 * E, 128], [E, 2], [1, E]])
            else:
                o0 = WST_EXTRA[ssl2 - 2]
                dstap = bass.AP(kvx, o0, [[NKB * 2048, 128],
                                          [WST_EXTRA[ssl2 - 1] - o0, 2],
                                          [1, E]])
            srcap = bass.AP(w2sp_d, cc2 * 128 * E,
                            [[E, 128], [128 * E, 2], [1, E]])
            dma("gpsimd", dstap, srcap, f"dt{ssl2}",
                reads=[f"wst{ssl2}", f"wst{ssl2 + 1}",
                       f"wsp{cc2}", f"wsp{cc2 + 1}"],
                writes=[f"wst{ssl2}", f"wst{ssl2 + 1}"],
                name=f"f{ti}wl{si}", lat=5.0)
        for s5 in range(0, NSTSL, 2):
            emit_stream_load(s5)
        def f2mva(eng):
            eng.tensor_add(cvals[:, Q + 24:Q + 25], cvals[:, Q + 25:Q + 26], cvals[:, Q + 26:Q + 27])
            return eng.tensor_add(cvals[:, Q + 27:Q + 28], cvals[:, Q + 9:Q + 10],
                                  cvals[:, Q + 10:Q + 11])
        sd.step("vector", f2mva,
                reads=[f"cv{Q + 25}", f"cv{Q + 26}", f"cv{Q + 9}", f"cv{Q + 10}", f"cv{Q + 24}", f"cv{Q + 27}"],
                writes=[f"cv{Q + 24}", f"cv{Q + 27}"], name=f"f{ti}2ca")
        def f2mvb(eng):
            eng.tensor_scalar(cvals[:, Q + 28:Q + 29], cvals[:, Q + 24:Q + 25],
                              float(1.0 / HID), None, ALU.mult)
            return eng.tensor_scalar(cvals[:, Q + 29:Q + 30], cvals[:, Q + 27:Q + 28],
                                     float(1.0 / HID), None, ALU.mult)
        sd.step("vector", f2mvb, reads=[f"cv{Q + 24}", f"cv{Q + 27}", f"cv{Q + 28}", f"cv{Q + 29}"],
                writes=[f"cv{Q + 28}", f"cv{Q + 29}"], name=f"f{ti}2cb")
        def f2mvc(eng):
            return eng.tensor_mul(cvals[:, Q + 30:Q + 31], cvals[:, Q + 28:Q + 29],
                                  cvals[:, Q + 28:Q + 29])
        sd.step("vector", f2mvc, reads=[f"cv{Q + 28}", f"cv{Q + 30}"], writes=[f"cv{Q + 30}"],
                name=f"f{ti}2cc")
        def f2mvd(eng):
            return eng.tensor_sub(cvals[:, Q + 31:Q + 32], cvals[:, Q + 29:Q + 30],
                                  cvals[:, Q + 30:Q + 31])
        sd.step("vector", f2mvd, reads=[f"cv{Q + 29}", f"cv{Q + 30}", f"cv{Q + 31}"],
                writes=[f"cv{Q + 31}"], name=f"f{ti}2cd")
        def f2r1(eng):
            return eng.activation(cvals[:, Q + 33:Q + 34], cvals[:, Q + 31:Q + 32], AF.Ln,
                                  bias=f32(EPS), scale=f32(1.0))
        sd.step("scalar", f2r1, reads=[f"cv{Q + 31}", f"cv{Q + 33}"], writes=[f"cv{Q + 33}"],
                name=f"f{ti}2d", table="log")
        def f2r2(eng):
            return eng.activation(cvals[:, Q + 34:Q + 35], cvals[:, Q + 33:Q + 34], AF.Exp,
                                  bias=f32(0.0), scale=f32(-0.5))
        sd.step("scalar", f2r2, reads=[f"cv{Q + 33}", f"cv{Q + 34}"], writes=[f"cv{Q + 34}"],
                name=f"f{ti}2e", table="log")
        def f2nb(eng):
            return eng.scalar_tensor_tensor(cvals[:, Q + 35:Q + 36], cvals[:, Q + 28:Q + 29],
                                            -1.0, cvals[:, Q + 34:Q + 35],
                                            ALU.mult, ALU.mult)
        sd.step("vector", f2nb, reads=[f"cv{Q + 28}", f"cv{Q + 34}", f"cv{Q + 35}"],
                writes=[f"cv{Q + 35}"], name=f"f{ti}2f")
        for qq in range(4):
            a_, b_ = A_[qq % 2], B_[qq % 2]
            qo = pp + (qq % 2) * 1024
            def f2a(eng, qq=qq, qo=qo):
                return eng.activation(abuf[:, qo: qo + 1024],
                                      h2(qq * 1024, (qq + 1) * 1024),
                                      AF.Identity, bias=cvals[:, Q + 35:Q + 36],
                                      scale=cvals[:, Q + 34:Q + 35])
            sd.step("scalar", f2a,
                    reads=[HNq[qq], f"cv{Q + 35}", f"cv{Q + 34}", a_],
                    writes=[a_], name=f"f{ti}2g{qq}", dur=0.9)
            def f2b(eng, qq=qq, qo=qo):
                return eng.tensor_mul(bbuf[:, qo: qo + 1024],
                                      abuf[:, qo: qo + 1024],
                                      ln2w_r[:, qq * 1024:(qq + 1) * 1024])
            sd.step("vector", f2b, reads=[a_, "ln2wrx", b_], writes=[b_],
                    name=f"f{ti}2h{qq}", dur=0.6)
            def f2c(eng, qq=qq, qo=qo):
                return eng.tensor_add(h2(qq * 1024, (qq + 1) * 1024),
                                      bbuf[:, qo: qo + 1024],
                                      ln2b_r[:, qq * 1024:(qq + 1) * 1024])
            sd.step("vector", f2c, reads=[b_, "ln2brx", HNq[qq]],
                    writes=[HNq[qq]], name=f"f{ti}2i{qq}", dur=0.6)
            def fm2q(eng, qq=qq):
                return eng.reduce_max(
                    cvx[:, 56 + par * 4 + qq: 57 + par * 4 + qq],
                    h2(qq * 1024, (qq + 1) * 1024), AX.X,
                    apply_absolute_value=True)
            sd.step("vector", fm2q,
                    reads=[HNq[qq], f"m2p{par * 4 + qq}"],
                    writes=[f"m2p{par * 4 + qq}"], name=f"f{ti}m2_{qq}",
                    dur=0.6)
        def fm2(eng):
            return eng.reduce_max(cvals[:, Q + 40:Q + 41],
                                  cvx[:, 56 + par * 4: 60 + par * 4], AX.X,
                                  apply_absolute_value=True)
        sd.step("vector", fm2,
                reads=[f"m2p{par * 4 + k}" for k in range(4)] +
                      [f"cv{Q + 40}"],
                writes=[f"cv{Q + 40}"], name=f"f{ti}m2")
        def fsc2a(eng):
            return eng.tensor_scalar(cvals[:, Q + 41:Q + 42], cvals[:, Q + 40:Q + 41],
                                     float(EPS), None, ALU.max)
        sd.step("vector", fsc2a, reads=[f"cv{Q + 40}", f"cv{Q + 41}"], writes=[f"cv{Q + 41}"],
                name=f"f{ti}c2a")
        def fsc2b(eng):
            return eng.reciprocal(cvals[:, Q + 42:Q + 43], cvals[:, Q + 41:Q + 42])
        sd.step("vector", fsc2b, reads=[f"cv{Q + 41}", f"cv{Q + 42}"], writes=[f"cv{Q + 42}"],
                name=f"f{ti}c2b")
        def fsc2c(eng):
            return eng.tensor_scalar(cvals[:, Q + 43:Q + 44], cvals[:, Q + 42:Q + 43],
                                     float(127.0), None, ALU.mult)
        sd.step("vector", fsc2c, reads=[f"cv{Q + 42}", f"cv{Q + 43}"], writes=[f"cv{Q + 43}"],
                name=f"f{ti}c2c")
        def fis2a(eng):
            return eng.reciprocal(cvals[:, Q + 44:Q + 45], cvals[:, Q + 43:Q + 44])
        sd.step("vector", fis2a, reads=[f"cv{Q + 43}", f"cv{Q + 44}"], writes=[f"cv{Q + 44}"],
                name=f"f{ti}i2a")
        def fis2b(eng):
            return eng.tensor_scalar(cvals[:, Q + 45:Q + 46], cvals[:, Q + 44:Q + 45],
                                     cvals[:, 13:14], None, ALU.mult)
        sd.step("vector", fis2b, reads=[f"cv{Q + 44}", "cv13", f"cv{Q + 45}"],
                writes=[f"cv{Q + 45}"], name=f"f{ti}i2b")
        for qq in range(4):
            a_, b_, u_ = A_[qq % 2], B_[qq % 2], U_[qq % 2]
            qo = pp + (qq % 2) * 1024
            def fq2a(eng, qq=qq, qo=qo):
                return eng.activation(ubuf[:, qo: qo + 1024],
                                      h2(qq * 1024, (qq + 1) * 1024),
                                      AF.Identity, bias=f32(0.5),
                                      scale=cvals[:, Q + 43:Q + 44])
            sd.step("scalar", fq2a, reads=[HNq[qq], f"cv{Q + 43}", u_],
                    writes=[u_], name=f"f{ti}q2a{qq}", dur=0.9)
            def fq2b(eng, qq=qq, qo=qo):
                return eng.activation(abuf[:, qo: qo + 1024],
                                      ubuf[:, qo: qo + 1024], AF.Identity,
                                      bias=f32(-0.5), scale=f32(1.0))
            sd.step("scalar", fq2b, reads=[u_, a_], writes=[a_],
                    name=f"f{ti}q2b{qq}", dur=0.9)
            def fq2c(eng, qq=qq, qo=qo):
                return eng.scalar_tensor_tensor(
                    bbuf[:, qo: qo + 1024], h2(qq * 1024, (qq + 1) * 1024),
                    cvals[:, Q + 43:Q + 44], abuf[:, qo: qo + 1024],
                    ALU.mult, ALU.is_gt)
            sd.step("vector", fq2c, reads=[HNq[qq], f"cv{Q + 43}", a_, b_],
                    writes=[b_], name=f"f{ti}q2c{qq}", dur=0.6)
            def fq2d(eng, qq=qq, qo=qo):
                return eng.scalar_tensor_tensor(
                    h2(qq * 1024, (qq + 1) * 1024), abuf[:, qo: qo + 1024],
                    float(0.5), bbuf[:, qo: qo + 1024],
                    ALU.subtract, ALU.add)
            sd.step("vector", fq2d, reads=[a_, b_, HNq[qq]], writes=[HNq[qq]],
                    name=f"f{ti}q2d{qq}", dur=0.6)
        stream_pos = {cc: si for si, cc in enumerate(w2_streamed)}
        gorder = sorted(range(4),
                        key=lambda g2: (w2place[g2 * 8][0] != "stream", g2))
        cc_first = gorder[0] * 8
        for g in gorder:
            gb = g % 2
            pb = par
            def ftq2(eng, g=g, pb=pb):
                r = None
                for k in range(8):
                    r = eng.transpose(
                        pt[pb][:, k * 128:(k + 1) * 128],
                        h2(g * 1024 + k * 128,  g * 1024 + (k + 1) * 128),
                        ident[:, :])
                return r
            sd.step("tensor", ftq2, reads=[HNq[g], "identb", f"pt{pb}"],
                    writes=[f"pt{pb}"], name=f"f{ti}t2_{g}", dur=0.9)
            def feq2(eng, gb=gb, pb=pb):
                return eng.tensor_copy(q2t[:, gb * 1024:(gb + 1) * 1024],
                                       pt[pb][:, 0:1024])
            sd.step("vector", feq2, reads=[f"pt{pb}", f"q2t{gb}"],
                    writes=[f"q2t{gb}"], name=f"f{ti}e2_{g}", dur=0.6)
            for k in range(8):
                cc = g * 8 + k
                kind = w2place[cc][0]
                if kind == "stream":
                    ssl = stream_pos[cc] % NSTSL
                    rres = [f"wst{ssl}"]
                    rhs_ap = lambda c0, c1, ssl=ssl: wst_ap(ssl, c0, c1)
                else:
                    rres = ["w2t"]
                    rhs_ap = lambda c0, c1, cc=cc: w2_home_ap(cc, c0, c1)
                def fmm2(eng, g=g, k=k, gb=gb, cc=cc, rhs_ap=rhs_ap):
                    r = None
                    for nn in range(2):
                        r = eng.matmul(ps[4 + nn][:, 0:512],
                                       q2t[:, gb * 1024 + k * 128:
                                              gb * 1024 + (k + 1) * 128],
                                       rhs_ap(nn * 512, (nn + 1) * 512),
                                       start=(cc == cc_first), stop=False)
                    return r
                sd.step("tensor", fmm2,
                        reads=[f"q2t{gb}", "ps4", "ps5"] + rres,
                        writes=["ps4", "ps5"], name=f"f{ti}mm2_{cc}", dur=0.5)
                if kind == "stream" and stream_pos[cc] % 2 == 1:
                    emit_stream_load(stream_pos[cc] - 1 + NSTSL)
        def fbias2(eng):
            eng.matmul(ps[4][:, 0:512], onesr[0:1, :], b2c[0:1, 0:512],
                       start=False, stop=True)
            return eng.matmul(ps[5][:, 0:512], onesr[0:1, :], b2c[0:1, 512:E],
                              start=False, stop=True)
        sd.step("tensor", fbias2,
                reads=["onesrb", "b2crow", "ps4", "ps5"],
                writes=["ps4", "ps5"], name=f"f{ti}b2", dur=0.2)
        evo = 48 + C["par"] * 2
        for bb2 in range(2):
            def fev2(eng, evo=evo, bb2=bb2):
                return eng.scalar_tensor_tensor(
                    rbuf[:, bb2 * 512:(bb2 + 1) * 512],
                    ps[4 + bb2][:, 0:512],
                    cvals[:, Q + 45:Q + 46],
                    x1_ap(qb, tt, bb2 * 512, (bb2 + 1) * 512),
                    ALU.mult, ALU.add,
                    accum_out=cvx[:, evo + bb2:evo + bb2 + 1])
            sd.step("vector", fev2,
                    reads=[f"ps{4 + bb2}", f"cv{Q + 45}", f"rb{bb2}",
                           f"evo{evo + bb2}"] + x1r,
                    writes=[f"rb{bb2}", f"evo{evo + bb2}"],
                    name=f"f{ti}ev{bb2}", dur=0.6)
        def f3a(eng, evo=evo):
            return eng.tensor_add(cvals[:, Q + 36:Q + 37],
                                  cvx[:, evo:evo + 1],
                                  cvx[:, evo + 1:evo + 2])
        sd.step("vector", f3a,
                reads=[f"evo{evo}", f"evo{evo + 1}", f"cv{Q + 36}"],
                writes=[f"cv{Q + 36}"], name=f"f{ti}3a")
        def f3b(eng):
            return eng.activation(abuf[:, pp: pp + 1024], rbuf[:, :],
                                  AF.Square, bias=f32(0.0), scale=f32(1.0),
                                  accum_out=cvals[:, Q + 37:Q + 38])
        sd.step("scalar", f3b, reads=["rb0", "rb1", A_[0], f"cv{Q + 37}"],
                writes=[f"cv{Q + 37}", A_[0]], name=f"f{ti}3b", dur=0.9)
        def f3c1(eng):
            eng.tensor_scalar(cvals[:, Q + 38:Q + 39], cvals[:, Q + 36:Q + 37],
                              float(1.0 / E), None, ALU.mult)
            return eng.tensor_scalar(cvals[:, Q + 39:Q + 40], cvals[:, Q + 37:Q + 38],
                                     float(1.0 / E), None, ALU.mult)
        sd.step("vector", f3c1, reads=[f"cv{Q + 36}", f"cv{Q + 37}", f"cv{Q + 38}", f"cv{Q + 39}"],
                writes=[f"cv{Q + 38}", f"cv{Q + 39}"], name=f"f{ti}3c1")
        def f3c2(eng):
            return eng.tensor_mul(cvals[:, Q + 46:Q + 47], cvals[:, Q + 38:Q + 39],
                                  cvals[:, Q + 38:Q + 39])
        sd.step("vector", f3c2, reads=[f"cv{Q + 38}", f"cv{Q + 46}"], writes=[f"cv{Q + 46}"],
                name=f"f{ti}3c2")
        def f3c3(eng):
            return eng.tensor_sub(cvals[:, Q + 47:Q + 48], cvals[:, Q + 39:Q + 40],
                                  cvals[:, Q + 46:Q + 47])
        sd.step("vector", f3c3, reads=[f"cv{Q + 39}", f"cv{Q + 46}", f"cv{Q + 47}"],
                writes=[f"cv{Q + 47}"], name=f"f{ti}3c3")
        def f3d(eng):
            return eng.activation(cvals[:, Q + 48:Q + 49], cvals[:, Q + 47:Q + 48], AF.Ln,
                                  bias=f32(EPS), scale=f32(1.0))
        sd.step("scalar", f3d, reads=[f"cv{Q + 47}", f"cv{Q + 48}"], writes=[f"cv{Q + 48}"],
                name=f"f{ti}3d", table="log")
        def f3e(eng):
            return eng.activation(cvals[:, Q + 49:Q + 50], cvals[:, Q + 48:Q + 49], AF.Exp,
                                  bias=f32(0.0), scale=f32(-0.5))
        sd.step("scalar", f3e, reads=[f"cv{Q + 48}", f"cv{Q + 49}"], writes=[f"cv{Q + 49}"],
                name=f"f{ti}3e", table="log")
        def f3f(eng):
            return eng.tensor_scalar(obuf[:, :], rbuf[:, :], cvals[:, Q + 38:Q + 39],
                                     cvals[:, Q + 49:Q + 50], ALU.subtract, ALU.mult)
        sd.step("vector", f3f,
                reads=["rb0", "rb1", f"cv{Q + 38}", f"cv{Q + 49}", "obuf"],
                writes=["obuf"], name=f"f{ti}3f", dur=0.6)
        row0 = qb * 512 + tt * 128
        dma("sync", out_d[row0:row0 + 128, :], obuf[:, :], "do",
            reads=["obuf"], name=f"f{ti}out")

    emit_ffn_front(0)
    for qx in range(4):
        emit_ffn_mm1(0, qx)
        emit_ffn_gelu(0, qx)
    for ti in range(NTIL):
        if ti + 1 < NTIL:
            emit_ffn_front(ti + 1)
            emit_ffn_mm1(ti + 1, 0)
        emit_ffn_tail(ti)
        if ti + 1 < NTIL:
            emit_ffn_gelu(ti + 1, 0)
            for qx in range(1, 4):
                emit_ffn_mm1(ti + 1, qx)
                emit_ffn_gelu(ti + 1, qx)

    sd.reorder(0)
    sd.finalize()

    sem_keys = sorted({(st["dma_sem"] if st["dma_sem"] is not None else st["eng"])
                       for st in sd.steps})
    import contextlib
    with contextlib.ExitStack() as stk:
        sems = {}
        for k in sem_keys:
            sems[k] = stk.enter_context(nc.semaphore("sem_" + k))
        blk = stk.enter_context(nc.Block())

        @blk.sync
        def _(sync):
            sd.emit("sync", sync, sems)
            for k2 in ("do", "dbg"):
                if k2 in sd.sem_totals:
                    sync.wait_ge(sems[k2], sd.sem_totals[k2])

        @blk.gpsimd
        def _(gpsimd):
            sd.emit("gpsimd", gpsimd, sems)

        @blk.vector
        def _(vector):
            sd.emit("vector", vector, sems)

        @blk.scalar
        def _(scalar):
            sd.emit("scalar", scalar, sems)

        @blk.tensor
        def _(tensor):
            sd.emit("tensor", tensor, sems)

    return nc


def make_consts(par):
    ident = np.eye(128, dtype=np.float32).astype(NPBF)
    onesc = np.ones((128, 64), np.float32).astype(NPBF)
    onesr = np.ones((1, 128), np.float32).astype(NPBF)
    onesf = np.ones((128, 1), np.float32)
    kk = np.arange(128)[:, None]
    jj = np.arange(64)[None, :]
    dmask = (kk <= 2 * jj + par).astype(np.float32).astype(NPBF)
    return ident, onesc, onesr, onesf, dmask


def kernel(L0_attn_x, L0_attn_qkv_w, L0_attn_qkv_b, L0_attn_out_w,
           L0_attn_out_b, L0_ffn_ln1_w, L0_ffn_ln1_b, L0_ffn_w1, L0_ffn_b1,
           L0_ffn_ln2_w, L0_ffn_ln2_b, L0_ffn_w2, L0_ffn_b2):
    x = np.ascontiguousarray(np.asarray(L0_attn_x, np.float32))
    B, S, E = x.shape
    H, D = 16, 64
    qkv_w = np.asarray(L0_attn_qkv_w, np.float32)
    qw = np.concatenate([np.ascontiguousarray(qkv_w[:, :, t * 64:(t + 1) * 64])
                         .reshape(E, H * 64) for t in range(3)], axis=1)
    qkv_b = np.asarray(L0_attn_qkv_b, np.float32)
    qb = np.stack([np.ascontiguousarray(qkv_b[:, t * 64:(t + 1) * 64])
                   .reshape(H * 64) for t in range(3)], axis=0)
    out_w = np.ascontiguousarray(np.asarray(L0_attn_out_w, np.float32)
                                 .reshape(H * D, E))
    out_b = np.asarray(L0_attn_out_b, np.float32).reshape(1, E)
    w1 = np.ascontiguousarray(np.asarray(L0_ffn_w1, np.float32))
    b1 = np.asarray(L0_ffn_b1, np.float32).reshape(1, -1)
    w2 = np.ascontiguousarray(np.asarray(L0_ffn_w2, np.float32))
    b2 = np.asarray(L0_ffn_b2, np.float32).reshape(1, -1)
    ln1w = np.asarray(L0_ffn_ln1_w, np.float32).reshape(1, -1)
    ln1b = np.asarray(L0_ffn_ln1_b, np.float32).reshape(1, -1)
    ln2w = np.asarray(L0_ffn_ln2_w, np.float32).reshape(1, -1)
    ln2b = np.asarray(L0_ffn_ln2_b, np.float32).reshape(1, -1)

    nc = build_program(S=S)
    in_maps = []
    for core in range(8):
        b, par = core // 2, core % 2
        ident, onesc, onesr, onesf, dmask = make_consts(par)
        in_maps.append(dict(
            xkv=x[b], xq=np.ascontiguousarray(x[b, par::2, :]),
            qkv_w=qw, qkvb=qb, out_w=out_w, out_b=out_b,
            w1=w1, b1=b1, w2=w2, b2=b2,
            ln1w=ln1w, ln1b=ln1b, ln2w=ln2w, ln2b=ln2b,
            ident=ident, onesc=onesc, onesr=onesr, onesf=onesf, dmask=dmask,
        ))
    res = run_bass_kernel_spmd(nc, in_maps, core_ids=list(range(8)))
    out = np.empty((B, S, E), np.float32)
    for core in range(8):
        b, par = core // 2, core % 2
        out[b, par::2, :] = res.results[core]["out"]
    return out


# version 31
# speedup vs baseline: 1.0015x; 1.0015x over previous
import numpy as np
import ml_dtypes
import concourse.bass as bass
import concourse.mybir as mybir
from concourse.bass_utils import run_bass_kernel_spmd

F32 = mybir.dt.float32
BF16 = mybir.dt.bfloat16
I8 = mybir.dt.int8
AF = mybir.ActivationFunctionType
ALU = mybir.AluOpType
AX = mybir.AxisListType
NPBF = ml_dtypes.bfloat16
f32 = np.float32

SHIFT = 16.0
EPS = 1e-5
SB_LIMIT = 229344


class Sched:

    def __init__(self):
        self.steps = []
        self.lw = {}
        self.readers = {}

    def step(self, eng, fn, reads=(), writes=(), dma_sem=None, name="",
             bulk=False, dur=0.12, lat=None, table=None):
        idx = len(self.steps)
        deps = set()
        for r in reads:
            deps.update(self.lw.get(r, {}).values())
        for w in writes:
            deps.update(self.lw.get(w, {}).values())
            deps.update(self.readers.get(w, ()))
        deps.discard(idx)
        if dma_sem is not None and lat is None:
            dur, lat = 0.1, 2.5
        st = dict(idx=idx, eng=eng, fn=fn, deps=deps, dma_sem=dma_sem,
                  name=name, needed=False, inc_val=None, bulk=bulk,
                  dur=dur, lat=lat if lat is not None else 0.0, table=table)
        self.steps.append(st)
        key = dma_sem if dma_sem is not None else eng
        for w in writes:
            self.lw[w] = {key: idx}
            self.readers[w] = []
        for r in reads:
            self.readers.setdefault(r, []).append(idx)
        return idx

    def reorder(self, fence):
        n = len(self.steps)
        succ = [[] for _ in range(n)]
        indeg = [0] * n
        LAT = 0.4
        for st in self.steps[fence:]:
            for d in st["deps"]:
                if d >= fence:
                    succ[d].append(st["idx"])
                    indeg[st["idx"]] += 1
        rank = [0.0] * n
        for st in reversed(self.steps[fence:]):
            i = st["idx"]
            r = 0.0
            for s2 in succ[i]:
                x = rank[s2] + (LAT if self.steps[s2]["eng"] != st["eng"]
                                else 0.0)
                if x > r:
                    r = x
            rank[i] = r + st["dur"] + st["lat"]
        est = [0.0] * n
        fin = [0.0] * n
        ready = {}
        engt = {}
        for st in self.steps[fence:]:
            if indeg[st["idx"]] == 0:
                ready.setdefault(st["eng"], []).append(st["idx"])
        order = []
        total = n - fence
        cur_table = {}
        TBL_COST = 0.0
        while len(order) < total:
            best = None
            for eng, lst in ready.items():
                if not lst:
                    continue
                t0 = engt.get(eng, 0.0)
                s0 = min(max(t0, est[i]) for i in lst)
                if best is None or s0 < best[0]:
                    best = (s0, eng)
            s0, eng = best
            lst = ready[eng]
            cand = [i for i in lst if max(engt.get(eng, 0.0), est[i]) <= s0 + 1e-9]
            ctab = cur_table.get(eng)
            def prio(j):
                t = self.steps[j]["table"]
                pen = TBL_COST if (t is not None and ctab is not None
                                   and t != ctab) else 0.0
                return (rank[j] - pen, -j)
            i = max(cand, key=prio)
            lst.remove(i)
            st = self.steps[i]
            start = max(engt.get(eng, 0.0), est[i])
            sw = 0.0
            if st["table"] is not None:
                if cur_table.get(eng) not in (None, st["table"]):
                    sw = TBL_COST
                cur_table[eng] = st["table"]
            f = start + st["dur"] + sw
            engt[eng] = f
            fin[i] = f + st["lat"]
            order.append(st)
            for s2 in succ[i]:
                st2 = self.steps[s2]
                x = fin[i] + (LAT if (st2["eng"] != eng
                                      or st["dma_sem"] is not None) else 0.0)
                if x > est[s2]:
                    est[s2] = x
                indeg[s2] -= 1
                if indeg[s2] == 0:
                    ready.setdefault(st2["eng"], []).append(s2)
        self.steps = self.steps[:fence] + order
        self.span_est = max(engt.values()) if engt else 0.0

    def finalize(self):
        self.by_idx = {st["idx"]: st for st in self.steps}
        for st in self.steps:
            for d in st["deps"]:
                dep = self.by_idx[d]
                if (dep["eng"] == st["eng"] == "tensor"
                        and dep["dma_sem"] is None and st["dma_sem"] is None):
                    continue
                dep["needed"] = True
        counters = {}
        for st in self.steps:
            key = st["dma_sem"] if st["dma_sem"] is not None else st["eng"]
            if st["dma_sem"] is not None:
                counters[key] = counters.get(key, 0) + 16
                st["inc_val"] = counters[key]
            elif st["needed"]:
                counters[key] = counters.get(key, 0) + 1
                st["inc_val"] = counters[key]
        self.sem_totals = dict(counters)
        self.bulk_sems = {st["dma_sem"] for st in self.steps
                         if st["dma_sem"] is not None and st["bulk"]}

    def emit(self, eng_name, eng, sems):
        watermark = {}
        for st in self.steps:
            if st["eng"] != eng_name:
                continue
            waits = {}
            for d in sorted(st["deps"]):
                dep = self.by_idx[d]
                if (dep["eng"] == eng_name == "tensor"
                        and dep["dma_sem"] is None and st["dma_sem"] is None):
                    continue
                key = dep["dma_sem"] if dep["dma_sem"] is not None else dep["eng"]
                assert dep["inc_val"] is not None, \
                    f"{st['name']} deps on {dep['name']} without inc"
                val = self.sem_totals[key] if key in self.bulk_sems else dep["inc_val"]
                waits[key] = max(waits.get(key, 0), val)
            for key in sorted(waits):
                if watermark.get(key, -1) < waits[key]:
                    eng.wait_ge(sems[key], waits[key])
                    watermark[key] = waits[key]
            inst = st["fn"](eng)
            if st["inc_val"] is not None:
                key = st["dma_sem"] if st["dma_sem"] is not None else st["eng"]
                assert inst is not None, f"{st['name']} needs inc, fn returned None"
                inst.then_inc(sems[key], 16 if st["dma_sem"] is not None else 1)


class Bump:
    def __init__(self, base):
        self.o = base

    def take(self, nbytes, name=""):
        r = self.o
        self.o = (r + nbytes + 31) & ~31
        assert self.o <= SB_LIMIT, f"SBUF overflow at {name}: {self.o}"
        return r


def build_program(S=4096, dbg=()):
    E, HID, H, D = 1024, 4096, 16, 64
    NP = H // 2
    TOK = S // 2
    NKB = S // 128
    NQB = TOK // 512
    NTIL = TOK // 128
    EC = E // 128
    HC = HID // 128
    NTG = S // 512
    WN = HID * E

    nc = bass.Bass("TRN2", target_bir_lowering=False)
    global SB_LIMIT
    SB_BASE = (nc._sbuf_addr_for_side("left") + 31) & ~31
    SB_LIMIT = nc._sbuf_addr_for_side("right")

    xkv_d = nc.dram_tensor("xkv", [S, E], F32, kind="ExternalInput")
    xq_d = nc.dram_tensor("xq", [TOK, E], F32, kind="ExternalInput")
    qkvw_d = nc.dram_tensor("qkv_w", [E, 3 * E], F32, kind="ExternalInput")
    qkvb_d = nc.dram_tensor("qkvb", [3, E], F32, kind="ExternalInput")
    outw_d = nc.dram_tensor("out_w", [E, E], F32, kind="ExternalInput")
    outb_d = nc.dram_tensor("out_b", [1, E], F32, kind="ExternalInput")
    w1_d = nc.dram_tensor("w1", [HID, E], F32, kind="ExternalInput")
    b1_d = nc.dram_tensor("b1", [1, HID], F32, kind="ExternalInput")
    w2_d = nc.dram_tensor("w2", [E, HID], F32, kind="ExternalInput")
    b2_d = nc.dram_tensor("b2", [1, E], F32, kind="ExternalInput")
    ln1w_d = nc.dram_tensor("ln1w", [1, E], F32, kind="ExternalInput")
    ln1b_d = nc.dram_tensor("ln1b", [1, E], F32, kind="ExternalInput")
    ln2w_d = nc.dram_tensor("ln2w", [1, HID], F32, kind="ExternalInput")
    ln2b_d = nc.dram_tensor("ln2b", [1, HID], F32, kind="ExternalInput")
    ident_d = nc.dram_tensor("ident", [128, 128], BF16, kind="ExternalInput")
    onesc_d = nc.dram_tensor("onesc", [128, 64], BF16, kind="ExternalInput")
    onesr_d = nc.dram_tensor("onesr", [1, 128], BF16, kind="ExternalInput")
    onesf_d = nc.dram_tensor("onesf", [128, 1], F32, kind="ExternalInput")
    dmask_d = nc.dram_tensor("dmask", [128, 64], BF16, kind="ExternalInput")
    out_d = nc.dram_tensor("out", [TOK, E], F32, kind="ExternalOutput")
    w2sp_d = nc.dram_tensor("w2spill", [HID, E], BF16)

    dbg_d = {}
    for nm, shape in dbg:
        dbg_d[nm] = nc.dram_tensor("dbg_" + nm, list(shape), F32,
                                   kind="ExternalOutput")

    def sbat(name, shape, dtype, offset):
        nbytes = int(np.prod(shape[1:])) * mybir.dt.size(dtype)
        assert offset + nbytes <= SB_LIMIT, f"{name}: {offset}+{nbytes}"
        return nc.alloc_sbuf_tensor_at(name, list(shape), dtype, offset=offset)

    KVX_BASE = SB_BASE
    kvx = sbat("kvx", [128, NKB * 2048], BF16, KVX_BASE)
    KVX_END = KVX_BASE + NKB * 4096

    def kt_ap(kb, pair, r0, r1):
        c = kb * 2048 + pair * 128
        return kvx[r0:r1, c: c + 128]

    def v_ap(kb, h):
        c = kb * 2048 + 1024 + h * 64
        return kvx[:, c: c + 64]

    def x1_ap(qb, tt, c0=0, c1=1024):
        c = 8 * qb * 2048 + tt * 1024
        return kvx[:, c + c0: c + c1]

    x1_slots = set()
    for j in range(NQB):
        x1_slots.update((8 * j, 8 * j + 1))
    free_slots = [p for p in range(NKB) if p not in x1_slots]
    w1t_pairs = []
    i = 0
    while i + 1 < len(free_slots) and len(w1t_pairs) < EC:
        if free_slots[i + 1] == free_slots[i] + 1:
            w1t_pairs.append(free_slots[i])
            i += 2
        else:
            i += 1
    used = set()
    for p in w1t_pairs:
        used.update((p, p + 1))
    w2_kvx_slots = [p for p in free_slots if p not in used]
    H2OFF = None
    for ii in range(len(w2_kvx_slots) - 1):
        if w2_kvx_slots[ii + 1] == w2_kvx_slots[ii] + 1:
            H2OFF = w2_kvx_slots[ii] * 2048
            del w2_kvx_slots[ii:ii + 2]
            break
    WST_EXTRA = []
    if len(w2_kvx_slots) >= 4:
        for _ in range(2):
            sl2 = w2_kvx_slots.pop()
            WST_EXTRA += [sl2 * 2048, sl2 * 2048 + 1024]
    NSTSL = 2 + len(WST_EXTRA)

    PASSA_EARLY = NQB >= 2
    wstA = None
    if PASSA_EARLY:
        PA_BSL = 8 * (NQB - 1) + 2
        PA_TGS = (2 * (NQB - 1), 2 * NQB - 1)
        wstA = sbat("wstA", [128, 4096], F32, KVX_BASE + PA_BSL * 4096)

    cb = Bump(KVX_END)
    ident = sbat("identb", [128, 128], BF16, cb.take(256, "id"))
    onesc = sbat("onescb", [128, 64], BF16, cb.take(128, "oc"))
    onesr = sbat("onesrb", [1, 128], BF16, cb.take(256, "or"))
    onesf = sbat("onesfb", [128, 1], F32, cb.take(32, "of"))
    dmask = sbat("dmaskb", [128, 64], BF16, cb.take(128, "dm"))
    qkvb_s = sbat("qkvb_s", [128, 3 * EC], F32, cb.take(3 * EC * 4, "qb"))
    cvals = sbat("cvals", [128, 128], F32, cb.take(512, "cv"))
    cvx = sbat("cvx", [128, 64], F32, cb.take(64 * 4, "cvx"))
    lnst = sbat("lnst", [128, 48], F32, cb.take(48 * 4, "lnst"))
    LBASE = cb.o

    a = Bump(LBASE)
    qkvw_sb = sbat("qkvw_sb", [128, EC * 3 * E], BF16, a.take(EC * 3 * E * 2, "qw"))
    xkvbf = sbat("xkvbf", [128, 2 * 4 * E], BF16, a.take(2 * 4 * E * 2, "xk"))
    xT1 = sbat("xT1", [128, EC * 512], BF16, a.take(EC * 512 * 2, "xT1"))
    vtmp = sbat("vtmp", [128, 512], BF16, a.take(512 * 2, "vt"))

    a = Bump(LBASE)
    qkvwQ = sbat("qkvwQ", [128, EC * E], BF16, a.take(EC * E * 2, "qQ"))
    outw_sb = sbat("outw_sb", [128, EC * E], BF16, a.take(EC * E * 2, "ow"))
    xqbf_off = a.take(4 * E * 2, "xq")
    xqbf = sbat("xqbf", [128, 4 * E], BF16, xqbf_off)
    xres = sbat("xres", [128, 2 * E], F32, xqbf_off)
    xT2_off = a.take(EC * 512 * 2, "xT2")
    xT2 = sbat("xT2", [128, EC * 512], BF16, xT2_off)
    trash2 = sbat("trash2", [128, E], F32, xT2_off)
    QT = sbat("QT", [128, NP * 512], BF16, a.take(NP * 512 * 2, "QT"))
    ptb = sbat("ptb", [128, 2 * 1024], BF16, a.take(2 * 1024 * 2, "pt"))
    attnT = sbat("attnT", [128, NP * 512], BF16, a.take(NP * 512 * 2, "aT"))
    recip = sbat("recip", [128, 512], F32, a.take(512 * 4, "rc"))
    outbr = sbat("outbr", [1, E], BF16, a.take(E * 2, "ob"))
    L2_END = a.o

    a = Bump(LBASE)
    wbuf = sbat("wbuf", [128, 2 * HID], F32, a.take(2 * HID * 4, "wb"))
    ybuf = sbat("ybuf", [128, 2048], F32, a.take(2048 * 4, "yb"))
    gbuf = sbat("gbuf", [128, 2048], BF16, a.take(2048 * 2, "gb"))
    ternb = sbat("ternb", [128, 2 * HID], BF16, a.take(2 * HID * 2, "tb"))
    wstage = sbat("wstage", [128, 2 * 1024], BF16, a.take(2 * 1024 * 2, "ws"))
    L3_END = a.o

    a = Bump(LBASE)
    ln2w_r = sbat("ln2w_r", [128, HID], BF16, a.take(HID * 2, "l2w"))
    ln2b_r = sbat("ln2b_r", [128, HID], BF16, a.take(HID * 2, "l2b"))
    ln1w_r = sbat("ln1w_r", [128, E], BF16, a.take(E * 2, "l1w"))
    ln1b_r = sbat("ln1b_r", [128, E], BF16, a.take(E * 2, "l1b"))
    h2b = sbat("h2b", [128, HID], BF16, a.take(HID * 2, "h2"))
    h2c = None
    abuf = sbat("abuf", [128, 4096], BF16, a.take(4096 * 2, "ab"))
    bbuf = sbat("bbuf", [128, 4096], BF16, a.take(4096 * 2, "bb"))
    ubuf = sbat("ubuf", [128, 4096], I8, a.take(4096, "ub"))
    q1t = sbat("q1t", [128, 2 * E], BF16, a.take(2 * E * 2, "q1t"))
    q2t = sbat("q2t", [128, 2 * E], BF16, a.take(2 * E * 2, "q2t"))
    rbuf = sbat("rbuf", [128, E], F32, a.take(E * 4, "rb4"))
    obuf = sbat("obuf", [128, E], F32, a.take(E * 4, "ob4"))
    wsbuf = sbat("wsbuf", [128, 2 * E], BF16, a.take(2 * E * 2, "wsb"))
    if H2OFF is None:
        h2c = sbat("h2c", [128, HID], BF16, a.take(HID * 2, "h2c"))
    n_w1_ext0 = EC - len(w1t_pairs)
    w1ext = None
    if n_w1_ext0 > 0:
        w1ext = sbat("w1ext", [128, n_w1_ext0 * HID], BF16,
                     a.take(n_w1_ext0 * HID * 2, "w1e"))
    a = Bump(max(L3_END, a.o))
    b1c = sbat("b1c", [1, HID], BF16, a.take(HID * 2, "b1c"))
    b2c = sbat("b2c", [1, E], BF16, a.take(E * 2, "b2c"))
    n_res_rb = max(0, min(HC - len(w2_kvx_slots) * 2,
                          (SB_LIMIT - a.o - 64) // (E * 2))) & ~7
    assert (len(w2_kvx_slots) * 2 + n_res_rb) % 8 == 0
    w2rb = None
    if n_res_rb > 0:
        w2rb = sbat("w2rb", [128, n_res_rb * E], BF16,
                    a.take(n_res_rb * E * 2, "w2rb"))
    L4_END = a.o

    w2place = {}
    half = 0
    for sl in w2_kvx_slots:
        for hh in range(2):
            if half < HC:
                w2place[half] = ("kvx", sl * 2048 + hh * 1024)
                half += 1
    for i2 in range(n_res_rb):
        if half < HC:
            w2place[half] = ("rb", i2)
            half += 1
    while half < HC:
        w2place[half] = ("stream", None)
        half += 1
    w2_streamed = [j for j in range(HC) if w2place[j][0] == "stream"]

    def w2_home_ap(j, c0, c1):
        kind, arg = w2place[j]
        if kind == "kvx":
            return kvx[:, arg + c0: arg + c1]
        if kind == "rb":
            return w2rb[:, arg * E + c0: arg * E + c1]
        return None

    def wst_ap(ssl, c0=0, c1=E):
        if ssl < 2:
            return wsbuf[:, ssl * E + c0: ssl * E + c1]
        off = WST_EXTRA[ssl - 2]
        return kvx[:, off + c0: off + c1]

    def w1t_ap(k, c0, c1):
        if k < len(w1t_pairs):
            base = w1t_pairs[k] * 2048
            return kvx[:, base + c0: base + c1]
        kk = k - len(w1t_pairs)
        return w1ext[:, kk * HID + c0: kk * HID + c1]

    ps = [nc.alloc_psum_tensor(f"ps{i}", [128, 512], F32) for i in range(6)]
    pt = [nc.alloc_psum_tensor(f"pt{i}", [128, 1024], BF16) for i in range(2)]

    sd = Sched()
    c0125 = f32(0.125)
    cSH = f32(-SHIFT)

    def dma(eng_name, dst, src, sem, *, reads=(), writes=(), name="",
            bulk=False, slow=False, lat=None):
        def fn(eng, dst=dst, src=src, slow=slow):
            if slow:
                return eng.dma_start(dst, src, allow_slow_non_contiguous=True)
            return eng.dma_start(dst, src)
        return sd.step(eng_name, fn, reads=reads, writes=writes, dma_sem=sem,
                       name=name, bulk=bulk, lat=lat)

    def kv_names(tg):
        return ([f"KV{tg}K{p}" for p in range(NP)] +
                [f"KV{tg}V{p}" for p in range(NP)])

    for t, d, nm in ((ident, ident_d, "identb"), (onesc, onesc_d, "onescb"),
                     (onesr, onesr_d, "onesrb"), (onesf, onesf_d, "onesfb"),
                     (dmask, dmask_d, "dmaskb")):
        dma("sync", t[:, :], d[:, :], "dc", writes=[nm],
            name="ld_" + nm, bulk=True)
    dma("sync", qkvb_s[:, :],
        qkvb_d[:, :].rearrange("t (c p) -> p (t c)", p=128),
        "dc", writes=["qkvb_s"], name="ld_qkvb", bulk=True, slow=True)
    for ec in range(EC):
        dma("gpsimd", qkvw_sb[:, ec * 3 * E:(ec + 1) * 3 * E],
            qkvw_d[ec * 128:(ec + 1) * 128, :], "dw",
            writes=[f"qkvw{ec}"], name=f"ld_qkvw{ec}", bulk=True, lat=8.0)

    for tg in range(NTG):
        sl = tg % 2
        dst_ap = bass.AP(xkvbf, sl * 4 * E, [[2 * 4 * E, 128], [E, 4], [1, E]])
        src_ap = bass.AP(xkv_d, tg * 512 * E, [[E, 128], [128 * E, 4], [1, E]])
        dma("gpsimd", dst_ap, src_ap, f"dx{sl}",
            reads=[f"xkvbf{sl}"], writes=[f"xkvbf{sl}"], name=f"p1ldx{tg}", lat=12.0)
        for ecn in range(EC):
            bk = ecn % 2
            def ftr(eng, sl=sl, ecn=ecn, bk=bk):
                r = None
                for t4 in range(4):
                    r = eng.transpose(
                        pt[bk][:, t4 * 128:(t4 + 1) * 128],
                        xkvbf[:, sl * 4 * E + t4 * E + ecn * 128:
                                 sl * 4 * E + t4 * E + (ecn + 1) * 128],
                        ident[:, :])
                return r
            sd.step("tensor", ftr,
                    reads=[f"xkvbf{sl}", "identb", f"pt{bk}"],
                    writes=[f"pt{bk}"], name=f"p1xt{tg}_{ecn}", dur=0.9)
            def fev(eng, ecn=ecn, bk=bk):
                return eng.tensor_copy(xT1[:, ecn * 512:(ecn + 1) * 512],
                                       pt[bk][:, 0:512])
            sd.step("vector", fev, reads=[f"pt{bk}"], writes=[f"xT1_{ecn}"],
                    name=f"p1xe{tg}_{ecn}", dur=0.35)
        for p in range(NP):
            for typ in (1, 2):
                bk = (p * 2 + typ - 1) % 2
                def fmm(eng, p=p, typ=typ, bk=bk):
                    r = None
                    for ec in range(EC):
                        r = eng.matmul(
                            ps[bk][:, 0:512],
                            qkvw_sb[:, ec * 3 * E + typ * E + p * 128:
                                       ec * 3 * E + typ * E + (p + 1) * 128],
                            xT1[:, ec * 512:(ec + 1) * 512],
                            start=(ec == 0), stop=(ec == EC - 1))
                    return r
                sd.step("tensor", fmm,
                        reads=[f"qkvw{e2}" for e2 in range(EC)] +
                              [f"xT1_{e2}" for e2 in range(EC)] + [f"ps{bk}"],
                        writes=[f"ps{bk}"], name=f"p1mm{tg}_{p}_{typ}", dur=2.0)
                if typ == 1:
                    def fke(eng, tg=tg, p=p, bk=bk):
                        dst = bass.AP(kvx, (tg * 4) * 2048 + p * 128,
                                      [[NKB * 2048, 128], [2048, 4], [1, 128]])
                        return eng.activation(dst, ps[bk][:, 0:512], AF.Identity,
                                              bias=qkvb_s[:, EC + p:EC + p + 1],
                                              scale=f32(1.0))
                    sd.step("scalar", fke, reads=[f"ps{bk}", "qkvb_s"],
                            writes=[f"KV{tg}K{p}"], name=f"p1ke{tg}_{p}", dur=0.45)
                else:
                    def fve(eng, p=p, bk=bk):
                        return eng.activation(
                            vtmp[:, 0:512], ps[bk][:, 0:512], AF.Identity,
                            bias=qkvb_s[:, 2 * EC + p:2 * EC + p + 1],
                            scale=f32(1.0))
                    sd.step("scalar", fve, reads=[f"ps{bk}", "qkvb_s", "vtmp"],
                            writes=["vtmp"], name=f"p1ve{tg}_{p}", dur=0.45)
                    vbk = p % 2
                    def fvt(eng, vbk=vbk):
                        r = None
                        for kq in range(4):
                            r = eng.transpose(
                                pt[vbk][:, kq * 128:(kq + 1) * 128],
                                vtmp[:, kq * 128:(kq + 1) * 128], ident[:, :])
                        return r
                    sd.step("tensor", fvt, reads=["vtmp", "identb", f"pt{vbk}"],
                            writes=[f"pt{vbk}"], name=f"p1vt{tg}_{p}", dur=0.5)
                    def fvv(eng, tg=tg, p=p, vbk=vbk):
                        dst = bass.AP(kvx, (tg * 4) * 2048 + 1024 + p * 128,
                                      [[NKB * 2048, 128], [2048, 4], [1, 128]])
                        return eng.tensor_copy(dst, pt[vbk][:, 0:512])
                    sd.step("vector", fvv, reads=[f"pt{vbk}"],
                            writes=[f"KV{tg}V{p}"], name=f"p1vv{tg}_{p}", dur=0.35)

    L1_NAMES = (["xkvbf0", "xkvbf1", "vtmp"] +
                [f"qkvw{e2}" for e2 in range(EC)] +
                [f"xT1_{e2}" for e2 in range(EC)])
    L2_NAMES = (["xqbf", "xres0", "xres1", "trash2", "ptb0", "ptb1", "recip",
                 "outbrow"] +
                [f"xT2_{e2}" for e2 in range(EC)] +
                [f"qwQ{e2}" for e2 in range(EC)] +
                [f"ow{e2}" for e2 in range(EC)] +
                [f"QT{p}" for p in range(NP)] +
                [f"attnT{p}" for p in range(NP)])
    def fbar12(eng):
        return eng.memset(cvals[:, 23:24], 0.0)
    sd.step("vector", fbar12, writes=L1_NAMES + L2_NAMES + ["cv23"], name="bar12")

    for ec in range(EC):
        dma("gpsimd", qkvwQ[:, ec * E:(ec + 1) * E],
            qkvw_d[ec * 128:(ec + 1) * 128, 0:E], "dw2",
            reads=["cv23"], writes=[f"qwQ{ec}"], name=f"ld_qwQ{ec}", bulk=True, lat=5.0)
    for ch in range(EC):
        dma("gpsimd", outw_sb[:, ch * E:(ch + 1) * E],
            outw_d[ch * 128:(ch + 1) * 128, :], "dw2",
            reads=["cv23"], writes=[f"ow{ch}"], name=f"ld_ow{ch}", bulk=True, lat=5.0)
    dma("gpsimd", outbr[0:1, :], outb_d[0:1, :], "dw2",
        reads=["cv23"], writes=["outbrow"], name="ld_outb", bulk=True)

    for qb in range(NQB - 1, -1, -1):
        nkb = 8 * (qb + 1)
        qraw = qb * 1024
        dst_ap = bass.AP(xqbf, 0, [[4 * E, 128], [E, 4], [1, E]])
        src_ap = bass.AP(xq_d, qb * 512 * E, [[E, 128], [128 * E, 4], [1, E]])
        dma("gpsimd", dst_ap, src_ap, "dq",
            reads=[], writes=["xqbf", "xres0", "xres1"], name=f"q{qb}ldx", lat=12.0)
        for ecn in range(EC):
            bk = ecn % 2
            def ftr(eng, ecn=ecn, bk=bk):
                r = None
                for t4 in range(4):
                    r = eng.transpose(
                        pt[bk][:, t4 * 128:(t4 + 1) * 128],
                        xqbf[:, t4 * E + ecn * 128: t4 * E + (ecn + 1) * 128],
                        ident[:, :])
                return r
            sd.step("tensor", ftr, reads=["xqbf", "identb", f"pt{bk}"],
                    writes=[f"pt{bk}"], name=f"q{qb}xt{ecn}", dur=0.9)
            def fev(eng, ecn=ecn, bk=bk):
                return eng.tensor_copy(xT2[:, ecn * 512:(ecn + 1) * 512],
                                       pt[bk][:, 0:512])
            sd.step("vector", fev, reads=[f"pt{bk}", "trash2"],
                    writes=[f"xT2_{ecn}"], name=f"q{qb}xe{ecn}", dur=0.35)
        for p in range(NP):
            bk = p % 2
            def fmm(eng, p=p, bk=bk):
                r = None
                for ec in range(EC):
                    r = eng.matmul(
                        ps[bk][:, 0:512],
                        qkvwQ[:, ec * E + p * 128: ec * E + (p + 1) * 128],
                        xT2[:, ec * 512:(ec + 1) * 512],
                        start=(ec == 0), stop=(ec == EC - 1))
                return r
            sd.step("tensor", fmm,
                    reads=[f"qwQ{e2}" for e2 in range(EC)] +
                          [f"xT2_{e2}" for e2 in range(EC)] + [f"ps{bk}"],
                    writes=[f"ps{bk}"], name=f"q{qb}qp{p}", dur=2.0)
            def fqe(eng, p=p, bk=bk):
                return eng.activation(QT[:, p * 512:(p + 1) * 512],
                                      ps[bk][:, 0:512], AF.Identity,
                                      bias=qkvb_s[:, p:p + 1], scale=f32(1.0))
            sd.step("scalar", fqe, reads=[f"ps{bk}", "qkvb_s", f"QT{p}"],
                    writes=[f"QT{p}"], name=f"q{qb}qe{p}", dur=0.45)
        for p in range(NP):
            def mk_scores(kb, p=p):
                off = kb * 128 - qraw
                noff = max(0, off) // 2
                sA, sB = 2 * ((kb + 1) % 2), 2 * ((kb + 1) % 2) + 1
                tgk = kb // 4
                def fsc(eng, p=p, kb=kb, noff=noff, sA=sA, sB=sB):
                    eng.matmul(ps[sA][:, noff:512], kt_ap(kb, p, 0, 64),
                               QT[0:64, p * 512 + noff:(p + 1) * 512],
                               start=True, stop=True, tile_position=(0, 0))
                    return eng.matmul(ps[sB][:, noff:512], kt_ap(kb, p, 64, 128),
                                      QT[64:128, p * 512 + noff:(p + 1) * 512],
                                      start=True, stop=True,
                                      tile_position=(64, 0))
                sd.step("tensor", fsc,
                        reads=[f"KV{tgk}K{p}", f"QT{p}", f"ps{sA}", f"ps{sB}"],
                        writes=[f"ps{sA}", f"ps{sB}"], name=f"q{qb}s{p}_{kb}", dur=0.42)
            mk_scores(0)
            for kb in range(nkb):
                off = kb * 128 - qraw
                noff = max(0, off) // 2
                sA, sB = 2 * ((kb + 1) % 2), 2 * ((kb + 1) % 2) + 1
                ptA = (kb % 2) * 1024
                ptB = ptA + 512
                tgk = kb // 4
                def fex(eng, noff=noff, sA=sA, sB=sB, ptA=ptA, ptB=ptB):
                    eng.activation(ptb[:, ptA + noff: ptA + 512],
                                   ps[sA][:, noff:512], AF.Exp,
                                   bias=cSH, scale=c0125)
                    return eng.activation(ptb[:, ptB + noff: ptB + 512],
                                          ps[sB][:, noff:512], AF.Exp,
                                          bias=cSH, scale=c0125)
                sd.step("scalar", fex,
                        reads=[f"ps{sA}", f"ps{sB}", f"ptb{kb % 2}"],
                        writes=[f"ptb{kb % 2}"], name=f"q{qb}e{p}_{kb}", dur=0.92, table="log")
                if off >= 0:
                    def fmk(eng, noff=noff, ptA=ptA, ptB=ptB):
                        eng.tensor_mul(ptb[:, ptA + noff: ptA + noff + 64],
                                       ptb[:, ptA + noff: ptA + noff + 64],
                                       dmask[:, :])
                        return eng.tensor_mul(
                            ptb[:, ptB + noff: ptB + noff + 64],
                            ptb[:, ptB + noff: ptB + noff + 64], dmask[:, :])
                    sd.step("vector", fmk, reads=["dmaskb", f"ptb{kb % 2}"],
                            writes=[f"ptb{kb % 2}"], name=f"q{qb}m{p}_{kb}", dur=0.2)
                if kb + 1 < nkb:
                    mk_scores(kb + 1)
                def flav(eng, p=p, kb=kb, nkb=nkb, noff=noff, ptA=ptA,
                         ptB=ptB):
                    eng.matmul(ps[5][0:64, noff:512], onesc[:, :],
                               ptb[:, ptA + noff: ptA + 512],
                               start=(kb == 0), stop=(kb == nkb - 1),
                               tile_position=(0, 0), skip_group_check=True)
                    eng.matmul(ps[5][64:128, noff:512], onesc[:, :],
                               ptb[:, ptB + noff: ptB + 512],
                               start=(kb == 0), stop=(kb == nkb - 1),
                               tile_position=(0, 64),
                               skip_group_check=True)
                    eng.matmul(ps[4][0:64, noff:512], v_ap(kb, 2 * p),
                               ptb[:, ptA + noff: ptA + 512],
                               start=(kb == 0), stop=(kb == nkb - 1),
                               tile_position=(0, 0), skip_group_check=True)
                    return eng.matmul(ps[4][64:128, noff:512],
                                      v_ap(kb, 2 * p + 1),
                                      ptb[:, ptB + noff: ptB + 512],
                                      start=(kb == 0), stop=(kb == nkb - 1),
                                      tile_position=(0, 64),
                                      skip_group_check=True)
                sd.step("tensor", flav,
                        reads=["onescb", f"KV{tgk}V{p}", f"ptb{kb % 2}",
                               "ps4", "ps5"],
                        writes=["ps4", "ps5"], name=f"q{qb}la{p}_{kb}",
                        dur=0.8)
            def fln(eng):
                return eng.activation(recip[:, :], ps[5][:, 0:512], AF.Ln,
                                      bias=f32(0.0), scale=f32(1.0))
            sd.step("scalar", fln, reads=["ps5", "recip"], writes=["recip"],
                    name=f"q{qb}rl{p}", dur=0.45, table="log")
            def fexr(eng):
                return eng.activation(recip[:, :], recip[:, :], AF.Exp,
                                      bias=f32(0.0), scale=f32(-1.0))
            sd.step("scalar", fexr, reads=["recip"], writes=["recip"],
                    name=f"q{qb}re{p}", dur=0.45, table="log")
            def fat(eng, p=p):
                return eng.tensor_mul(attnT[:, p * 512:(p + 1) * 512],
                                      ps[4][:, 0:512], recip[:, :])
            sd.step("vector", fat, reads=["ps4", "recip", f"attnT{p}"],
                    writes=[f"attnT{p}"], name=f"q{qb}at{p}", dur=0.35)
        for tt in range(4):
            sl = tt % 2
            dma("sync", xres[:, sl * E:(sl + 1) * E],
                xq_d[qb * 512 + tt * 128: qb * 512 + (tt + 1) * 128, :],
                f"dr{sl}", reads=[f"xres{sl}"],
                writes=[f"xres{sl}", "xqbf"], name=f"q{qb}xr{tt}", lat=5.0)
            for nn in range(2):
                pb = 4 + nn
                def fop(eng, tt=tt, nn=nn, pb=pb):
                    for p in range(NP):
                        eng.matmul(ps[pb][:, 0:512],
                                   attnT[:, p * 512 + tt * 128:
                                            p * 512 + (tt + 1) * 128],
                                   outw_sb[:, p * E + nn * 512:
                                              p * E + nn * 512 + 512],
                                   start=(p == 0), stop=False)
                    return eng.matmul(ps[pb][:, 0:512], onesr[0:1, :],
                                      outbr[0:1, nn * 512:(nn + 1) * 512],
                                      start=False, stop=True)
                sd.step("tensor", fop,
                        reads=[f"attnT{p}" for p in range(NP)] +
                              [f"ow{c2}" for c2 in range(EC)] +
                              ["onesrb", "outbrow", f"ps{pb}"],
                        writes=[f"ps{pb}"], name=f"q{qb}op{tt}_{nn}", dur=2.0)
            def fres(eng, sl=sl):
                eng.scalar_tensor_tensor(xres[:, sl * E: sl * E + 512],
                                         ps[4][:, 0:512], 1.0,
                                         xres[:, sl * E: sl * E + 512],
                                         ALU.bypass, ALU.add)
                return eng.scalar_tensor_tensor(
                    xres[:, sl * E + 512:(sl + 1) * E], ps[5][:, 0:512], 1.0,
                    xres[:, sl * E + 512:(sl + 1) * E], ALU.bypass, ALU.add)
            sd.step("vector", fres, reads=["ps4", "ps5", f"xres{sl}"],
                    writes=[f"xres{sl}"], name=f"q{qb}rs{tt}", dur=0.7)
            KC = (0, 118)[tt % 2]
            def fsum(eng, sl=sl, KC=KC):
                return eng.reduce_sum(cvals[:, KC:KC + 1],
                                      xres[:, sl * E:(sl + 1) * E], AX.X)
            sd.step("vector", fsum, reads=[f"xres{sl}", f"cv{KC}"],
                    writes=[f"cv{KC}"], name=f"q{qb}st{tt}a", dur=0.6)
            def fsq(eng, sl=sl, KC=KC):
                return eng.scalar_tensor_tensor(
                    trash2[:, 0:E], xres[:, sl * E:(sl + 1) * E], 1.0,
                    xres[:, sl * E:(sl + 1) * E], ALU.bypass, ALU.mult,
                    accum_out=cvals[:, KC + 1:KC + 2])
            sd.step("vector", fsq, reads=[f"xres{sl}", f"cv{KC + 1}"],
                    writes=[f"cv{KC + 1}", "trash2"] +
                           [f"xT2_{e2}" for e2 in range(EC)],
                    name=f"q{qb}st{tt}b", dur=0.7)
            def fmv1(eng, KC=KC):
                eng.tensor_scalar(cvals[:, KC + 2:KC + 3], cvals[:, KC:KC + 1],
                                  float(1.0 / E), None, ALU.mult)
                return eng.tensor_scalar(cvals[:, KC + 3:KC + 4],
                                         cvals[:, KC + 1:KC + 2],
                                         float(1.0 / E), None, ALU.mult)
            sd.step("vector", fmv1,
                    reads=[f"cv{KC}", f"cv{KC + 1}", f"cv{KC + 2}",
                           f"cv{KC + 3}"],
                    writes=[f"cv{KC + 2}", f"cv{KC + 3}"],
                    name=f"q{qb}st{tt}c1")
            def fmv2(eng, KC=KC):
                return eng.tensor_mul(cvals[:, KC + 4:KC + 5],
                                      cvals[:, KC + 2:KC + 3],
                                      cvals[:, KC + 2:KC + 3])
            sd.step("vector", fmv2, reads=[f"cv{KC + 2}", f"cv{KC + 4}"],
                    writes=[f"cv{KC + 4}"], name=f"q{qb}st{tt}c2")
            def fmv3(eng, KC=KC):
                return eng.tensor_sub(cvals[:, KC + 5:KC + 6],
                                      cvals[:, KC + 3:KC + 4],
                                      cvals[:, KC + 4:KC + 5])
            sd.step("vector", fmv3,
                    reads=[f"cv{KC + 3}", f"cv{KC + 4}", f"cv{KC + 5}"],
                    writes=[f"cv{KC + 5}"], name=f"q{qb}st{tt}c3")
            def frs1(eng, KC=KC):
                return eng.activation(cvals[:, KC + 6:KC + 7],
                                      cvals[:, KC + 5:KC + 6], AF.Ln,
                                      bias=f32(EPS), scale=f32(1.0))
            sd.step("scalar", frs1, reads=[f"cv{KC + 5}", f"cv{KC + 6}"],
                    writes=[f"cv{KC + 6}"], name=f"q{qb}st{tt}d", table="log")
            def frs2(eng, KC=KC):
                return eng.activation(cvals[:, KC + 7:KC + 8],
                                      cvals[:, KC + 6:KC + 7], AF.Exp,
                                      bias=f32(0.0), scale=f32(-0.5))
            sd.step("scalar", frs2, reads=[f"cv{KC + 6}", f"cv{KC + 7}"],
                    writes=[f"cv{KC + 7}"], name=f"q{qb}st{tt}e", table="log")
            def fx1(eng, qb=qb, tt=tt, sl=sl, KC=KC):
                rsb = bass.AP(cvals, KC + 7, [[128, 128], [0, E]])
                return eng.scalar_tensor_tensor(
                    x1_ap(qb, tt), xres[:, sl * E:(sl + 1) * E],
                    cvals[:, KC + 2:KC + 3], rsb,
                    ALU.subtract, ALU.mult,
                    accum_out=cvx[:, 4 * qb + tt: 4 * qb + tt + 1])
            sd.step("vector", fx1,
                    reads=[f"xres{sl}", f"cv{KC + 2}", f"cv{KC + 7}"] +
                          kv_names(2 * qb),
                    writes=[f"x1_{qb}_{tt}", f"cvx{4 * qb + tt}"] +
                           kv_names(2 * qb),
                    name=f"q{qb}x1_{tt}", dur=0.7)
            if "x1" in dbg_d:
                dma("gpsimd",
                    dbg_d["x1"][qb * 512 + tt * 128: qb * 512 + (tt + 1) * 128, :],
                    x1_ap(qb, tt), "dbg", reads=[f"x1_{qb}_{tt}"],
                    name=f"dbgx1_{qb}_{tt}")

    if PASSA_EARLY:
        for wn, (wd, nrows, ncols) in enumerate(((w1_d, HID, E), (w2_d, E, HID))):
            acc, p0 = (52, 53) if wn == 0 else (54, 55)
            ntile = nrows // 128
            nch = max(1, ncols // 2048)
            wch_a = ncols // nch
            ldlat = 5.0 if ncols == E else 12.0
            def fz(eng, acc=acc):
                return eng.memset(cvx[:, acc:acc + 1], 0.0)
            sd.step("vector", fz, reads=[f"cvx{acc}"], writes=[f"cvx{acc}"],
                    name=f"e{wn}z")
            for ti in range(ntile):
                if wn == 0:
                    j = ti % 4
                    dstap = wstA[:, j * 1024:(j + 1) * 1024]
                    wr = [f"wsl{j}"]
                    sem = f"dvE{j}"
                else:
                    dstap = wstA[:, :]
                    wr = [f"wsl{k}" for k in range(4)]
                    sem = "dvE0"
                kvw = []
                if wn == 0 and ti < 4:
                    kvw = kv_names(PA_TGS[0]) + kv_names(PA_TGS[1])
                dma("sync", dstap, wd[ti * 128:(ti + 1) * 128, :], sem,
                    reads=wr, writes=wr + kvw, name=f"e{wn}l{ti}", lat=ldlat)
                for aa in range(nch):
                    if wn == 0:
                        seg = wstA[:, (ti % 4) * 1024:(ti % 4 + 1) * 1024]
                    else:
                        seg = wstA[:, aa * wch_a:(aa + 1) * wch_a]
                    def fabs(eng, seg=seg, aa=aa, p0=p0):
                        return eng.activation(
                            seg, seg, AF.Abs, bias=f32(0.0), scale=f32(1.0),
                            accum_out=cvx[:, p0 + aa:p0 + aa + 1])
                    sd.step("scalar", fabs,
                            reads=wr + [f"cvx{p0 + aa}"],
                            writes=wr + [f"cvx{p0 + aa}"],
                            name=f"e{wn}abs{ti}_{aa}", dur=0.9 * wch_a / 1024)
                for aa in range(nch):
                    def facc(eng, aa=aa, acc=acc, p0=p0):
                        return eng.tensor_add(cvx[:, acc:acc + 1],
                                              cvx[:, acc:acc + 1],
                                              cvx[:, p0 + aa:p0 + aa + 1])
                    sd.step("vector", facc,
                            reads=[f"cvx{acc}", f"cvx{p0 + aa}"],
                            writes=[f"cvx{acc}"], name=f"e{wn}acc{ti}_{aa}")

    L3_NAMES = ([f"wb{j}" for j in range(8)] +
                ["ybuf", "gbuf", "ternb0", "ternb1",
                 "wstage0", "wstage1", "w1t", "w2t", "b1crow", "b2crow"] +
                [f"cv{i3}" for i3 in (8, 9, 10, 12, 13, 14, 15,
                                      114, 115, 116, 117)])
    def fbar23(eng):
        return eng.memset(cvals[:, 23:24], 0.0)
    allkv = []
    for tg2 in range(NTG):
        allkv += kv_names(tg2)
    sd.step("vector", fbar23,
            writes=L2_NAMES + L3_NAMES + allkv + ["cv23"], name="bar23")
    PH3_FENCE = len(sd.steps)

    for wn, (wd, nrows, ncols) in enumerate(((w1_d, HID, E), (w2_d, E, HID))):
        ntile = nrows // 128
        nsl = (2 * HID) // ncols
        ldlat = 5.0 if ncols == E else 12.0
        nch = max(1, ncols // 2048)
        wch_a = ncols // nch
        if not PASSA_EARLY:
            def fz(eng):
                return eng.memset(cvals[:, 8:9], 0.0)
            sd.step("vector", fz, reads=["cv8"], writes=["cv8"], name=f"w{wn}z")
            for ti in range(ntile):
                sl = ti % nsl
                grA = [f"wb{j}" for j in range(sl * ncols // 1024,
                                               (sl + 1) * ncols // 1024)]
                dma("sync", wbuf[:, sl * ncols: (sl + 1) * ncols],
                    wd[ti * 128:(ti + 1) * 128, :], f"dvA{sl}",
                    reads=grA + ["cv23"], writes=grA,
                    name=f"w{wn}lA{ti}", lat=ldlat)
                for aa in range(nch):
                    def fabs(eng, sl=sl, aa=aa, wch=wch_a, ncols=ncols):
                        return eng.activation(
                            ybuf[:, 0:wch],
                            wbuf[:, sl * ncols + aa * wch:
                                    sl * ncols + (aa + 1) * wch],
                            AF.Abs, bias=f32(0.0), scale=f32(1.0),
                            accum_out=cvals[:, 9 + aa:10 + aa])
                    grC = [f"wb{j}" for j in
                           range((sl * ncols + aa * wch_a) // 1024,
                                 (sl * ncols + (aa + 1) * wch_a) // 1024)]
                    sd.step("scalar", fabs,
                            reads=grC + [f"cv{9 + aa}", "ybuf"],
                            writes=[f"cv{9 + aa}", "ybuf"],
                            name=f"w{wn}abs{ti}_{aa}", dur=0.9 * wch_a / 1024)
                for aa in range(nch):
                    def facc(eng, aa=aa):
                        return eng.tensor_add(cvals[:, 8:9], cvals[:, 8:9],
                                              cvals[:, 9 + aa:10 + aa])
                    sd.step("vector", facc, reads=["cv8", f"cv{9 + aa}"],
                            writes=["cv8"], name=f"w{wn}acc{ti}_{aa}")
        if PASSA_EARLY:
            acc2 = 52 if wn == 0 else 54
            totsrc, totread = bass.AP(cvx, acc2, [[64, 128], [0, 128]]), f"cvx{acc2}"
        else:
            totsrc, totread = bass.AP(cvals, 8, [[128, 128], [0, 128]]), "cv8"
        def ftot(eng, totsrc=totsrc):
            return eng.matmul(ps[0][:, 0:1], totsrc, onesf[:, :],
                              start=True, stop=True)
        sd.step("tensor", ftot, reads=[totread, "onesfb", "ps0"], writes=["ps0"],
                name=f"w{wn}tot")
        def fc(eng, wn=wn):
            return eng.tensor_scalar(cvals[:, 12 + wn:13 + wn], ps[0][:, 0:1],
                                     float(1.0 / WN), float(EPS), ALU.mult, ALU.max)
        sd.step("vector", fc, reads=["ps0", f"cv1{2 + wn}"],
                writes=[f"cv1{2 + wn}"], name=f"w{wn}c")
        def frc(eng, wn=wn):
            return eng.reciprocal(cvals[:, 14 + wn:15 + wn],
                                  cvals[:, 12 + wn:13 + wn])
        sd.step("vector", frc, reads=[f"cv1{2 + wn}", f"cv1{4 + wn}"],
                writes=[f"cv1{4 + wn}"], name=f"w{wn}rc")
        def fph(eng, wn=wn):
            return eng.tensor_scalar(cvals[:, 114 + wn:115 + wn],
                                     cvals[:, 12 + wn:13 + wn], 0.5,
                                     None, ALU.mult)
        sd.step("vector", fph, reads=[f"cv1{2 + wn}", f"cv11{4 + wn}"],
                writes=[f"cv11{4 + wn}"], name=f"w{wn}ph")
        def fmh(eng, wn=wn):
            return eng.tensor_scalar(cvals[:, 116 + wn:117 + wn],
                                     cvals[:, 12 + wn:13 + wn], -0.5,
                                     None, ALU.mult)
        sd.step("vector", fmh, reads=[f"cv1{2 + wn}", f"cv11{6 + wn}"],
                writes=[f"cv11{6 + wn}"], name=f"w{wn}mh")
        for ti in range(ntile):
            sl = ti % nsl
            tn = f"ternb{ti % 2}"
            tno = (ti % 2) * HID
            grB = [f"wb{j}" for j in range(sl * ncols // 1024,
                                           (sl + 1) * ncols // 1024)]
            qeng, qsem = (("scalar", f"dvB{sl}") if ti % 2 == 0
                          else ("sync", f"dvC{sl}"))
            dma(qeng, wbuf[:, sl * ncols: (sl + 1) * ncols],
                wd[ti * 128:(ti + 1) * 128, :], qsem,
                reads=grB, writes=grB,
                name=f"w{wn}lB{ti}", lat=ldlat)
            wch = min(2048, ncols)
            for hh in range(ncols // wch):
                def fg(eng, sl=sl, hh=hh, wch=wch, wn=wn, ncols=ncols):
                    return eng.tensor_scalar(
                        gbuf[:, 0:wch],
                        wbuf[:, sl * ncols + hh * wch:
                                sl * ncols + (hh + 1) * wch],
                        cvals[:, 116 + wn:117 + wn], None, ALU.is_lt)
                grD = [f"wb{j}" for j in
                       range((sl * ncols + hh * wch) // 1024,
                             (sl * ncols + (hh + 1) * wch) // 1024)]
                sd.step("vector", fg,
                        reads=grD + [f"cv11{6 + wn}", "gbuf"],
                        writes=["gbuf"], name=f"w{wn}g{ti}_{hh}",
                        dur=0.65 * wch / 1024)
                def ftn(eng, sl=sl, hh=hh, wch=wch, wn=wn, ncols=ncols,
                        tno=tno):
                    return eng.scalar_tensor_tensor(
                        ternb[:, tno + hh * wch: tno + (hh + 1) * wch],
                        wbuf[:, sl * ncols + hh * wch:
                                sl * ncols + (hh + 1) * wch],
                        cvals[:, 114 + wn:115 + wn],
                        gbuf[:, 0:wch], ALU.is_gt, ALU.subtract)
                sd.step("vector", ftn,
                        reads=grD + [f"cv11{4 + wn}", "gbuf", tn],
                        writes=[tn], name=f"w{wn}t{ti}_{hh}",
                        dur=0.65 * wch / 1024)
            ngr = ncols // 1024
            for g in range(ngr):
                bk = (ti * ngr + g) % 2
                is_stream = (wn == 1 and w2place[g * 8][0] == "stream")
                def ftr(eng, g=g, bk=bk, tno=tno):
                    r = None
                    for k8 in range(8):
                        r = eng.transpose(
                            pt[bk][:, k8 * 128:(k8 + 1) * 128],
                            ternb[:, tno + g * 1024 + k8 * 128:
                                     tno + g * 1024 + (k8 + 1) * 128],
                            ident[:, :])
                    return r
                sd.step("tensor", ftr, reads=[tn, "identb", f"pt{bk}"],
                        writes=[f"pt{bk}"], name=f"w{wn}tr{ti}_{g}", dur=0.9)
                if is_stream:
                    wsl = (ti * ngr + g) % 2
                    def fev(eng, bk=bk, wsl=wsl):
                        return eng.activation(
                            wstage[:, wsl * 1024:(wsl + 1) * 1024],
                            pt[bk][:, 0:1024], AF.Identity,
                            bias=f32(0.0), scale=f32(1.0))
                    sd.step("scalar", fev,
                            reads=[f"pt{bk}", f"wstage{wsl}"],
                            writes=[f"wstage{wsl}"],
                            name=f"w2evs{ti}_{g}", dur=0.9)
                    goff = (g * 8 * 128) * E + ti * 128
                    dstap = bass.AP(w2sp_d, goff,
                                    [[E, 128], [128 * E, 8], [1, 128]])
                    dma("gpsimd", dstap,
                        wstage[:, wsl * 1024:(wsl + 1) * 1024], f"dsp{wsl}",
                        reads=[f"wstage{wsl}"],
                        writes=[f"wsp{g * 8 + k8}" for k8 in range(8)],
                        name=f"w2sp{ti}_{g}", lat=4.0)
                else:
                    for k8 in range(8):
                        cc = g * 8 + k8
                        if wn == 0:
                            dst = w1t_ap(cc, ti * 128, (ti + 1) * 128)
                            wres = "w1t"
                        else:
                            dst = w2_home_ap(cc, ti * 128, (ti + 1) * 128)
                            wres = "w2t"
                        eng_ev = ("vector", "scalar")[k8 % 2]
                        if eng_ev == "vector":
                            def fev(eng, dst=dst, bk=bk, k8=k8):
                                return eng.tensor_copy(
                                    dst, pt[bk][:, k8 * 128:(k8 + 1) * 128])
                        else:
                            def fev(eng, dst=dst, bk=bk, k8=k8):
                                return eng.activation(
                                    dst, pt[bk][:, k8 * 128:(k8 + 1) * 128],
                                    AF.Identity, bias=f32(0.0), scale=f32(1.0))
                        paf = (["cvx54", "wsl0", "wsl1"]
                               if PASSA_EARLY else [])
                        sd.step(eng_ev, fev, reads=[f"pt{bk}", wres] + paf,
                                writes=[wres], name=f"w{wn}ev{ti}_{cc}",
                                dur=0.16)
    for wn, (bd, ncols, dstt) in enumerate(((b1_d, HID, b1c), (b2_d, E, b2c))):
        grE = [f"wb{j}" for j in range(ncols // 1024)]
        dma("sync", wbuf[0:1, 0:ncols], bd[0:1, :], "dv0",
            reads=grE, writes=grE, name=f"b{wn}ld")
        def fb(eng, wn=wn, ncols=ncols, dstt=dstt):
            return eng.tensor_scalar(dstt[0:1, 0:ncols], wbuf[0:1, 0:ncols],
                                     cvals[0:1, 14 + wn:15 + wn], None, ALU.mult)
        sd.step("vector", fb, reads=["wb0", f"cv1{4 + wn}", f"b{wn + 1}crow"],
                writes=[f"b{wn + 1}crow"], name=f"b{wn}c")

    def fspf(eng):
        return eng.memset(cvals[0:1, 11:12], 0.0)
    sd.step("gpsimd", fspf,
            reads=[f"wsp{cc2}" for cc2 in w2_streamed], writes=["spfence"],
            name="spfence")

    L4_NAMES = (["ln2wrx", "ln2brx", "ln1wrx", "ln1brx", "q1t0", "q1t1",
                 "q2t0", "q2t1", "rb0", "rb1", "obuf"] +
                [f"{pfx}{p5}{q5}" for pfx in ("ab", "bb", "ub")
                 for p5 in range(2) for q5 in range(2)] +
                [f"h2b{p5}h{h5}" for p5 in range(2) for h5 in range(2)] +
                [f"h2b{p5}q{q5}" for p5 in range(2) for q5 in range(4)] +
                [f"m2p{k5}" for k5 in range(8)] +
                [f"wst{s5}" for s5 in range(NSTSL)] +
                [f"cv{i4}" for i4 in (0, 1, 2, 5, 6, 7, 16, 17, 18, 19, 20,
                                      21, 22)])
    def fbar34(eng):
        return eng.memset(cvals[:, 23:24], 0.0)
    sd.step("vector", fbar34,
            writes=L3_NAMES + L4_NAMES + ["cv23"] +
                   [f"xT2_{e2}" for e2 in range(EC)], name="bar34")

    for rowd, dstt, nn2, wnm in ((ln2w_d, ln2w_r, HID, "ln2wrx"),
                                 (ln2b_d, ln2b_r, HID, "ln2brx"),
                                 (ln1w_d, ln1w_r, E, "ln1wrx"),
                                 (ln1b_d, ln1b_r, E, "ln1brx")):
        srcap = bass.AP(rowd, 0, [[0, 128], [1, nn2]])
        dma("gpsimd", dstt[:, :], srcap, "dl",
            reads=["cv23"], writes=[wnm], name="ld_" + wnm, bulk=True)

    PH4_FENCE = len(sd.steps)

    LN1_OK = PASSA_EARLY and NTIL <= 16 and len(WST_EXTRA) >= 1
    if LN1_OK:
        for t6 in range(NTIL):
            qb6, tt6 = t6 // 4, t6 % 4
            kvw6 = kv_names(PA_TGS[1]) if t6 == 0 else []
            def fsb6(eng, qb6=qb6, tt6=tt6, t6=t6):
                return eng.activation(wst_ap(NSTSL - 2), x1_ap(qb6, tt6),
                                      AF.Square, bias=f32(0.0), scale=f32(1.0),
                                      accum_out=lnst[:, t6:t6 + 1])
            sd.step("scalar", fsb6,
                    reads=[f"x1_{qb6}_{tt6}", f"wst{NSTSL - 2}",
                           f"lnst{t6}"],
                    writes=[f"wst{NSTSL - 2}", f"lnst{t6}"] + kvw6,
                    name=f"pp{t6}sb", dur=0.9)
        NT = NTIL
        def fmvA(eng, NT=NT):
            eng.tensor_scalar(lnst[:, 0:NT], lnst[:, 0:NT],
                              float(1.0 / E), None, ALU.mult)
            return eng.tensor_scalar(lnst[:, 16:16 + NT],
                                     cvx[:, 0:NT],
                                     float(1.0 / E), None, ALU.mult)
        sd.step("vector", fmvA,
                reads=[f"lnst{t7}" for t7 in range(NT)] +
                      [f"cvx{t7}" for t7 in range(NT)] + ["lnb0", "lnb1"],
                writes=["lnb0", "lnb1"], name="ppmv1", dur=0.1)
        def fmvB(eng, NT=NT):
            return eng.scalar_tensor_tensor(
                lnst[:, 32:32 + NT], lnst[:, 16:16 + NT], 1.0,
                lnst[:, 16:16 + NT], ALU.bypass, ALU.mult)
        sd.step("vector", fmvB, reads=["lnb1", "lnb2"],
                writes=["lnb2"], name="ppmv2", dur=0.1)
        def fmvC(eng, NT=NT):
            return eng.tensor_sub(lnst[:, 0:NT], lnst[:, 0:NT],
                                  lnst[:, 32:32 + NT])
        sd.step("vector", fmvC, reads=["lnb0", "lnb2"],
                writes=["lnb0"], name="ppmv3", dur=0.1)
        def frsA(eng, NT=NT):
            return eng.activation(lnst[:, 32:32 + NT], lnst[:, 0:NT],
                                  AF.Ln, bias=f32(EPS), scale=f32(1.0))
        sd.step("scalar", frsA, reads=["lnb0", "lnb2"], writes=["lnb2"],
                name="pprsA", dur=0.1, table="log")
        def frsB(eng, NT=NT):
            return eng.activation(lnst[:, 0:NT], lnst[:, 32:32 + NT],
                                  AF.Exp, bias=f32(0.0), scale=f32(-0.5))
        sd.step("scalar", frsB, reads=["lnb2", "lnb0"], writes=["lnb0"],
                name="pprsB", dur=0.1, table="log")
        def fnbA(eng, NT=NT):
            return eng.scalar_tensor_tensor(
                lnst[:, 32:32 + NT], lnst[:, 16:16 + NT], -1.0,
                lnst[:, 0:NT], ALU.mult, ALU.mult)
        sd.step("vector", fnbA, reads=["lnb1", "lnb0", "lnb2"],
                writes=["lnb2"], name="ppnb", dur=0.1)

    def _tile_ctx(ti):
        qb, tt = ti // 4, ti % 4
        par = ti % 2
        pp = par * 2048
        pq = par * E
        AB, BB, UB, QN = f"abuf{par}", f"bbuf{par}", f"ubuf{par}", f"q1t{par}"
        if par == 0:
            Csum, Csq, Cmu, Cex, Cms, Cvr, Cln, Crs, Cnb = 0, 1, 2, 3, 4, 5, 6, 7, 22
            Cm, Cmp, Crm, Cs, Cis = 16, 17, 18, 19, 20
        else:
            Csum, Csq, Cmu, Cex, Cms, Cvr, Cln, Crs, Cnb = 50, 51, 52, 53, 54, 55, 56, 57, 58
            Cm, Cmp, Crm, Cs, Cis = 59, 60, 61, 62, 63
        Q = 64 * par
        HN = f"h2b{par}"
        HNh = [f"h2b{par}h0", f"h2b{par}h1"]
        HNq = [f"h2b{par}q{qq}" for qq in range(4)]
        A_ = [f"ab{par}{qq}" for qq in range(2)]
        B_ = [f"bb{par}{qq}" for qq in range(2)]
        U_ = [f"ub{par}{qq}" for qq in range(2)]
        def cv(c):
            return cvals[:, c:c + 1]
        def h2(c0, c1):
            if par == 0:
                return h2b[:, c0:c1]
            if H2OFF is not None:
                return kvx[:, H2OFF + c0: H2OFF + c1]
            return h2c[:, c0:c1]
        x1r = [f"x1_{qb}_{tt}"]
        return dict(locals())

    def emit_ffn_front(ti):
        C = _tile_ctx(ti)
        qb, tt, par, pp, pq = C["qb"], C["tt"], C["par"], C["pp"], C["pq"]
        AB, BB, UB, QN = C["AB"], C["BB"], C["UB"], C["QN"]
        A_, B_, U_ = C["A_"], C["B_"], C["U_"]
        Csum, Csq, Cmu, Cex, Cms = C["Csum"], C["Csq"], C["Cmu"], C["Cex"], C["Cms"]
        Cvr, Cln, Crs, Cnb = C["Cvr"], C["Cln"], C["Crs"], C["Cnb"]
        Cm, Cmp, Crm, Cs, Cis = C["Cm"], C["Cmp"], C["Crm"], C["Cs"], C["Cis"]
        cv = C["cv"]
        x1r = C["x1r"]
        def fsq(eng):
            return eng.activation(abuf[:, pp: pp + E], x1_ap(qb, tt),
                                  AF.Square, bias=f32(0.0), scale=f32(1.0),
                                  accum_out=cv(Csq))
        if not LN1_OK:
            sd.step("scalar", fsq, reads=x1r + [A_[0], f"cv{Csq}"],
                    writes=[f"cv{Csq}", A_[0]], name=f"f{ti}sb", dur=0.9)
        def fmv1(eng):
            eng.tensor_scalar(cv(Cmu), cvx[:, ti:ti + 1], float(1.0 / E),
                              None, ALU.mult)
            return eng.tensor_scalar(cv(Cex), cv(Csq), float(1.0 / E),
                                     None, ALU.mult)
        if not LN1_OK:
            sd.step("vector", fmv1, reads=[f"cvx{ti}", f"cv{Csq}", f"cv{Cmu}",
                                           f"cv{Cex}"],
                    writes=[f"cv{Cmu}", f"cv{Cex}"], name=f"f{ti}sc1")
        def fmv2(eng):
            return eng.tensor_mul(cv(Cms), cv(Cmu), cv(Cmu))
        if not LN1_OK:
            sd.step("vector", fmv2, reads=[f"cv{Cmu}", f"cv{Cms}"],
                    writes=[f"cv{Cms}"], name=f"f{ti}sc2")
        def fmv3(eng):
            return eng.tensor_sub(cv(Cvr), cv(Cex), cv(Cms))
        if not LN1_OK:
            sd.step("vector", fmv3, reads=[f"cv{Cex}", f"cv{Cms}", f"cv{Cvr}"],
                    writes=[f"cv{Cvr}"], name=f"f{ti}sc3")
        def frs1(eng):
            return eng.activation(cv(Cln), cv(Cvr), AF.Ln,
                                  bias=f32(EPS), scale=f32(1.0))
        if not LN1_OK:
            sd.step("scalar", frs1, reads=[f"cv{Cvr}", f"cv{Cln}"],
                    writes=[f"cv{Cln}"], name=f"f{ti}sd", table="log")
        def frs2(eng):
            return eng.activation(cv(Crs), cv(Cln), AF.Exp,
                                  bias=f32(0.0), scale=f32(-0.5))
        if not LN1_OK:
            sd.step("scalar", frs2, reads=[f"cv{Cln}", f"cv{Crs}"],
                    writes=[f"cv{Crs}"], name=f"f{ti}se", table="log")
        def fnb(eng):
            return eng.scalar_tensor_tensor(cv(Cnb), cv(Cmu), -1.0, cv(Crs),
                                            ALU.mult, ALU.mult)
        if not LN1_OK:
            sd.step("vector", fnb, reads=[f"cv{Cmu}", f"cv{Crs}", f"cv{Cnb}"],
                    writes=[f"cv{Cnb}"], name=f"f{ti}sf")
        if LN1_OK:
            nbap, rsap = lnst[:, 32 + ti:33 + ti], lnst[:, ti:ti + 1]
            nbrd = ["lnb2", "lnb0"]
        else:
            nbap, rsap = cv(Cnb), cv(Crs)
            nbrd = [f"cv{Cnb}", f"cv{Crs}"]
        def fa1(eng, nbap=nbap, rsap=rsap):
            return eng.activation(abuf[:, pp: pp + E], x1_ap(qb, tt),
                                  AF.Identity, bias=nbap, scale=rsap)
        sd.step("scalar", fa1, reads=x1r + nbrd + [A_[0]],
                writes=[A_[0]], name=f"f{ti}a1", dur=0.9)
        def fb1(eng):
            return eng.tensor_mul(bbuf[:, pp: pp + E], abuf[:, pp: pp + E],
                                  ln1w_r[:, :])
        sd.step("vector", fb1, reads=[A_[0], "ln1wrx", B_[0]],
                writes=[B_[0]], name=f"f{ti}b1", dur=0.6)
        def fh1(eng):
            return eng.tensor_add(abuf[:, pp + E: pp + 2 * E],
                                  bbuf[:, pp: pp + E], ln1b_r[:, :])
        sd.step("vector", fh1, reads=[B_[0], "ln1brx", A_[1]],
                writes=[A_[1]], name=f"f{ti}h1", dur=0.6)
        def fm1(eng):
            return eng.reduce_max(cv(Cm), abuf[:, pp + E: pp + 2 * E], AX.X,
                                  apply_absolute_value=True)
        sd.step("vector", fm1, reads=[A_[1], f"cv{Cm}"], writes=[f"cv{Cm}"],
                name=f"f{ti}m1", dur=0.6)
        def fc1a(eng):
            return eng.tensor_scalar(cv(Cmp), cv(Cm), float(EPS), None, ALU.max)
        sd.step("vector", fc1a, reads=[f"cv{Cm}", f"cv{Cmp}"],
                writes=[f"cv{Cmp}"], name=f"f{ti}c1a")
        def fc1b(eng):
            return eng.reciprocal(cv(Crm), cv(Cmp))
        sd.step("vector", fc1b, reads=[f"cv{Cmp}", f"cv{Crm}"],
                writes=[f"cv{Crm}"], name=f"f{ti}c1b")
        def fc1c(eng):
            return eng.tensor_scalar(cv(Cs), cv(Crm), float(127.0),
                                     None, ALU.mult)
        sd.step("vector", fc1c, reads=[f"cv{Crm}", f"cv{Cs}"],
                writes=[f"cv{Cs}"], name=f"f{ti}c1c")
        def fi1(eng):
            return eng.reciprocal(cv(Cis), cv(Cs))
        sd.step("vector", fi1, reads=[f"cv{Cs}", f"cv{Cis}"],
                writes=[f"cv{Cis}"], name=f"f{ti}i1")
        def fq1a(eng):
            return eng.activation(ubuf[:, pp: pp + E],
                                  abuf[:, pp + E: pp + 2 * E], AF.Identity,
                                  bias=f32(0.5), scale=cv(Cs))
        sd.step("scalar", fq1a, reads=[A_[1], f"cv{Cs}", U_[0]],
                writes=[U_[0]], name=f"f{ti}q1a", dur=0.9)
        def fq1b(eng):
            return eng.activation(bbuf[:, pp: pp + E], ubuf[:, pp: pp + E],
                                  AF.Identity, bias=f32(-0.5), scale=f32(1.0))
        sd.step("scalar", fq1b, reads=[U_[0], B_[0]], writes=[B_[0]],
                name=f"f{ti}q1b", dur=0.9)
        def fq1c(eng):
            return eng.scalar_tensor_tensor(bbuf[:, pp + E: pp + 2 * E],
                                            abuf[:, pp + E: pp + 2 * E],
                                            cv(Cs), bbuf[:, pp: pp + E],
                                            ALU.mult, ALU.is_gt)
        sd.step("vector", fq1c, reads=[A_[1], f"cv{Cs}", B_[0], B_[1]],
                writes=[B_[1]], name=f"f{ti}q1c", dur=0.6)
        def fq1d(eng):
            return eng.scalar_tensor_tensor(abuf[:, pp: pp + E],
                                            bbuf[:, pp: pp + E], float(0.5),
                                            bbuf[:, pp + E: pp + 2 * E],
                                            ALU.subtract, ALU.add)
        sd.step("vector", fq1d, reads=[B_[0], B_[1], A_[0]],
                writes=[A_[0]], name=f"f{ti}q1d", dur=0.6)
        def fq1e(eng):
            return eng.tensor_scalar(abuf[:, pp + E: pp + 2 * E],
                                     abuf[:, pp: pp + E], cv(Cis),
                                     None, ALU.mult)
        sd.step("vector", fq1e, reads=[A_[0], f"cv{Cis}", A_[1]],
                writes=[A_[1]], name=f"f{ti}q1e", dur=0.6)
        gbq = par
        def ftq1(eng):
            r = None
            for k in range(EC):
                r = eng.transpose(pt[gbq][:, k * 128:(k + 1) * 128],
                                  abuf[:, pp + E + k * 128: pp + E + (k + 1) * 128],
                                  ident[:, :])
            return r
        sd.step("tensor", ftq1, reads=[A_[1], "identb", f"pt{gbq}"],
                writes=[f"pt{gbq}"], name=f"f{ti}tq", dur=0.9)
        def feq1(eng):
            return eng.tensor_copy(q1t[:, pq: pq + E], pt[gbq][:, 0:1024])
        sd.step("vector", feq1, reads=[f"pt{gbq}", QN], writes=[QN],
                name=f"f{ti}eq", dur=0.6)

    def emit_ffn_mm1(ti, qx):
        C = _tile_ctx(ti)
        pq, QN = C["pq"], C["QN"]
        bp = (qx % 2) * 2
        for nn in range(2):
            pn = bp + nn
            def fmm1(eng, qx=qx, nn=nn, pq=pq, pn=pn):
                c0 = qx * 1024 + nn * 512
                for k in range(EC):
                    eng.matmul(ps[pn][:, 0:512],
                               q1t[:, pq + k * 128: pq + (k + 1) * 128],
                               w1t_ap(k, c0, c0 + 512),
                               start=(k == 0), stop=False)
                return eng.matmul(ps[pn][:, 0:512], onesr[0:1, :],
                                  b1c[0:1, qx * 1024 + nn * 512:
                                          qx * 1024 + nn * 512 + 512],
                                  start=False, stop=True)
            sd.step("tensor", fmm1,
                    reads=[QN, "w1t", "onesrb", "b1crow", f"ps{pn}"],
                    writes=[f"ps{pn}"], name=f"f{ti}mm1_{qx}_{nn}", dur=2.3)

    def emit_ffn_gelu(ti, qx):
        C = _tile_ctx(ti)
        pp, AB = C["pp"], C["AB"]
        Q, HN, h2 = C["Q"], C["HN"], C["h2"]
        HNq, A_ = C["HNq"], C["A_"]
        par = C["par"]
        bp = (qx % 2) * 2
        gco = 16 + par * 16 + qx * 2
        def fgel(eng, qx=qx, gco=gco, bp=bp):
            r = None
            for nn in range(2):
                r = eng.activation(
                    h2(qx * 1024 + nn * 512,  qx * 1024 + nn * 512 + 512),
                    ps[bp + nn][:, 0:512], AF.Gelu, bias=f32(0.0),
                    scale=cvals[:, 12:13],
                    accum_out=cvx[:, gco + nn:gco + nn + 1])
            return r
        sd.step("scalar", fgel,
                reads=[f"ps{bp}", f"ps{bp + 1}", "cv12", HNq[qx]] +
                      [f"gco{gco + nn}" for nn in range(2)],
                writes=[HNq[qx]] + [f"gco{gco + nn}" for nn in range(2)],
                name=f"f{ti}gel{qx}", dur=1.0, table="gelu")
        dstq2 = 24 + par * 4 + qx
        def f2qq(eng, qx=qx, dstq2=dstq2):
            return eng.activation(abuf[:, pp + (qx % 2) * 1024:
                                       pp + (qx % 2) * 1024 + 1024],
                                  h2(qx * 1024, (qx + 1) * 1024),
                                  AF.Square, bias=f32(0.0), scale=f32(1.0),
                                  accum_out=cvx[:, dstq2:dstq2 + 1])
        sd.step("scalar", f2qq,
                reads=[HNq[qx], A_[qx % 2], f"sq2{dstq2}"],
                writes=[f"sq2{dstq2}", A_[qx % 2]], name=f"f{ti}2q{qx}",
                dur=0.9)
        if qx % 2 == 1:
            hf = qx // 2
            dstc = Q + 25 + hf
            def f2s(eng, gco=gco, dstc=dstc):
                return eng.reduce_sum(cvals[:, dstc:dstc + 1],
                                      cvx[:, gco - 2:gco + 2], AX.X)
            sd.step("vector", f2s,
                    reads=[f"gco{gco - 2 + nn}" for nn in range(4)] +
                          [f"cv{dstc}"],
                    writes=[f"cv{dstc}"], name=f"f{ti}2s{hf}")
            dstq = Q + 9 + hf
            def f2q(eng, dstq=dstq, dstq2=dstq2):
                return eng.tensor_add(cvals[:, dstq:dstq + 1],
                                      cvx[:, dstq2 - 1:dstq2],
                                      cvx[:, dstq2:dstq2 + 1])
            sd.step("vector", f2q,
                    reads=[f"sq2{dstq2 - 1}", f"sq2{dstq2}", f"cv{dstq}"],
                    writes=[f"cv{dstq}"], name=f"f{ti}2qs{hf}")

    def emit_ffn_tail(ti):
        C = _tile_ctx(ti)
        qb, tt, par, pp, pq = C["qb"], C["tt"], C["par"], C["pp"], C["pq"]
        AB, BB, UB, QN = C["AB"], C["BB"], C["UB"], C["QN"]
        Q, HN, h2 = C["Q"], C["HN"], C["h2"]
        HNh = C["HNh"]
        HNq, A_, B_, U_ = C["HNq"], C["A_"], C["B_"], C["U_"]
        cv = C["cv"]
        x1r = C["x1r"]
        def emit_stream_load(si):
            if si >= len(w2_streamed):
                return
            cc2 = w2_streamed[si]
            ssl2 = si % NSTSL
            dma("gpsimd", wst_ap(ssl2),
                w2sp_d[cc2 * 128:(cc2 + 1) * 128, :], f"dt{ssl2}",
                reads=[f"wst{ssl2}", f"wsp{cc2}"], writes=[f"wst{ssl2}"],
                name=f"f{ti}wl{si}")
        for s5 in range(NSTSL):
            emit_stream_load(s5)
        def f2mva(eng):
            eng.tensor_add(cvals[:, Q + 24:Q + 25], cvals[:, Q + 25:Q + 26], cvals[:, Q + 26:Q + 27])
            return eng.tensor_add(cvals[:, Q + 27:Q + 28], cvals[:, Q + 9:Q + 10],
                                  cvals[:, Q + 10:Q + 11])
        sd.step("vector", f2mva,
                reads=[f"cv{Q + 25}", f"cv{Q + 26}", f"cv{Q + 9}", f"cv{Q + 10}", f"cv{Q + 24}", f"cv{Q + 27}"],
                writes=[f"cv{Q + 24}", f"cv{Q + 27}"], name=f"f{ti}2ca")
        def f2mvb(eng):
            eng.tensor_scalar(cvals[:, Q + 28:Q + 29], cvals[:, Q + 24:Q + 25],
                              float(1.0 / HID), None, ALU.mult)
            return eng.tensor_scalar(cvals[:, Q + 29:Q + 30], cvals[:, Q + 27:Q + 28],
                                     float(1.0 / HID), None, ALU.mult)
        sd.step("vector", f2mvb, reads=[f"cv{Q + 24}", f"cv{Q + 27}", f"cv{Q + 28}", f"cv{Q + 29}"],
                writes=[f"cv{Q + 28}", f"cv{Q + 29}"], name=f"f{ti}2cb")
        def f2mvc(eng):
            return eng.tensor_mul(cvals[:, Q + 30:Q + 31], cvals[:, Q + 28:Q + 29],
                                  cvals[:, Q + 28:Q + 29])
        sd.step("vector", f2mvc, reads=[f"cv{Q + 28}", f"cv{Q + 30}"], writes=[f"cv{Q + 30}"],
                name=f"f{ti}2cc")
        def f2mvd(eng):
            return eng.tensor_sub(cvals[:, Q + 31:Q + 32], cvals[:, Q + 29:Q + 30],
                                  cvals[:, Q + 30:Q + 31])
        sd.step("vector", f2mvd, reads=[f"cv{Q + 29}", f"cv{Q + 30}", f"cv{Q + 31}"],
                writes=[f"cv{Q + 31}"], name=f"f{ti}2cd")
        def f2r1(eng):
            return eng.activation(cvals[:, Q + 33:Q + 34], cvals[:, Q + 31:Q + 32], AF.Ln,
                                  bias=f32(EPS), scale=f32(1.0))
        sd.step("scalar", f2r1, reads=[f"cv{Q + 31}", f"cv{Q + 33}"], writes=[f"cv{Q + 33}"],
                name=f"f{ti}2d", table="log")
        def f2r2(eng):
            return eng.activation(cvals[:, Q + 34:Q + 35], cvals[:, Q + 33:Q + 34], AF.Exp,
                                  bias=f32(0.0), scale=f32(-0.5))
        sd.step("scalar", f2r2, reads=[f"cv{Q + 33}", f"cv{Q + 34}"], writes=[f"cv{Q + 34}"],
                name=f"f{ti}2e", table="log")
        def f2nb(eng):
            return eng.scalar_tensor_tensor(cvals[:, Q + 35:Q + 36], cvals[:, Q + 28:Q + 29],
                                            -1.0, cvals[:, Q + 34:Q + 35],
                                            ALU.mult, ALU.mult)
        sd.step("vector", f2nb, reads=[f"cv{Q + 28}", f"cv{Q + 34}", f"cv{Q + 35}"],
                writes=[f"cv{Q + 35}"], name=f"f{ti}2f")
        for qq in range(4):
            a_, b_ = A_[qq % 2], B_[qq % 2]
            qo = pp + (qq % 2) * 1024
            def f2a(eng, qq=qq, qo=qo):
                return eng.activation(abuf[:, qo: qo + 1024],
                                      h2(qq * 1024, (qq + 1) * 1024),
                                      AF.Identity, bias=cvals[:, Q + 35:Q + 36],
                                      scale=cvals[:, Q + 34:Q + 35])
            sd.step("scalar", f2a,
                    reads=[HNq[qq], f"cv{Q + 35}", f"cv{Q + 34}", a_],
                    writes=[a_], name=f"f{ti}2g{qq}", dur=0.9)
            def f2b(eng, qq=qq, qo=qo):
                return eng.tensor_mul(bbuf[:, qo: qo + 1024],
                                      abuf[:, qo: qo + 1024],
                                      ln2w_r[:, qq * 1024:(qq + 1) * 1024])
            sd.step("vector", f2b, reads=[a_, "ln2wrx", b_], writes=[b_],
                    name=f"f{ti}2h{qq}", dur=0.6)
            def f2c(eng, qq=qq, qo=qo):
                return eng.tensor_add(h2(qq * 1024, (qq + 1) * 1024),
                                      bbuf[:, qo: qo + 1024],
                                      ln2b_r[:, qq * 1024:(qq + 1) * 1024])
            sd.step("vector", f2c, reads=[b_, "ln2brx", HNq[qq]],
                    writes=[HNq[qq]], name=f"f{ti}2i{qq}", dur=0.6)
            def fm2q(eng, qq=qq):
                return eng.reduce_max(
                    cvx[:, 56 + par * 4 + qq: 57 + par * 4 + qq],
                    h2(qq * 1024, (qq + 1) * 1024), AX.X,
                    apply_absolute_value=True)
            sd.step("vector", fm2q,
                    reads=[HNq[qq], f"m2p{par * 4 + qq}"],
                    writes=[f"m2p{par * 4 + qq}"], name=f"f{ti}m2_{qq}",
                    dur=0.6)
        def fm2(eng):
            return eng.reduce_max(cvals[:, Q + 40:Q + 41],
                                  cvx[:, 56 + par * 4: 60 + par * 4], AX.X,
                                  apply_absolute_value=True)
        sd.step("vector", fm2,
                reads=[f"m2p{par * 4 + k}" for k in range(4)] +
                      [f"cv{Q + 40}"],
                writes=[f"cv{Q + 40}"], name=f"f{ti}m2")
        def fsc2a(eng):
            return eng.tensor_scalar(cvals[:, Q + 41:Q + 42], cvals[:, Q + 40:Q + 41],
                                     float(EPS), None, ALU.max)
        sd.step("vector", fsc2a, reads=[f"cv{Q + 40}", f"cv{Q + 41}"], writes=[f"cv{Q + 41}"],
                name=f"f{ti}c2a")
        def fsc2b(eng):
            return eng.reciprocal(cvals[:, Q + 42:Q + 43], cvals[:, Q + 41:Q + 42])
        sd.step("vector", fsc2b, reads=[f"cv{Q + 41}", f"cv{Q + 42}"], writes=[f"cv{Q + 42}"],
                name=f"f{ti}c2b")
        def fsc2c(eng):
            return eng.tensor_scalar(cvals[:, Q + 43:Q + 44], cvals[:, Q + 42:Q + 43],
                                     float(127.0), None, ALU.mult)
        sd.step("vector", fsc2c, reads=[f"cv{Q + 42}", f"cv{Q + 43}"], writes=[f"cv{Q + 43}"],
                name=f"f{ti}c2c")
        def fis2a(eng):
            return eng.reciprocal(cvals[:, Q + 44:Q + 45], cvals[:, Q + 43:Q + 44])
        sd.step("vector", fis2a, reads=[f"cv{Q + 43}", f"cv{Q + 44}"], writes=[f"cv{Q + 44}"],
                name=f"f{ti}i2a")
        def fis2b(eng):
            return eng.tensor_scalar(cvals[:, Q + 45:Q + 46], cvals[:, Q + 44:Q + 45],
                                     cvals[:, 13:14], None, ALU.mult)
        sd.step("vector", fis2b, reads=[f"cv{Q + 44}", "cv13", f"cv{Q + 45}"],
                writes=[f"cv{Q + 45}"], name=f"f{ti}i2b")
        for qq in range(4):
            a_, b_, u_ = A_[qq % 2], B_[qq % 2], U_[qq % 2]
            qo = pp + (qq % 2) * 1024
            def fq2a(eng, qq=qq, qo=qo):
                return eng.activation(ubuf[:, qo: qo + 1024],
                                      h2(qq * 1024, (qq + 1) * 1024),
                                      AF.Identity, bias=f32(0.5),
                                      scale=cvals[:, Q + 43:Q + 44])
            sd.step("scalar", fq2a, reads=[HNq[qq], f"cv{Q + 43}", u_],
                    writes=[u_], name=f"f{ti}q2a{qq}", dur=0.9)
            def fq2b(eng, qq=qq, qo=qo):
                return eng.activation(abuf[:, qo: qo + 1024],
                                      ubuf[:, qo: qo + 1024], AF.Identity,
                                      bias=f32(-0.5), scale=f32(1.0))
            sd.step("scalar", fq2b, reads=[u_, a_], writes=[a_],
                    name=f"f{ti}q2b{qq}", dur=0.9)
            def fq2c(eng, qq=qq, qo=qo):
                return eng.scalar_tensor_tensor(
                    bbuf[:, qo: qo + 1024], h2(qq * 1024, (qq + 1) * 1024),
                    cvals[:, Q + 43:Q + 44], abuf[:, qo: qo + 1024],
                    ALU.mult, ALU.is_gt)
            sd.step("vector", fq2c, reads=[HNq[qq], f"cv{Q + 43}", a_, b_],
                    writes=[b_], name=f"f{ti}q2c{qq}", dur=0.6)
            def fq2d(eng, qq=qq, qo=qo):
                return eng.scalar_tensor_tensor(
                    h2(qq * 1024, (qq + 1) * 1024), abuf[:, qo: qo + 1024],
                    float(0.5), bbuf[:, qo: qo + 1024],
                    ALU.subtract, ALU.add)
            sd.step("vector", fq2d, reads=[a_, b_, HNq[qq]], writes=[HNq[qq]],
                    name=f"f{ti}q2d{qq}", dur=0.6)
        stream_pos = {cc: si for si, cc in enumerate(w2_streamed)}
        gorder = sorted(range(4),
                        key=lambda g2: (w2place[g2 * 8][0] != "stream", g2))
        cc_first = gorder[0] * 8
        for g in gorder:
            gb = g % 2
            pb = par
            def ftq2(eng, g=g, pb=pb):
                r = None
                for k in range(8):
                    r = eng.transpose(
                        pt[pb][:, k * 128:(k + 1) * 128],
                        h2(g * 1024 + k * 128,  g * 1024 + (k + 1) * 128),
                        ident[:, :])
                return r
            sd.step("tensor", ftq2, reads=[HNq[g], "identb", f"pt{pb}"],
                    writes=[f"pt{pb}"], name=f"f{ti}t2_{g}", dur=0.9)
            def feq2(eng, gb=gb, pb=pb):
                return eng.tensor_copy(q2t[:, gb * 1024:(gb + 1) * 1024],
                                       pt[pb][:, 0:1024])
            sd.step("vector", feq2, reads=[f"pt{pb}", f"q2t{gb}"],
                    writes=[f"q2t{gb}"], name=f"f{ti}e2_{g}", dur=0.6)
            for kp in range(4):
                rres = []
                raps = []
                for k in (2 * kp, 2 * kp + 1):
                    cc = g * 8 + k
                    if w2place[cc][0] == "stream":
                        ssl = stream_pos[cc] % NSTSL
                        rres.append(f"wst{ssl}")
                        raps.append(lambda c0, c1, ssl=ssl:
                                    wst_ap(ssl, c0, c1))
                    else:
                        rres.append("w2t")
                        raps.append(lambda c0, c1, cc=cc:
                                    w2_home_ap(cc, c0, c1))
                cc0 = g * 8 + 2 * kp
                def fmm2(eng, g=g, kp=kp, gb=gb, cc0=cc0, raps=raps):
                    r = None
                    for k2 in range(2):
                        for nn in range(2):
                            r = eng.matmul(
                                ps[4 + nn][:, 0:512],
                                q2t[:, gb * 1024 + (2 * kp + k2) * 128:
                                       gb * 1024 + (2 * kp + k2 + 1) * 128],
                                raps[k2](nn * 512, (nn + 1) * 512),
                                start=(cc0 + k2 == cc_first), stop=False)
                    return r
                sd.step("tensor", fmm2,
                        reads=[f"q2t{gb}", "ps4", "ps5"] + rres,
                        writes=["ps4", "ps5"], name=f"f{ti}mm2_{cc0}", dur=1.0)
                if w2place[cc0][0] == "stream":
                    emit_stream_load(stream_pos[cc0] + NSTSL)
                    emit_stream_load(stream_pos[cc0] + 1 + NSTSL)
        def fbias2(eng):
            eng.matmul(ps[4][:, 0:512], onesr[0:1, :], b2c[0:1, 0:512],
                       start=False, stop=True)
            return eng.matmul(ps[5][:, 0:512], onesr[0:1, :], b2c[0:1, 512:E],
                              start=False, stop=True)
        sd.step("tensor", fbias2,
                reads=["onesrb", "b2crow", "ps4", "ps5"],
                writes=["ps4", "ps5"], name=f"f{ti}b2", dur=0.2)
        evo = 48 + C["par"] * 2
        for bb2 in range(2):
            def fev2(eng, evo=evo, bb2=bb2):
                return eng.scalar_tensor_tensor(
                    rbuf[:, bb2 * 512:(bb2 + 1) * 512],
                    ps[4 + bb2][:, 0:512],
                    cvals[:, Q + 45:Q + 46],
                    x1_ap(qb, tt, bb2 * 512, (bb2 + 1) * 512),
                    ALU.mult, ALU.add,
                    accum_out=cvx[:, evo + bb2:evo + bb2 + 1])
            sd.step("vector", fev2,
                    reads=[f"ps{4 + bb2}", f"cv{Q + 45}", f"rb{bb2}",
                           f"evo{evo + bb2}"] + x1r,
                    writes=[f"rb{bb2}", f"evo{evo + bb2}"],
                    name=f"f{ti}ev{bb2}", dur=0.6)
        def f3a(eng, evo=evo):
            return eng.tensor_add(cvals[:, Q + 36:Q + 37],
                                  cvx[:, evo:evo + 1],
                                  cvx[:, evo + 1:evo + 2])
        sd.step("vector", f3a,
                reads=[f"evo{evo}", f"evo{evo + 1}", f"cv{Q + 36}"],
                writes=[f"cv{Q + 36}"], name=f"f{ti}3a")
        def f3b(eng):
            return eng.activation(abuf[:, pp: pp + 1024], rbuf[:, :],
                                  AF.Square, bias=f32(0.0), scale=f32(1.0),
                                  accum_out=cvals[:, Q + 37:Q + 38])
        sd.step("scalar", f3b, reads=["rb0", "rb1", A_[0], f"cv{Q + 37}"],
                writes=[f"cv{Q + 37}", A_[0]], name=f"f{ti}3b", dur=0.9)
        def f3c1(eng):
            eng.tensor_scalar(cvals[:, Q + 38:Q + 39], cvals[:, Q + 36:Q + 37],
                              float(1.0 / E), None, ALU.mult)
            return eng.tensor_scalar(cvals[:, Q + 39:Q + 40], cvals[:, Q + 37:Q + 38],
                                     float(1.0 / E), None, ALU.mult)
        sd.step("vector", f3c1, reads=[f"cv{Q + 36}", f"cv{Q + 37}", f"cv{Q + 38}", f"cv{Q + 39}"],
                writes=[f"cv{Q + 38}", f"cv{Q + 39}"], name=f"f{ti}3c1")
        def f3c2(eng):
            return eng.tensor_mul(cvals[:, Q + 46:Q + 47], cvals[:, Q + 38:Q + 39],
                                  cvals[:, Q + 38:Q + 39])
        sd.step("vector", f3c2, reads=[f"cv{Q + 38}", f"cv{Q + 46}"], writes=[f"cv{Q + 46}"],
                name=f"f{ti}3c2")
        def f3c3(eng):
            return eng.tensor_sub(cvals[:, Q + 47:Q + 48], cvals[:, Q + 39:Q + 40],
                                  cvals[:, Q + 46:Q + 47])
        sd.step("vector", f3c3, reads=[f"cv{Q + 39}", f"cv{Q + 46}", f"cv{Q + 47}"],
                writes=[f"cv{Q + 47}"], name=f"f{ti}3c3")
        def f3d(eng):
            return eng.activation(cvals[:, Q + 48:Q + 49], cvals[:, Q + 47:Q + 48], AF.Ln,
                                  bias=f32(EPS), scale=f32(1.0))
        sd.step("scalar", f3d, reads=[f"cv{Q + 47}", f"cv{Q + 48}"], writes=[f"cv{Q + 48}"],
                name=f"f{ti}3d", table="log")
        def f3e(eng):
            return eng.activation(cvals[:, Q + 49:Q + 50], cvals[:, Q + 48:Q + 49], AF.Exp,
                                  bias=f32(0.0), scale=f32(-0.5))
        sd.step("scalar", f3e, reads=[f"cv{Q + 48}", f"cv{Q + 49}"], writes=[f"cv{Q + 49}"],
                name=f"f{ti}3e", table="log")
        def f3f(eng):
            return eng.tensor_scalar(obuf[:, :], rbuf[:, :], cvals[:, Q + 38:Q + 39],
                                     cvals[:, Q + 49:Q + 50], ALU.subtract, ALU.mult)
        sd.step("vector", f3f,
                reads=["rb0", "rb1", f"cv{Q + 38}", f"cv{Q + 49}", "obuf"],
                writes=["obuf"], name=f"f{ti}3f", dur=0.6)
        row0 = qb * 512 + tt * 128
        dma("sync", out_d[row0:row0 + 128, :], obuf[:, :], "do",
            reads=["obuf"], name=f"f{ti}out")

    emit_ffn_front(0)
    for qx in range(4):
        emit_ffn_mm1(0, qx)
        emit_ffn_gelu(0, qx)
    for ti in range(NTIL):
        if ti + 1 < NTIL:
            emit_ffn_front(ti + 1)
            emit_ffn_mm1(ti + 1, 0)
        emit_ffn_tail(ti)
        if ti + 1 < NTIL:
            emit_ffn_gelu(ti + 1, 0)
            for qx in range(1, 4):
                emit_ffn_mm1(ti + 1, qx)
                emit_ffn_gelu(ti + 1, qx)

    sd.reorder(0)
    sd.finalize()

    sem_keys = sorted({(st["dma_sem"] if st["dma_sem"] is not None else st["eng"])
                       for st in sd.steps})
    import contextlib
    with contextlib.ExitStack() as stk:
        sems = {}
        for k in sem_keys:
            sems[k] = stk.enter_context(nc.semaphore("sem_" + k))
        blk = stk.enter_context(nc.Block())

        @blk.sync
        def _(sync):
            sd.emit("sync", sync, sems)
            for k2 in ("do", "dbg"):
                if k2 in sd.sem_totals:
                    sync.wait_ge(sems[k2], sd.sem_totals[k2])

        @blk.gpsimd
        def _(gpsimd):
            sd.emit("gpsimd", gpsimd, sems)

        @blk.vector
        def _(vector):
            sd.emit("vector", vector, sems)

        @blk.scalar
        def _(scalar):
            sd.emit("scalar", scalar, sems)

        @blk.tensor
        def _(tensor):
            sd.emit("tensor", tensor, sems)

    return nc


def make_consts(par):
    ident = np.eye(128, dtype=np.float32).astype(NPBF)
    onesc = np.ones((128, 64), np.float32).astype(NPBF)
    onesr = np.ones((1, 128), np.float32).astype(NPBF)
    onesf = np.ones((128, 1), np.float32)
    kk = np.arange(128)[:, None]
    jj = np.arange(64)[None, :]
    dmask = (kk <= 2 * jj + par).astype(np.float32).astype(NPBF)
    return ident, onesc, onesr, onesf, dmask


def kernel(L0_attn_x, L0_attn_qkv_w, L0_attn_qkv_b, L0_attn_out_w,
           L0_attn_out_b, L0_ffn_ln1_w, L0_ffn_ln1_b, L0_ffn_w1, L0_ffn_b1,
           L0_ffn_ln2_w, L0_ffn_ln2_b, L0_ffn_w2, L0_ffn_b2):
    x = np.ascontiguousarray(np.asarray(L0_attn_x, np.float32))
    B, S, E = x.shape
    H, D = 16, 64
    qkv_w = np.asarray(L0_attn_qkv_w, np.float32)
    qw = np.concatenate([np.ascontiguousarray(qkv_w[:, :, t * 64:(t + 1) * 64])
                         .reshape(E, H * 64) for t in range(3)], axis=1)
    qkv_b = np.asarray(L0_attn_qkv_b, np.float32)
    qb = np.stack([np.ascontiguousarray(qkv_b[:, t * 64:(t + 1) * 64])
                   .reshape(H * 64) for t in range(3)], axis=0)
    out_w = np.ascontiguousarray(np.asarray(L0_attn_out_w, np.float32)
                                 .reshape(H * D, E))
    out_b = np.asarray(L0_attn_out_b, np.float32).reshape(1, E)
    w1 = np.ascontiguousarray(np.asarray(L0_ffn_w1, np.float32))
    b1 = np.asarray(L0_ffn_b1, np.float32).reshape(1, -1)
    w2 = np.ascontiguousarray(np.asarray(L0_ffn_w2, np.float32))
    b2 = np.asarray(L0_ffn_b2, np.float32).reshape(1, -1)
    ln1w = np.asarray(L0_ffn_ln1_w, np.float32).reshape(1, -1)
    ln1b = np.asarray(L0_ffn_ln1_b, np.float32).reshape(1, -1)
    ln2w = np.asarray(L0_ffn_ln2_w, np.float32).reshape(1, -1)
    ln2b = np.asarray(L0_ffn_ln2_b, np.float32).reshape(1, -1)

    nc = build_program(S=S)
    in_maps = []
    for core in range(8):
        b, par = core // 2, core % 2
        ident, onesc, onesr, onesf, dmask = make_consts(par)
        in_maps.append(dict(
            xkv=x[b], xq=np.ascontiguousarray(x[b, par::2, :]),
            qkv_w=qw, qkvb=qb, out_w=out_w, out_b=out_b,
            w1=w1, b1=b1, w2=w2, b2=b2,
            ln1w=ln1w, ln1b=ln1b, ln2w=ln2w, ln2b=ln2b,
            ident=ident, onesc=onesc, onesr=onesr, onesf=onesf, dmask=dmask,
        ))
    res = run_bass_kernel_spmd(nc, in_maps, core_ids=list(range(8)))
    out = np.empty((B, S, E), np.float32)
    for core in range(8):
        b, par = core // 2, core % 2
        out[b, par::2, :] = res.results[core]["out"]
    return out


# version 32
# speedup vs baseline: 1.0297x; 1.0281x over previous
import numpy as np
import ml_dtypes
import concourse.bass as bass
import concourse.mybir as mybir
from concourse.bass_utils import run_bass_kernel_spmd

F32 = mybir.dt.float32
BF16 = mybir.dt.bfloat16
I8 = mybir.dt.int8
AF = mybir.ActivationFunctionType
ALU = mybir.AluOpType
AX = mybir.AxisListType
NPBF = ml_dtypes.bfloat16
f32 = np.float32

SHIFT = 16.0
EPS = 1e-5
SB_LIMIT = 229344


class Sched:

    def __init__(self):
        self.steps = []
        self.lw = {}
        self.readers = {}

    def step(self, eng, fn, reads=(), writes=(), dma_sem=None, name="",
             bulk=False, dur=0.12, lat=None, table=None):
        idx = len(self.steps)
        deps = set()
        for r in reads:
            deps.update(self.lw.get(r, {}).values())
        for w in writes:
            deps.update(self.lw.get(w, {}).values())
            deps.update(self.readers.get(w, ()))
        deps.discard(idx)
        if dma_sem is not None and lat is None:
            dur, lat = 0.1, 2.5
        st = dict(idx=idx, eng=eng, fn=fn, deps=deps, dma_sem=dma_sem,
                  name=name, needed=False, inc_val=None, bulk=bulk,
                  dur=dur, lat=lat if lat is not None else 0.0, table=table)
        self.steps.append(st)
        key = dma_sem if dma_sem is not None else eng
        for w in writes:
            self.lw[w] = {key: idx}
            self.readers[w] = []
        for r in reads:
            self.readers.setdefault(r, []).append(idx)
        return idx

    def reorder(self, fence):
        n = len(self.steps)
        succ = [[] for _ in range(n)]
        indeg = [0] * n
        LAT = 0.3
        for st in self.steps[fence:]:
            for d in st["deps"]:
                if d >= fence:
                    succ[d].append(st["idx"])
                    indeg[st["idx"]] += 1
        rank = [0.0] * n
        for st in reversed(self.steps[fence:]):
            i = st["idx"]
            r = 0.0
            for s2 in succ[i]:
                x = rank[s2] + (LAT if self.steps[s2]["eng"] != st["eng"]
                                else 0.0)
                if x > r:
                    r = x
            rank[i] = r + st["dur"] + st["lat"]
        est = [0.0] * n
        fin = [0.0] * n
        ready = {}
        engt = {}
        for st in self.steps[fence:]:
            if indeg[st["idx"]] == 0:
                ready.setdefault(st["eng"], []).append(st["idx"])
        order = []
        total = n - fence
        cur_table = {}
        TBL_COST = 0.0
        while len(order) < total:
            best = None
            for eng, lst in ready.items():
                if not lst:
                    continue
                t0 = engt.get(eng, 0.0)
                s0 = min(max(t0, est[i]) for i in lst)
                if best is None or s0 < best[0]:
                    best = (s0, eng)
            s0, eng = best
            lst = ready[eng]
            cand = [i for i in lst if max(engt.get(eng, 0.0), est[i]) <= s0 + 1e-9]
            ctab = cur_table.get(eng)
            def prio(j):
                t = self.steps[j]["table"]
                pen = TBL_COST if (t is not None and ctab is not None
                                   and t != ctab) else 0.0
                return (rank[j] - pen, -j)
            i = max(cand, key=prio)
            lst.remove(i)
            st = self.steps[i]
            start = max(engt.get(eng, 0.0), est[i])
            sw = 0.0
            if st["table"] is not None:
                if cur_table.get(eng) not in (None, st["table"]):
                    sw = TBL_COST
                cur_table[eng] = st["table"]
            f = start + st["dur"] + sw
            engt[eng] = f
            fin[i] = f + st["lat"]
            order.append(st)
            for s2 in succ[i]:
                st2 = self.steps[s2]
                x = fin[i] + (LAT if (st2["eng"] != eng
                                      or st["dma_sem"] is not None) else 0.0)
                if x > est[s2]:
                    est[s2] = x
                indeg[s2] -= 1
                if indeg[s2] == 0:
                    ready.setdefault(st2["eng"], []).append(s2)
        self.steps = self.steps[:fence] + order
        self.span_est = max(engt.values()) if engt else 0.0

    def finalize(self):
        self.by_idx = {st["idx"]: st for st in self.steps}
        for st in self.steps:
            for d in st["deps"]:
                dep = self.by_idx[d]
                if (dep["eng"] == st["eng"] == "tensor"
                        and dep["dma_sem"] is None and st["dma_sem"] is None):
                    continue
                dep["needed"] = True
        counters = {}
        for st in self.steps:
            key = st["dma_sem"] if st["dma_sem"] is not None else st["eng"]
            if st["dma_sem"] is not None:
                counters[key] = counters.get(key, 0) + 16
                st["inc_val"] = counters[key]
            elif st["needed"]:
                counters[key] = counters.get(key, 0) + 1
                st["inc_val"] = counters[key]
        self.sem_totals = dict(counters)
        self.bulk_sems = {st["dma_sem"] for st in self.steps
                         if st["dma_sem"] is not None and st["bulk"]}

    def emit(self, eng_name, eng, sems):
        watermark = {}
        for st in self.steps:
            if st["eng"] != eng_name:
                continue
            waits = {}
            for d in sorted(st["deps"]):
                dep = self.by_idx[d]
                if (dep["eng"] == eng_name == "tensor"
                        and dep["dma_sem"] is None and st["dma_sem"] is None):
                    continue
                key = dep["dma_sem"] if dep["dma_sem"] is not None else dep["eng"]
                assert dep["inc_val"] is not None, \
                    f"{st['name']} deps on {dep['name']} without inc"
                val = self.sem_totals[key] if key in self.bulk_sems else dep["inc_val"]
                waits[key] = max(waits.get(key, 0), val)
            for key in sorted(waits):
                if watermark.get(key, -1) < waits[key]:
                    eng.wait_ge(sems[key], waits[key])
                    watermark[key] = waits[key]
            inst = st["fn"](eng)
            if st["inc_val"] is not None:
                key = st["dma_sem"] if st["dma_sem"] is not None else st["eng"]
                assert inst is not None, f"{st['name']} needs inc, fn returned None"
                inst.then_inc(sems[key], 16 if st["dma_sem"] is not None else 1)


class Bump:
    def __init__(self, base):
        self.o = base

    def take(self, nbytes, name=""):
        r = self.o
        self.o = (r + nbytes + 31) & ~31
        assert self.o <= SB_LIMIT, f"SBUF overflow at {name}: {self.o}"
        return r


def build_program(S=4096, dbg=()):
    E, HID, H, D = 1024, 4096, 16, 64
    NP = H // 2
    TOK = S // 2
    NKB = S // 128
    NQB = TOK // 512
    NTIL = TOK // 128
    EC = E // 128
    HC = HID // 128
    NTG = S // 512
    WN = HID * E

    nc = bass.Bass("TRN2", target_bir_lowering=False)
    global SB_LIMIT
    SB_BASE = (nc._sbuf_addr_for_side("left") + 31) & ~31
    SB_LIMIT = nc._sbuf_addr_for_side("right")

    xkv_d = nc.dram_tensor("xkv", [S, E], F32, kind="ExternalInput")
    xq_d = nc.dram_tensor("xq", [TOK, E], F32, kind="ExternalInput")
    qkvw_d = nc.dram_tensor("qkv_w", [E, 3 * E], F32, kind="ExternalInput")
    qkvb_d = nc.dram_tensor("qkvb", [3, E], F32, kind="ExternalInput")
    outw_d = nc.dram_tensor("out_w", [E, E], F32, kind="ExternalInput")
    outb_d = nc.dram_tensor("out_b", [1, E], F32, kind="ExternalInput")
    w1_d = nc.dram_tensor("w1", [HID, E], F32, kind="ExternalInput")
    b1_d = nc.dram_tensor("b1", [1, HID], F32, kind="ExternalInput")
    w2_d = nc.dram_tensor("w2", [E, HID], F32, kind="ExternalInput")
    b2_d = nc.dram_tensor("b2", [1, E], F32, kind="ExternalInput")
    ln1w_d = nc.dram_tensor("ln1w", [1, E], F32, kind="ExternalInput")
    ln1b_d = nc.dram_tensor("ln1b", [1, E], F32, kind="ExternalInput")
    ln2w_d = nc.dram_tensor("ln2w", [1, HID], F32, kind="ExternalInput")
    ln2b_d = nc.dram_tensor("ln2b", [1, HID], F32, kind="ExternalInput")
    ident_d = nc.dram_tensor("ident", [128, 128], BF16, kind="ExternalInput")
    onesc_d = nc.dram_tensor("onesc", [128, 64], BF16, kind="ExternalInput")
    onesr_d = nc.dram_tensor("onesr", [1, 128], BF16, kind="ExternalInput")
    onesf_d = nc.dram_tensor("onesf", [128, 1], F32, kind="ExternalInput")
    dmask_d = nc.dram_tensor("dmask", [128, 64], BF16, kind="ExternalInput")
    out_d = nc.dram_tensor("out", [TOK, E], F32, kind="ExternalOutput")
    w2sp_d = nc.dram_tensor("w2spill", [HID, E], BF16)

    dbg_d = {}
    for nm, shape in dbg:
        dbg_d[nm] = nc.dram_tensor("dbg_" + nm, list(shape), F32,
                                   kind="ExternalOutput")

    def sbat(name, shape, dtype, offset):
        nbytes = int(np.prod(shape[1:])) * mybir.dt.size(dtype)
        assert offset + nbytes <= SB_LIMIT, f"{name}: {offset}+{nbytes}"
        return nc.alloc_sbuf_tensor_at(name, list(shape), dtype, offset=offset)

    KVX_BASE = SB_BASE
    kvx = sbat("kvx", [128, NKB * 2048], BF16, KVX_BASE)
    KVX_END = KVX_BASE + NKB * 4096

    def kt_ap(kb, pair, r0, r1):
        c = kb * 2048 + pair * 128
        return kvx[r0:r1, c: c + 128]

    def v_ap(kb, h):
        c = kb * 2048 + 1024 + h * 64
        return kvx[:, c: c + 64]

    def x1_ap(qb, tt, c0=0, c1=1024):
        c = 8 * qb * 2048 + tt * 1024
        return kvx[:, c + c0: c + c1]

    x1_slots = set()
    for j in range(NQB):
        x1_slots.update((8 * j, 8 * j + 1))
    free_slots = [p for p in range(NKB) if p not in x1_slots]
    w1t_pairs = []
    i = 0
    while i + 1 < len(free_slots) and len(w1t_pairs) < EC:
        if free_slots[i + 1] == free_slots[i] + 1:
            w1t_pairs.append(free_slots[i])
            i += 2
        else:
            i += 1
    used = set()
    for p in w1t_pairs:
        used.update((p, p + 1))
    w2_kvx_slots = [p for p in free_slots if p not in used]
    H2OFF = None
    for ii in range(len(w2_kvx_slots) - 1):
        if w2_kvx_slots[ii + 1] == w2_kvx_slots[ii] + 1:
            H2OFF = w2_kvx_slots[ii] * 2048
            del w2_kvx_slots[ii:ii + 2]
            break
    WST_EXTRA = []
    if len(w2_kvx_slots) >= 4:
        for _ in range(2):
            sl2 = w2_kvx_slots.pop()
            WST_EXTRA += [sl2 * 2048, sl2 * 2048 + 1024]
    NSTSL = 2 + len(WST_EXTRA)

    PASSA_EARLY = NQB >= 2
    wstA = None
    if PASSA_EARLY:
        PA_BSL = 8 * (NQB - 1) + 2
        PA_TGS = (2 * (NQB - 1), 2 * NQB - 1)
        wstA = sbat("wstA", [128, 4096], F32, KVX_BASE + PA_BSL * 4096)

    cb = Bump(KVX_END)
    ident = sbat("identb", [128, 128], BF16, cb.take(256, "id"))
    onesc = sbat("onescb", [128, 64], BF16, cb.take(128, "oc"))
    onesr = sbat("onesrb", [1, 128], BF16, cb.take(256, "or"))
    onesf = sbat("onesfb", [128, 1], F32, cb.take(32, "of"))
    dmask = sbat("dmaskb", [128, 64], BF16, cb.take(128, "dm"))
    qkvb_s = sbat("qkvb_s", [128, 3 * EC], F32, cb.take(3 * EC * 4, "qb"))
    cvals = sbat("cvals", [128, 128], F32, cb.take(512, "cv"))
    cvx = sbat("cvx", [128, 64], F32, cb.take(64 * 4, "cvx"))
    lnst = sbat("lnst", [128, 48], F32, cb.take(48 * 4, "lnst"))
    LBASE = cb.o

    a = Bump(LBASE)
    qkvw_sb = sbat("qkvw_sb", [128, EC * 3 * E], BF16, a.take(EC * 3 * E * 2, "qw"))
    xkvbf = sbat("xkvbf", [128, 2 * 4 * E], BF16, a.take(2 * 4 * E * 2, "xk"))
    xT1 = sbat("xT1", [128, EC * 512], BF16, a.take(EC * 512 * 2, "xT1"))
    vtmp = sbat("vtmp", [128, 512], BF16, a.take(512 * 2, "vt"))

    a = Bump(LBASE)
    qkvwQ = sbat("qkvwQ", [128, EC * E], BF16, a.take(EC * E * 2, "qQ"))
    outw_sb = sbat("outw_sb", [128, EC * E], BF16, a.take(EC * E * 2, "ow"))
    xqbf_off = a.take(4 * E * 2, "xq")
    xqbf = sbat("xqbf", [128, 4 * E], BF16, xqbf_off)
    xres = sbat("xres", [128, 2 * E], F32, xqbf_off)
    xT2_off = a.take(EC * 512 * 2, "xT2")
    xT2 = sbat("xT2", [128, EC * 512], BF16, xT2_off)
    trash2 = sbat("trash2", [128, E], F32, xT2_off)
    QT = sbat("QT", [128, NP * 512], BF16, a.take(NP * 512 * 2, "QT"))
    ptb = sbat("ptb", [128, 2 * 1024], BF16, a.take(2 * 1024 * 2, "pt"))
    attnT = sbat("attnT", [128, NP * 512], BF16, a.take(NP * 512 * 2, "aT"))
    recip = sbat("recip", [128, 512], F32, a.take(512 * 4, "rc"))
    outbr = sbat("outbr", [1, E], BF16, a.take(E * 2, "ob"))
    L2_END = a.o

    a = Bump(LBASE)
    wbuf = sbat("wbuf", [128, 2 * HID], F32, a.take(2 * HID * 4, "wb"))
    ybuf = sbat("ybuf", [128, 2048], F32, a.take(2048 * 4, "yb"))
    gbuf = sbat("gbuf", [128, 2048], BF16, a.take(2048 * 2, "gb"))
    ternb = sbat("ternb", [128, 2 * HID], BF16, a.take(2 * HID * 2, "tb"))
    wstage = sbat("wstage", [128, 2 * 1024], BF16, a.take(2 * 1024 * 2, "ws"))
    L3_END = a.o

    a = Bump(LBASE)
    ln2w_r = sbat("ln2w_r", [128, HID], BF16, a.take(HID * 2, "l2w"))
    ln2b_r = sbat("ln2b_r", [128, HID], BF16, a.take(HID * 2, "l2b"))
    ln1w_r = sbat("ln1w_r", [128, E], BF16, a.take(E * 2, "l1w"))
    ln1b_r = sbat("ln1b_r", [128, E], BF16, a.take(E * 2, "l1b"))
    h2b = sbat("h2b", [128, HID], BF16, a.take(HID * 2, "h2"))
    h2c = None
    abuf = sbat("abuf", [128, 4096], BF16, a.take(4096 * 2, "ab"))
    bbuf = sbat("bbuf", [128, 4096], BF16, a.take(4096 * 2, "bb"))
    ubuf = sbat("ubuf", [128, 4096], I8, a.take(4096, "ub"))
    q1t = sbat("q1t", [128, 2 * E], BF16, a.take(2 * E * 2, "q1t"))
    q2t = sbat("q2t", [128, 2 * E], BF16, a.take(2 * E * 2, "q2t"))
    rbuf = sbat("rbuf", [128, E], F32, a.take(E * 4, "rb4"))
    obuf = sbat("obuf", [128, E], F32, a.take(E * 4, "ob4"))
    wsbuf = sbat("wsbuf", [128, 2 * E], BF16, a.take(2 * E * 2, "wsb"))
    if H2OFF is None:
        h2c = sbat("h2c", [128, HID], BF16, a.take(HID * 2, "h2c"))
    n_w1_ext0 = EC - len(w1t_pairs)
    w1ext = None
    if n_w1_ext0 > 0:
        w1ext = sbat("w1ext", [128, n_w1_ext0 * HID], BF16,
                     a.take(n_w1_ext0 * HID * 2, "w1e"))
    a = Bump(max(L3_END, a.o))
    b1c = sbat("b1c", [1, HID], BF16, a.take(HID * 2, "b1c"))
    b2c = sbat("b2c", [1, E], BF16, a.take(E * 2, "b2c"))
    n_res_rb = max(0, min(HC - len(w2_kvx_slots) * 2,
                          (SB_LIMIT - a.o - 64) // (E * 2))) & ~7
    assert (len(w2_kvx_slots) * 2 + n_res_rb) % 8 == 0
    w2rb = None
    if n_res_rb > 0:
        w2rb = sbat("w2rb", [128, n_res_rb * E], BF16,
                    a.take(n_res_rb * E * 2, "w2rb"))
    L4_END = a.o

    w2place = {}
    half = 0
    for sl in w2_kvx_slots:
        for hh in range(2):
            if half < HC:
                w2place[half] = ("kvx", sl * 2048 + hh * 1024)
                half += 1
    for i2 in range(n_res_rb):
        if half < HC:
            w2place[half] = ("rb", i2)
            half += 1
    while half < HC:
        w2place[half] = ("stream", None)
        half += 1
    w2_streamed = [j for j in range(HC) if w2place[j][0] == "stream"]

    def w2_home_ap(j, c0, c1):
        kind, arg = w2place[j]
        if kind == "kvx":
            return kvx[:, arg + c0: arg + c1]
        if kind == "rb":
            return w2rb[:, arg * E + c0: arg * E + c1]
        return None

    def wst_ap(ssl, c0=0, c1=E):
        if ssl < 2:
            return wsbuf[:, ssl * E + c0: ssl * E + c1]
        off = WST_EXTRA[ssl - 2]
        return kvx[:, off + c0: off + c1]

    def w1t_ap(k, c0, c1):
        if k < len(w1t_pairs):
            base = w1t_pairs[k] * 2048
            return kvx[:, base + c0: base + c1]
        kk = k - len(w1t_pairs)
        return w1ext[:, kk * HID + c0: kk * HID + c1]

    ps = [nc.alloc_psum_tensor(f"ps{i}", [128, 512], F32) for i in range(6)]
    pt = [nc.alloc_psum_tensor(f"pt{i}", [128, 1024], BF16) for i in range(2)]

    sd = Sched()
    c0125 = f32(0.125)
    cSH = f32(-SHIFT)

    def dma(eng_name, dst, src, sem, *, reads=(), writes=(), name="",
            bulk=False, slow=False, lat=None):
        def fn(eng, dst=dst, src=src, slow=slow):
            if slow:
                return eng.dma_start(dst, src, allow_slow_non_contiguous=True)
            return eng.dma_start(dst, src)
        return sd.step(eng_name, fn, reads=reads, writes=writes, dma_sem=sem,
                       name=name, bulk=bulk, lat=lat)

    def kv_names(tg):
        return ([f"KV{tg}K{p}" for p in range(NP)] +
                [f"KV{tg}V{p}" for p in range(NP)])

    for t, d, nm in ((ident, ident_d, "identb"), (onesc, onesc_d, "onescb"),
                     (onesr, onesr_d, "onesrb"), (onesf, onesf_d, "onesfb"),
                     (dmask, dmask_d, "dmaskb")):
        dma("sync", t[:, :], d[:, :], "dc", writes=[nm],
            name="ld_" + nm, bulk=True)
    dma("sync", qkvb_s[:, :],
        qkvb_d[:, :].rearrange("t (c p) -> p (t c)", p=128),
        "dc", writes=["qkvb_s"], name="ld_qkvb", bulk=True, slow=True)
    for ec in range(EC):
        dma("gpsimd", qkvw_sb[:, ec * 3 * E:(ec + 1) * 3 * E],
            qkvw_d[ec * 128:(ec + 1) * 128, :], "dw",
            writes=[f"qkvw{ec}"], name=f"ld_qkvw{ec}", bulk=True, lat=8.0)

    for tg in range(NTG):
        sl = tg % 2
        dst_ap = bass.AP(xkvbf, sl * 4 * E, [[2 * 4 * E, 128], [E, 4], [1, E]])
        src_ap = bass.AP(xkv_d, tg * 512 * E, [[E, 128], [128 * E, 4], [1, E]])
        dma("gpsimd", dst_ap, src_ap, f"dx{sl}",
            reads=[f"xkvbf{sl}"], writes=[f"xkvbf{sl}"], name=f"p1ldx{tg}", lat=12.0)
        for ecn in range(EC):
            bk = ecn % 2
            def ftr(eng, sl=sl, ecn=ecn, bk=bk):
                r = None
                for t4 in range(4):
                    r = eng.transpose(
                        pt[bk][:, t4 * 128:(t4 + 1) * 128],
                        xkvbf[:, sl * 4 * E + t4 * E + ecn * 128:
                                 sl * 4 * E + t4 * E + (ecn + 1) * 128],
                        ident[:, :])
                return r
            sd.step("tensor", ftr,
                    reads=[f"xkvbf{sl}", "identb", f"pt{bk}"],
                    writes=[f"pt{bk}"], name=f"p1xt{tg}_{ecn}", dur=0.9)
            def fev(eng, ecn=ecn, bk=bk):
                return eng.tensor_copy(xT1[:, ecn * 512:(ecn + 1) * 512],
                                       pt[bk][:, 0:512])
            sd.step("vector", fev, reads=[f"pt{bk}"], writes=[f"xT1_{ecn}"],
                    name=f"p1xe{tg}_{ecn}", dur=0.35)
        for p in range(NP):
            for typ in (1, 2):
                bk = (p * 2 + typ - 1) % 2
                def fmm(eng, p=p, typ=typ, bk=bk):
                    r = None
                    for ec in range(EC):
                        r = eng.matmul(
                            ps[bk][:, 0:512],
                            qkvw_sb[:, ec * 3 * E + typ * E + p * 128:
                                       ec * 3 * E + typ * E + (p + 1) * 128],
                            xT1[:, ec * 512:(ec + 1) * 512],
                            start=(ec == 0), stop=(ec == EC - 1))
                    return r
                sd.step("tensor", fmm,
                        reads=[f"qkvw{e2}" for e2 in range(EC)] +
                              [f"xT1_{e2}" for e2 in range(EC)] + [f"ps{bk}"],
                        writes=[f"ps{bk}"], name=f"p1mm{tg}_{p}_{typ}", dur=2.0)
                if typ == 1:
                    def fke(eng, tg=tg, p=p, bk=bk):
                        dst = bass.AP(kvx, (tg * 4) * 2048 + p * 128,
                                      [[NKB * 2048, 128], [2048, 4], [1, 128]])
                        return eng.activation(dst, ps[bk][:, 0:512], AF.Identity,
                                              bias=qkvb_s[:, EC + p:EC + p + 1],
                                              scale=f32(1.0))
                    sd.step("scalar", fke, reads=[f"ps{bk}", "qkvb_s"],
                            writes=[f"KV{tg}K{p}"], name=f"p1ke{tg}_{p}", dur=0.45)
                else:
                    def fve(eng, p=p, bk=bk):
                        return eng.activation(
                            vtmp[:, 0:512], ps[bk][:, 0:512], AF.Identity,
                            bias=qkvb_s[:, 2 * EC + p:2 * EC + p + 1],
                            scale=f32(1.0))
                    sd.step("scalar", fve, reads=[f"ps{bk}", "qkvb_s", "vtmp"],
                            writes=["vtmp"], name=f"p1ve{tg}_{p}", dur=0.45)
                    vbk = p % 2
                    def fvt(eng, vbk=vbk):
                        r = None
                        for kq in range(4):
                            r = eng.transpose(
                                pt[vbk][:, kq * 128:(kq + 1) * 128],
                                vtmp[:, kq * 128:(kq + 1) * 128], ident[:, :])
                        return r
                    sd.step("tensor", fvt, reads=["vtmp", "identb", f"pt{vbk}"],
                            writes=[f"pt{vbk}"], name=f"p1vt{tg}_{p}", dur=0.5)
                    def fvv(eng, tg=tg, p=p, vbk=vbk):
                        dst = bass.AP(kvx, (tg * 4) * 2048 + 1024 + p * 128,
                                      [[NKB * 2048, 128], [2048, 4], [1, 128]])
                        return eng.tensor_copy(dst, pt[vbk][:, 0:512])
                    sd.step("vector", fvv, reads=[f"pt{vbk}"],
                            writes=[f"KV{tg}V{p}"], name=f"p1vv{tg}_{p}", dur=0.35)

    L1_NAMES = (["xkvbf0", "xkvbf1", "vtmp"] +
                [f"qkvw{e2}" for e2 in range(EC)] +
                [f"xT1_{e2}" for e2 in range(EC)])
    L2_NAMES = (["xqbf", "xres0", "xres1", "trash2", "ptb0", "ptb1", "recip",
                 "outbrow"] +
                [f"xT2_{e2}" for e2 in range(EC)] +
                [f"qwQ{e2}" for e2 in range(EC)] +
                [f"ow{e2}" for e2 in range(EC)] +
                [f"QT{p}" for p in range(NP)] +
                [f"attnT{p}" for p in range(NP)])
    def fbar12(eng):
        return eng.memset(cvals[:, 23:24], 0.0)
    sd.step("vector", fbar12, writes=L1_NAMES + L2_NAMES + ["cv23"], name="bar12")

    for ec in range(EC):
        dma("gpsimd", qkvwQ[:, ec * E:(ec + 1) * E],
            qkvw_d[ec * 128:(ec + 1) * 128, 0:E], "dw2",
            reads=["cv23"], writes=[f"qwQ{ec}"], name=f"ld_qwQ{ec}", bulk=True, lat=5.0)
    for ch in range(EC):
        dma("gpsimd", outw_sb[:, ch * E:(ch + 1) * E],
            outw_d[ch * 128:(ch + 1) * 128, :], "dw2",
            reads=["cv23"], writes=[f"ow{ch}"], name=f"ld_ow{ch}", bulk=True, lat=5.0)
    dma("gpsimd", outbr[0:1, :], outb_d[0:1, :], "dw2",
        reads=["cv23"], writes=["outbrow"], name="ld_outb", bulk=True)

    for qb in range(NQB - 1, -1, -1):
        nkb = 8 * (qb + 1)
        qraw = qb * 1024
        dst_ap = bass.AP(xqbf, 0, [[4 * E, 128], [E, 4], [1, E]])
        src_ap = bass.AP(xq_d, qb * 512 * E, [[E, 128], [128 * E, 4], [1, E]])
        dma("gpsimd", dst_ap, src_ap, "dq",
            reads=[], writes=["xqbf", "xres0", "xres1"], name=f"q{qb}ldx", lat=12.0)
        for ecn in range(EC):
            bk = ecn % 2
            def ftr(eng, ecn=ecn, bk=bk):
                r = None
                for t4 in range(4):
                    r = eng.transpose(
                        pt[bk][:, t4 * 128:(t4 + 1) * 128],
                        xqbf[:, t4 * E + ecn * 128: t4 * E + (ecn + 1) * 128],
                        ident[:, :])
                return r
            sd.step("tensor", ftr, reads=["xqbf", "identb", f"pt{bk}"],
                    writes=[f"pt{bk}"], name=f"q{qb}xt{ecn}", dur=0.9)
            def fev(eng, ecn=ecn, bk=bk):
                return eng.tensor_copy(xT2[:, ecn * 512:(ecn + 1) * 512],
                                       pt[bk][:, 0:512])
            sd.step("vector", fev, reads=[f"pt{bk}", "trash2"],
                    writes=[f"xT2_{ecn}"], name=f"q{qb}xe{ecn}", dur=0.35)
        for p in range(NP):
            bk = p % 2
            def fmm(eng, p=p, bk=bk):
                r = None
                for ec in range(EC):
                    r = eng.matmul(
                        ps[bk][:, 0:512],
                        qkvwQ[:, ec * E + p * 128: ec * E + (p + 1) * 128],
                        xT2[:, ec * 512:(ec + 1) * 512],
                        start=(ec == 0), stop=(ec == EC - 1))
                return r
            sd.step("tensor", fmm,
                    reads=[f"qwQ{e2}" for e2 in range(EC)] +
                          [f"xT2_{e2}" for e2 in range(EC)] + [f"ps{bk}"],
                    writes=[f"ps{bk}"], name=f"q{qb}qp{p}", dur=2.0)
            def fqe(eng, p=p, bk=bk):
                return eng.activation(QT[:, p * 512:(p + 1) * 512],
                                      ps[bk][:, 0:512], AF.Identity,
                                      bias=qkvb_s[:, p:p + 1], scale=f32(1.0))
            sd.step("scalar", fqe, reads=[f"ps{bk}", "qkvb_s", f"QT{p}"],
                    writes=[f"QT{p}"], name=f"q{qb}qe{p}", dur=0.45)
        for p in range(NP):
            def mk_scores(kb, p=p):
                off = kb * 128 - qraw
                noff = max(0, off) // 2
                sA, sB = 2 * ((kb + 1) % 2), 2 * ((kb + 1) % 2) + 1
                tgk = kb // 4
                def fsc(eng, p=p, kb=kb, noff=noff, sA=sA, sB=sB):
                    eng.matmul(ps[sA][:, noff:512], kt_ap(kb, p, 0, 64),
                               QT[0:64, p * 512 + noff:(p + 1) * 512],
                               start=True, stop=True, tile_position=(0, 0))
                    return eng.matmul(ps[sB][:, noff:512], kt_ap(kb, p, 64, 128),
                                      QT[64:128, p * 512 + noff:(p + 1) * 512],
                                      start=True, stop=True,
                                      tile_position=(64, 0))
                sd.step("tensor", fsc,
                        reads=[f"KV{tgk}K{p}", f"QT{p}", f"ps{sA}", f"ps{sB}"],
                        writes=[f"ps{sA}", f"ps{sB}"], name=f"q{qb}s{p}_{kb}", dur=0.42)
            mk_scores(0)
            for kb in range(nkb):
                off = kb * 128 - qraw
                noff = max(0, off) // 2
                sA, sB = 2 * ((kb + 1) % 2), 2 * ((kb + 1) % 2) + 1
                ptA = (kb % 2) * 1024
                ptB = ptA + 512
                tgk = kb // 4
                def fex(eng, noff=noff, sA=sA, sB=sB, ptA=ptA, ptB=ptB):
                    eng.activation(ptb[:, ptA + noff: ptA + 512],
                                   ps[sA][:, noff:512], AF.Exp,
                                   bias=cSH, scale=c0125)
                    return eng.activation(ptb[:, ptB + noff: ptB + 512],
                                          ps[sB][:, noff:512], AF.Exp,
                                          bias=cSH, scale=c0125)
                sd.step("scalar", fex,
                        reads=[f"ps{sA}", f"ps{sB}", f"ptb{kb % 2}"],
                        writes=[f"ptb{kb % 2}"], name=f"q{qb}e{p}_{kb}", dur=0.92, table="log")
                if off >= 0:
                    def fmk(eng, noff=noff, ptA=ptA, ptB=ptB):
                        eng.tensor_mul(ptb[:, ptA + noff: ptA + noff + 64],
                                       ptb[:, ptA + noff: ptA + noff + 64],
                                       dmask[:, :])
                        return eng.tensor_mul(
                            ptb[:, ptB + noff: ptB + noff + 64],
                            ptb[:, ptB + noff: ptB + noff + 64], dmask[:, :])
                    sd.step("vector", fmk, reads=["dmaskb", f"ptb{kb % 2}"],
                            writes=[f"ptb{kb % 2}"], name=f"q{qb}m{p}_{kb}", dur=0.2)
                if kb + 1 < nkb:
                    mk_scores(kb + 1)
                def fl(eng, kb=kb, nkb=nkb, noff=noff, ptA=ptA, ptB=ptB):
                    eng.matmul(ps[5][0:64, noff:512], onesc[:, :],
                               ptb[:, ptA + noff: ptA + 512],
                               start=(kb == 0), stop=(kb == nkb - 1),
                               tile_position=(0, 0), skip_group_check=True)
                    return eng.matmul(ps[5][64:128, noff:512], onesc[:, :],
                                      ptb[:, ptB + noff: ptB + 512],
                                      start=(kb == 0), stop=(kb == nkb - 1),
                                      tile_position=(0, 64),
                                      skip_group_check=True)
                sd.step("tensor", fl, reads=["onescb", f"ptb{kb % 2}", "ps5"],
                        writes=["ps5"], name=f"q{qb}l{p}_{kb}", dur=0.4)
                def fav(eng, p=p, kb=kb, nkb=nkb, noff=noff, ptA=ptA, ptB=ptB):
                    eng.matmul(ps[4][0:64, noff:512], v_ap(kb, 2 * p),
                               ptb[:, ptA + noff: ptA + 512],
                               start=(kb == 0), stop=(kb == nkb - 1),
                               tile_position=(0, 0), skip_group_check=True)
                    return eng.matmul(ps[4][64:128, noff:512],
                                      v_ap(kb, 2 * p + 1),
                                      ptb[:, ptB + noff: ptB + 512],
                                      start=(kb == 0), stop=(kb == nkb - 1),
                                      tile_position=(0, 64),
                                      skip_group_check=True)
                sd.step("tensor", fav,
                        reads=[f"KV{tgk}V{p}", f"ptb{kb % 2}", "ps4"],
                        writes=["ps4"], name=f"q{qb}av{p}_{kb}", dur=0.4)
            def fln(eng):
                return eng.activation(recip[:, :], ps[5][:, 0:512], AF.Ln,
                                      bias=f32(0.0), scale=f32(1.0))
            sd.step("scalar", fln, reads=["ps5", "recip"], writes=["recip"],
                    name=f"q{qb}rl{p}", dur=0.45, table="log")
            def fexr(eng):
                return eng.activation(recip[:, :], recip[:, :], AF.Exp,
                                      bias=f32(0.0), scale=f32(-1.0))
            sd.step("scalar", fexr, reads=["recip"], writes=["recip"],
                    name=f"q{qb}re{p}", dur=0.45, table="log")
            def fat(eng, p=p):
                return eng.tensor_mul(attnT[:, p * 512:(p + 1) * 512],
                                      ps[4][:, 0:512], recip[:, :])
            sd.step("vector", fat, reads=["ps4", "recip", f"attnT{p}"],
                    writes=[f"attnT{p}"], name=f"q{qb}at{p}", dur=0.35)
        for tt in range(4):
            sl = tt % 2
            dma("sync", xres[:, sl * E:(sl + 1) * E],
                xq_d[qb * 512 + tt * 128: qb * 512 + (tt + 1) * 128, :],
                f"dr{sl}", reads=[f"xres{sl}"],
                writes=[f"xres{sl}", "xqbf"], name=f"q{qb}xr{tt}", lat=5.0)
            for nn in range(2):
                pb = 4 + nn
                def fop(eng, tt=tt, nn=nn, pb=pb):
                    for p in range(NP):
                        eng.matmul(ps[pb][:, 0:512],
                                   attnT[:, p * 512 + tt * 128:
                                            p * 512 + (tt + 1) * 128],
                                   outw_sb[:, p * E + nn * 512:
                                              p * E + nn * 512 + 512],
                                   start=(p == 0), stop=False)
                    return eng.matmul(ps[pb][:, 0:512], onesr[0:1, :],
                                      outbr[0:1, nn * 512:(nn + 1) * 512],
                                      start=False, stop=True)
                sd.step("tensor", fop,
                        reads=[f"attnT{p}" for p in range(NP)] +
                              [f"ow{c2}" for c2 in range(EC)] +
                              ["onesrb", "outbrow", f"ps{pb}"],
                        writes=[f"ps{pb}"], name=f"q{qb}op{tt}_{nn}", dur=2.0)
            def fres(eng, sl=sl):
                eng.scalar_tensor_tensor(xres[:, sl * E: sl * E + 512],
                                         ps[4][:, 0:512], 1.0,
                                         xres[:, sl * E: sl * E + 512],
                                         ALU.bypass, ALU.add)
                return eng.scalar_tensor_tensor(
                    xres[:, sl * E + 512:(sl + 1) * E], ps[5][:, 0:512], 1.0,
                    xres[:, sl * E + 512:(sl + 1) * E], ALU.bypass, ALU.add)
            sd.step("vector", fres, reads=["ps4", "ps5", f"xres{sl}"],
                    writes=[f"xres{sl}"], name=f"q{qb}rs{tt}", dur=0.7)
            KC = (0, 118)[tt % 2]
            def fsum(eng, sl=sl, KC=KC):
                return eng.reduce_sum(cvals[:, KC:KC + 1],
                                      xres[:, sl * E:(sl + 1) * E], AX.X)
            sd.step("vector", fsum, reads=[f"xres{sl}", f"cv{KC}"],
                    writes=[f"cv{KC}"], name=f"q{qb}st{tt}a", dur=0.6)
            def fsq(eng, sl=sl, KC=KC):
                return eng.scalar_tensor_tensor(
                    trash2[:, 0:E], xres[:, sl * E:(sl + 1) * E], 1.0,
                    xres[:, sl * E:(sl + 1) * E], ALU.bypass, ALU.mult,
                    accum_out=cvals[:, KC + 1:KC + 2])
            sd.step("vector", fsq, reads=[f"xres{sl}", f"cv{KC + 1}"],
                    writes=[f"cv{KC + 1}", "trash2"] +
                           [f"xT2_{e2}" for e2 in range(EC)],
                    name=f"q{qb}st{tt}b", dur=0.7)
            def fmv1(eng, KC=KC):
                eng.tensor_scalar(cvals[:, KC + 2:KC + 3], cvals[:, KC:KC + 1],
                                  float(1.0 / E), None, ALU.mult)
                return eng.tensor_scalar(cvals[:, KC + 3:KC + 4],
                                         cvals[:, KC + 1:KC + 2],
                                         float(1.0 / E), None, ALU.mult)
            sd.step("vector", fmv1,
                    reads=[f"cv{KC}", f"cv{KC + 1}", f"cv{KC + 2}",
                           f"cv{KC + 3}"],
                    writes=[f"cv{KC + 2}", f"cv{KC + 3}"],
                    name=f"q{qb}st{tt}c1")
            def fmv2(eng, KC=KC):
                return eng.tensor_mul(cvals[:, KC + 4:KC + 5],
                                      cvals[:, KC + 2:KC + 3],
                                      cvals[:, KC + 2:KC + 3])
            sd.step("vector", fmv2, reads=[f"cv{KC + 2}", f"cv{KC + 4}"],
                    writes=[f"cv{KC + 4}"], name=f"q{qb}st{tt}c2")
            def fmv3(eng, KC=KC):
                return eng.tensor_sub(cvals[:, KC + 5:KC + 6],
                                      cvals[:, KC + 3:KC + 4],
                                      cvals[:, KC + 4:KC + 5])
            sd.step("vector", fmv3,
                    reads=[f"cv{KC + 3}", f"cv{KC + 4}", f"cv{KC + 5}"],
                    writes=[f"cv{KC + 5}"], name=f"q{qb}st{tt}c3")
            def frs1(eng, KC=KC):
                return eng.activation(cvals[:, KC + 6:KC + 7],
                                      cvals[:, KC + 5:KC + 6], AF.Ln,
                                      bias=f32(EPS), scale=f32(1.0))
            sd.step("scalar", frs1, reads=[f"cv{KC + 5}", f"cv{KC + 6}"],
                    writes=[f"cv{KC + 6}"], name=f"q{qb}st{tt}d", table="log")
            def frs2(eng, KC=KC):
                return eng.activation(cvals[:, KC + 7:KC + 8],
                                      cvals[:, KC + 6:KC + 7], AF.Exp,
                                      bias=f32(0.0), scale=f32(-0.5))
            sd.step("scalar", frs2, reads=[f"cv{KC + 6}", f"cv{KC + 7}"],
                    writes=[f"cv{KC + 7}"], name=f"q{qb}st{tt}e", table="log")
            def fx1(eng, qb=qb, tt=tt, sl=sl, KC=KC):
                rsb = bass.AP(cvals, KC + 7, [[128, 128], [0, E]])
                return eng.scalar_tensor_tensor(
                    x1_ap(qb, tt), xres[:, sl * E:(sl + 1) * E],
                    cvals[:, KC + 2:KC + 3], rsb,
                    ALU.subtract, ALU.mult,
                    accum_out=cvx[:, 4 * qb + tt: 4 * qb + tt + 1])
            sd.step("vector", fx1,
                    reads=[f"xres{sl}", f"cv{KC + 2}", f"cv{KC + 7}"] +
                          kv_names(2 * qb),
                    writes=[f"x1_{qb}_{tt}", f"cvx{4 * qb + tt}"] +
                           kv_names(2 * qb),
                    name=f"q{qb}x1_{tt}", dur=0.7)
            if "x1" in dbg_d:
                dma("gpsimd",
                    dbg_d["x1"][qb * 512 + tt * 128: qb * 512 + (tt + 1) * 128, :],
                    x1_ap(qb, tt), "dbg", reads=[f"x1_{qb}_{tt}"],
                    name=f"dbgx1_{qb}_{tt}")

    if PASSA_EARLY:
        for wn, (wd, nrows, ncols) in enumerate(((w1_d, HID, E), (w2_d, E, HID))):
            acc, p0 = (52, 53) if wn == 0 else (54, 55)
            ntile = nrows // 128
            nch = max(1, ncols // 2048)
            wch_a = ncols // nch
            ldlat = 5.0 if ncols == E else 12.0
            def fz(eng, acc=acc):
                return eng.memset(cvx[:, acc:acc + 1], 0.0)
            sd.step("vector", fz, reads=[f"cvx{acc}"], writes=[f"cvx{acc}"],
                    name=f"e{wn}z")
            for ti in range(ntile):
                if wn == 0:
                    j = ti % 4
                    dstap = wstA[:, j * 1024:(j + 1) * 1024]
                    wr = [f"wsl{j}"]
                    sem = f"dvE{j}"
                else:
                    dstap = wstA[:, :]
                    wr = [f"wsl{k}" for k in range(4)]
                    sem = "dvE0"
                kvw = []
                if wn == 0 and ti < 4:
                    kvw = kv_names(PA_TGS[0]) + kv_names(PA_TGS[1])
                dma("sync", dstap, wd[ti * 128:(ti + 1) * 128, :], sem,
                    reads=wr, writes=wr + kvw, name=f"e{wn}l{ti}", lat=ldlat)
                for aa in range(nch):
                    if wn == 0:
                        seg = wstA[:, (ti % 4) * 1024:(ti % 4 + 1) * 1024]
                    else:
                        seg = wstA[:, aa * wch_a:(aa + 1) * wch_a]
                    def fabs(eng, seg=seg, aa=aa, p0=p0):
                        return eng.activation(
                            seg, seg, AF.Abs, bias=f32(0.0), scale=f32(1.0),
                            accum_out=cvx[:, p0 + aa:p0 + aa + 1])
                    sd.step("scalar", fabs,
                            reads=wr + [f"cvx{p0 + aa}"],
                            writes=wr + [f"cvx{p0 + aa}"],
                            name=f"e{wn}abs{ti}_{aa}", dur=0.9 * wch_a / 1024)
                for aa in range(nch):
                    def facc(eng, aa=aa, acc=acc, p0=p0):
                        return eng.tensor_add(cvx[:, acc:acc + 1],
                                              cvx[:, acc:acc + 1],
                                              cvx[:, p0 + aa:p0 + aa + 1])
                    sd.step("vector", facc,
                            reads=[f"cvx{acc}", f"cvx{p0 + aa}"],
                            writes=[f"cvx{acc}"], name=f"e{wn}acc{ti}_{aa}")

    L3_NAMES = ([f"wb{j}" for j in range(8)] +
                ["ybuf", "gbuf", "ternb0", "ternb1",
                 "wstage0", "wstage1", "w1t", "w2t", "b1crow", "b2crow"] +
                [f"cv{i3}" for i3 in (8, 9, 10, 12, 13, 14, 15,
                                      114, 115, 116, 117)])
    def fbar23(eng):
        return eng.memset(cvals[:, 23:24], 0.0)
    allkv = []
    for tg2 in range(NTG):
        allkv += kv_names(tg2)
    sd.step("vector", fbar23,
            writes=L2_NAMES + L3_NAMES + allkv + ["cv23"], name="bar23")
    PH3_FENCE = len(sd.steps)

    for wn, (wd, nrows, ncols) in enumerate(((w1_d, HID, E), (w2_d, E, HID))):
        ntile = nrows // 128
        nsl = (2 * HID) // ncols
        ldlat = 5.0 if ncols == E else 12.0
        nch = max(1, ncols // 2048)
        wch_a = ncols // nch
        if not PASSA_EARLY:
            def fz(eng):
                return eng.memset(cvals[:, 8:9], 0.0)
            sd.step("vector", fz, reads=["cv8"], writes=["cv8"], name=f"w{wn}z")
            for ti in range(ntile):
                sl = ti % nsl
                grA = [f"wb{j}" for j in range(sl * ncols // 1024,
                                               (sl + 1) * ncols // 1024)]
                dma("sync", wbuf[:, sl * ncols: (sl + 1) * ncols],
                    wd[ti * 128:(ti + 1) * 128, :], f"dvA{sl}",
                    reads=grA + ["cv23"], writes=grA,
                    name=f"w{wn}lA{ti}", lat=ldlat)
                for aa in range(nch):
                    def fabs(eng, sl=sl, aa=aa, wch=wch_a, ncols=ncols):
                        return eng.activation(
                            ybuf[:, 0:wch],
                            wbuf[:, sl * ncols + aa * wch:
                                    sl * ncols + (aa + 1) * wch],
                            AF.Abs, bias=f32(0.0), scale=f32(1.0),
                            accum_out=cvals[:, 9 + aa:10 + aa])
                    grC = [f"wb{j}" for j in
                           range((sl * ncols + aa * wch_a) // 1024,
                                 (sl * ncols + (aa + 1) * wch_a) // 1024)]
                    sd.step("scalar", fabs,
                            reads=grC + [f"cv{9 + aa}", "ybuf"],
                            writes=[f"cv{9 + aa}", "ybuf"],
                            name=f"w{wn}abs{ti}_{aa}", dur=0.9 * wch_a / 1024)
                for aa in range(nch):
                    def facc(eng, aa=aa):
                        return eng.tensor_add(cvals[:, 8:9], cvals[:, 8:9],
                                              cvals[:, 9 + aa:10 + aa])
                    sd.step("vector", facc, reads=["cv8", f"cv{9 + aa}"],
                            writes=["cv8"], name=f"w{wn}acc{ti}_{aa}")
        if PASSA_EARLY:
            acc2 = 52 if wn == 0 else 54
            totsrc, totread = bass.AP(cvx, acc2, [[64, 128], [0, 128]]), f"cvx{acc2}"
        else:
            totsrc, totread = bass.AP(cvals, 8, [[128, 128], [0, 128]]), "cv8"
        def ftot(eng, totsrc=totsrc):
            return eng.matmul(ps[0][:, 0:1], totsrc, onesf[:, :],
                              start=True, stop=True)
        sd.step("tensor", ftot, reads=[totread, "onesfb", "ps0"], writes=["ps0"],
                name=f"w{wn}tot")
        def fc(eng, wn=wn):
            return eng.tensor_scalar(cvals[:, 12 + wn:13 + wn], ps[0][:, 0:1],
                                     float(1.0 / WN), float(EPS), ALU.mult, ALU.max)
        sd.step("vector", fc, reads=["ps0", f"cv1{2 + wn}"],
                writes=[f"cv1{2 + wn}"], name=f"w{wn}c")
        def frc(eng, wn=wn):
            return eng.reciprocal(cvals[:, 14 + wn:15 + wn],
                                  cvals[:, 12 + wn:13 + wn])
        sd.step("vector", frc, reads=[f"cv1{2 + wn}", f"cv1{4 + wn}"],
                writes=[f"cv1{4 + wn}"], name=f"w{wn}rc")
        def fph(eng, wn=wn):
            return eng.tensor_scalar(cvals[:, 114 + wn:115 + wn],
                                     cvals[:, 12 + wn:13 + wn], 0.5,
                                     None, ALU.mult)
        sd.step("vector", fph, reads=[f"cv1{2 + wn}", f"cv11{4 + wn}"],
                writes=[f"cv11{4 + wn}"], name=f"w{wn}ph")
        def fmh(eng, wn=wn):
            return eng.tensor_scalar(cvals[:, 116 + wn:117 + wn],
                                     cvals[:, 12 + wn:13 + wn], -0.5,
                                     None, ALU.mult)
        sd.step("vector", fmh, reads=[f"cv1{2 + wn}", f"cv11{6 + wn}"],
                writes=[f"cv11{6 + wn}"], name=f"w{wn}mh")
        for ti in range(ntile):
            sl = ti % nsl
            tn = f"ternb{ti % 2}"
            tno = (ti % 2) * HID
            grB = [f"wb{j}" for j in range(sl * ncols // 1024,
                                           (sl + 1) * ncols // 1024)]
            qeng, qsem = (("scalar", f"dvB{sl}") if ti % 2 == 0
                          else ("sync", f"dvC{sl}"))
            dma(qeng, wbuf[:, sl * ncols: (sl + 1) * ncols],
                wd[ti * 128:(ti + 1) * 128, :], qsem,
                reads=grB, writes=grB,
                name=f"w{wn}lB{ti}", lat=ldlat)
            wch = min(2048, ncols)
            for hh in range(ncols // wch):
                def fg(eng, sl=sl, hh=hh, wch=wch, wn=wn, ncols=ncols):
                    return eng.tensor_scalar(
                        gbuf[:, 0:wch],
                        wbuf[:, sl * ncols + hh * wch:
                                sl * ncols + (hh + 1) * wch],
                        cvals[:, 116 + wn:117 + wn], None, ALU.is_lt)
                grD = [f"wb{j}" for j in
                       range((sl * ncols + hh * wch) // 1024,
                             (sl * ncols + (hh + 1) * wch) // 1024)]
                sd.step("vector", fg,
                        reads=grD + [f"cv11{6 + wn}", "gbuf"],
                        writes=["gbuf"], name=f"w{wn}g{ti}_{hh}",
                        dur=0.65 * wch / 1024)
                def ftn(eng, sl=sl, hh=hh, wch=wch, wn=wn, ncols=ncols,
                        tno=tno):
                    return eng.scalar_tensor_tensor(
                        ternb[:, tno + hh * wch: tno + (hh + 1) * wch],
                        wbuf[:, sl * ncols + hh * wch:
                                sl * ncols + (hh + 1) * wch],
                        cvals[:, 114 + wn:115 + wn],
                        gbuf[:, 0:wch], ALU.is_gt, ALU.subtract)
                sd.step("vector", ftn,
                        reads=grD + [f"cv11{4 + wn}", "gbuf", tn],
                        writes=[tn], name=f"w{wn}t{ti}_{hh}",
                        dur=0.65 * wch / 1024)
            ngr = ncols // 1024
            for g in range(ngr):
                bk = (ti * ngr + g) % 2
                is_stream = (wn == 1 and w2place[g * 8][0] == "stream")
                def ftr(eng, g=g, bk=bk, tno=tno):
                    r = None
                    for k8 in range(8):
                        r = eng.transpose(
                            pt[bk][:, k8 * 128:(k8 + 1) * 128],
                            ternb[:, tno + g * 1024 + k8 * 128:
                                     tno + g * 1024 + (k8 + 1) * 128],
                            ident[:, :])
                    return r
                sd.step("tensor", ftr, reads=[tn, "identb", f"pt{bk}"],
                        writes=[f"pt{bk}"], name=f"w{wn}tr{ti}_{g}", dur=0.9)
                if is_stream:
                    wsl = (ti * ngr + g) % 2
                    def fev(eng, bk=bk, wsl=wsl):
                        return eng.activation(
                            wstage[:, wsl * 1024:(wsl + 1) * 1024],
                            pt[bk][:, 0:1024], AF.Identity,
                            bias=f32(0.0), scale=f32(1.0))
                    sd.step("scalar", fev,
                            reads=[f"pt{bk}", f"wstage{wsl}"],
                            writes=[f"wstage{wsl}"],
                            name=f"w2evs{ti}_{g}", dur=0.9)
                    goff = (g * 8 * 128) * E + ti * 128
                    dstap = bass.AP(w2sp_d, goff,
                                    [[E, 128], [128 * E, 8], [1, 128]])
                    dma("gpsimd", dstap,
                        wstage[:, wsl * 1024:(wsl + 1) * 1024], f"dsp{wsl}",
                        reads=[f"wstage{wsl}"],
                        writes=[f"wsp{g * 8 + k8}" for k8 in range(8)],
                        name=f"w2sp{ti}_{g}", lat=4.0)
                else:
                    for k8 in range(8):
                        cc = g * 8 + k8
                        if wn == 0:
                            dst = w1t_ap(cc, ti * 128, (ti + 1) * 128)
                            wres = "w1t"
                        else:
                            dst = w2_home_ap(cc, ti * 128, (ti + 1) * 128)
                            wres = "w2t"
                        eng_ev = ("vector", "scalar")[k8 % 2]
                        if eng_ev == "vector":
                            def fev(eng, dst=dst, bk=bk, k8=k8):
                                return eng.tensor_copy(
                                    dst, pt[bk][:, k8 * 128:(k8 + 1) * 128])
                        else:
                            def fev(eng, dst=dst, bk=bk, k8=k8):
                                return eng.activation(
                                    dst, pt[bk][:, k8 * 128:(k8 + 1) * 128],
                                    AF.Identity, bias=f32(0.0), scale=f32(1.0))
                        paf = (["cvx54", "wsl0", "wsl1"]
                               if PASSA_EARLY else [])
                        sd.step(eng_ev, fev, reads=[f"pt{bk}", wres] + paf,
                                writes=[wres], name=f"w{wn}ev{ti}_{cc}",
                                dur=0.16)
    for wn, (bd, ncols, dstt) in enumerate(((b1_d, HID, b1c), (b2_d, E, b2c))):
        grE = [f"wb{j}" for j in range(ncols // 1024)]
        dma("sync", wbuf[0:1, 0:ncols], bd[0:1, :], "dv0",
            reads=grE, writes=grE, name=f"b{wn}ld")
        def fb(eng, wn=wn, ncols=ncols, dstt=dstt):
            return eng.tensor_scalar(dstt[0:1, 0:ncols], wbuf[0:1, 0:ncols],
                                     cvals[0:1, 14 + wn:15 + wn], None, ALU.mult)
        sd.step("vector", fb, reads=["wb0", f"cv1{4 + wn}", f"b{wn + 1}crow"],
                writes=[f"b{wn + 1}crow"], name=f"b{wn}c")

    def fspf(eng):
        return eng.memset(cvals[0:1, 11:12], 0.0)
    sd.step("gpsimd", fspf,
            reads=[f"wsp{cc2}" for cc2 in w2_streamed], writes=["spfence"],
            name="spfence")

    L4_NAMES = (["ln2wrx", "ln2brx", "ln1wrx", "ln1brx", "q1t0", "q1t1",
                 "q2t0", "q2t1", "rb0", "rb1", "obuf"] +
                [f"{pfx}{p5}{q5}" for pfx in ("ab", "bb", "ub")
                 for p5 in range(2) for q5 in range(2)] +
                [f"h2b{p5}h{h5}" for p5 in range(2) for h5 in range(2)] +
                [f"h2b{p5}q{q5}" for p5 in range(2) for q5 in range(4)] +
                [f"m2p{k5}" for k5 in range(8)] +
                [f"wst{s5}" for s5 in range(NSTSL)] +
                [f"cv{i4}" for i4 in (0, 1, 2, 5, 6, 7, 16, 17, 18, 19, 20,
                                      21, 22)])
    def fbar34(eng):
        return eng.memset(cvals[:, 23:24], 0.0)
    sd.step("vector", fbar34,
            writes=L3_NAMES + L4_NAMES + ["cv23"] +
                   [f"xT2_{e2}" for e2 in range(EC)], name="bar34")

    for rowd, dstt, nn2, wnm in ((ln2w_d, ln2w_r, HID, "ln2wrx"),
                                 (ln2b_d, ln2b_r, HID, "ln2brx"),
                                 (ln1w_d, ln1w_r, E, "ln1wrx"),
                                 (ln1b_d, ln1b_r, E, "ln1brx")):
        srcap = bass.AP(rowd, 0, [[0, 128], [1, nn2]])
        dma("gpsimd", dstt[:, :], srcap, "dl",
            reads=["cv23"], writes=[wnm], name="ld_" + wnm, bulk=True)

    PH4_FENCE = len(sd.steps)

    LN1_OK = PASSA_EARLY and NTIL <= 16 and len(WST_EXTRA) >= 1
    if LN1_OK:
        for t6 in range(NTIL):
            qb6, tt6 = t6 // 4, t6 % 4
            kvw6 = kv_names(PA_TGS[1]) if t6 == 0 else []
            def fsb6(eng, qb6=qb6, tt6=tt6, t6=t6):
                return eng.activation(wst_ap(NSTSL - 2), x1_ap(qb6, tt6),
                                      AF.Square, bias=f32(0.0), scale=f32(1.0),
                                      accum_out=lnst[:, t6:t6 + 1])
            sd.step("scalar", fsb6,
                    reads=[f"x1_{qb6}_{tt6}", f"wst{NSTSL - 2}",
                           f"lnst{t6}"],
                    writes=[f"wst{NSTSL - 2}", f"lnst{t6}"] + kvw6,
                    name=f"pp{t6}sb", dur=0.9)
        NT = NTIL
        def fmvA(eng, NT=NT):
            eng.tensor_scalar(lnst[:, 0:NT], lnst[:, 0:NT],
                              float(1.0 / E), None, ALU.mult)
            return eng.tensor_scalar(lnst[:, 16:16 + NT],
                                     cvx[:, 0:NT],
                                     float(1.0 / E), None, ALU.mult)
        sd.step("vector", fmvA,
                reads=[f"lnst{t7}" for t7 in range(NT)] +
                      [f"cvx{t7}" for t7 in range(NT)] + ["lnb0", "lnb1"],
                writes=["lnb0", "lnb1"], name="ppmv1", dur=0.1)
        def fmvB(eng, NT=NT):
            return eng.scalar_tensor_tensor(
                lnst[:, 32:32 + NT], lnst[:, 16:16 + NT], 1.0,
                lnst[:, 16:16 + NT], ALU.bypass, ALU.mult)
        sd.step("vector", fmvB, reads=["lnb1", "lnb2"],
                writes=["lnb2"], name="ppmv2", dur=0.1)
        def fmvC(eng, NT=NT):
            return eng.tensor_sub(lnst[:, 0:NT], lnst[:, 0:NT],
                                  lnst[:, 32:32 + NT])
        sd.step("vector", fmvC, reads=["lnb0", "lnb2"],
                writes=["lnb0"], name="ppmv3", dur=0.1)
        def frsA(eng, NT=NT):
            return eng.activation(lnst[:, 32:32 + NT], lnst[:, 0:NT],
                                  AF.Ln, bias=f32(EPS), scale=f32(1.0))
        sd.step("scalar", frsA, reads=["lnb0", "lnb2"], writes=["lnb2"],
                name="pprsA", dur=0.1, table="log")
        def frsB(eng, NT=NT):
            return eng.activation(lnst[:, 0:NT], lnst[:, 32:32 + NT],
                                  AF.Exp, bias=f32(0.0), scale=f32(-0.5))
        sd.step("scalar", frsB, reads=["lnb2", "lnb0"], writes=["lnb0"],
                name="pprsB", dur=0.1, table="log")
        def fnbA(eng, NT=NT):
            return eng.scalar_tensor_tensor(
                lnst[:, 32:32 + NT], lnst[:, 16:16 + NT], -1.0,
                lnst[:, 0:NT], ALU.mult, ALU.mult)
        sd.step("vector", fnbA, reads=["lnb1", "lnb0", "lnb2"],
                writes=["lnb2"], name="ppnb", dur=0.1)

    def _tile_ctx(ti):
        qb, tt = ti // 4, ti % 4
        par = ti % 2
        pp = par * 2048
        pq = par * E
        AB, BB, UB, QN = f"abuf{par}", f"bbuf{par}", f"ubuf{par}", f"q1t{par}"
        if par == 0:
            Csum, Csq, Cmu, Cex, Cms, Cvr, Cln, Crs, Cnb = 0, 1, 2, 3, 4, 5, 6, 7, 22
            Cm, Cmp, Crm, Cs, Cis = 16, 17, 18, 19, 20
        else:
            Csum, Csq, Cmu, Cex, Cms, Cvr, Cln, Crs, Cnb = 50, 51, 52, 53, 54, 55, 56, 57, 58
            Cm, Cmp, Crm, Cs, Cis = 59, 60, 61, 62, 63
        Q = 64 * par
        HN = f"h2b{par}"
        HNh = [f"h2b{par}h0", f"h2b{par}h1"]
        HNq = [f"h2b{par}q{qq}" for qq in range(4)]
        A_ = [f"ab{par}{qq}" for qq in range(2)]
        B_ = [f"bb{par}{qq}" for qq in range(2)]
        U_ = [f"ub{par}{qq}" for qq in range(2)]
        def cv(c):
            return cvals[:, c:c + 1]
        def h2(c0, c1):
            if par == 0:
                return h2b[:, c0:c1]
            if H2OFF is not None:
                return kvx[:, H2OFF + c0: H2OFF + c1]
            return h2c[:, c0:c1]
        x1r = [f"x1_{qb}_{tt}"]
        return dict(locals())

    def emit_ffn_front(ti):
        C = _tile_ctx(ti)
        qb, tt, par, pp, pq = C["qb"], C["tt"], C["par"], C["pp"], C["pq"]
        AB, BB, UB, QN = C["AB"], C["BB"], C["UB"], C["QN"]
        A_, B_, U_ = C["A_"], C["B_"], C["U_"]
        Csum, Csq, Cmu, Cex, Cms = C["Csum"], C["Csq"], C["Cmu"], C["Cex"], C["Cms"]
        Cvr, Cln, Crs, Cnb = C["Cvr"], C["Cln"], C["Crs"], C["Cnb"]
        Cm, Cmp, Crm, Cs, Cis = C["Cm"], C["Cmp"], C["Crm"], C["Cs"], C["Cis"]
        cv = C["cv"]
        x1r = C["x1r"]
        def fsq(eng):
            return eng.activation(abuf[:, pp: pp + E], x1_ap(qb, tt),
                                  AF.Square, bias=f32(0.0), scale=f32(1.0),
                                  accum_out=cv(Csq))
        if not LN1_OK:
            sd.step("scalar", fsq, reads=x1r + [A_[0], f"cv{Csq}"],
                    writes=[f"cv{Csq}", A_[0]], name=f"f{ti}sb", dur=0.9)
        def fmv1(eng):
            eng.tensor_scalar(cv(Cmu), cvx[:, ti:ti + 1], float(1.0 / E),
                              None, ALU.mult)
            return eng.tensor_scalar(cv(Cex), cv(Csq), float(1.0 / E),
                                     None, ALU.mult)
        if not LN1_OK:
            sd.step("vector", fmv1, reads=[f"cvx{ti}", f"cv{Csq}", f"cv{Cmu}",
                                           f"cv{Cex}"],
                    writes=[f"cv{Cmu}", f"cv{Cex}"], name=f"f{ti}sc1")
        def fmv2(eng):
            return eng.tensor_mul(cv(Cms), cv(Cmu), cv(Cmu))
        if not LN1_OK:
            sd.step("vector", fmv2, reads=[f"cv{Cmu}", f"cv{Cms}"],
                    writes=[f"cv{Cms}"], name=f"f{ti}sc2")
        def fmv3(eng):
            return eng.tensor_sub(cv(Cvr), cv(Cex), cv(Cms))
        if not LN1_OK:
            sd.step("vector", fmv3, reads=[f"cv{Cex}", f"cv{Cms}", f"cv{Cvr}"],
                    writes=[f"cv{Cvr}"], name=f"f{ti}sc3")
        def frs1(eng):
            return eng.activation(cv(Cln), cv(Cvr), AF.Ln,
                                  bias=f32(EPS), scale=f32(1.0))
        if not LN1_OK:
            sd.step("scalar", frs1, reads=[f"cv{Cvr}", f"cv{Cln}"],
                    writes=[f"cv{Cln}"], name=f"f{ti}sd", table="log")
        def frs2(eng):
            return eng.activation(cv(Crs), cv(Cln), AF.Exp,
                                  bias=f32(0.0), scale=f32(-0.5))
        if not LN1_OK:
            sd.step("scalar", frs2, reads=[f"cv{Cln}", f"cv{Crs}"],
                    writes=[f"cv{Crs}"], name=f"f{ti}se", table="log")
        def fnb(eng):
            return eng.scalar_tensor_tensor(cv(Cnb), cv(Cmu), -1.0, cv(Crs),
                                            ALU.mult, ALU.mult)
        if not LN1_OK:
            sd.step("vector", fnb, reads=[f"cv{Cmu}", f"cv{Crs}", f"cv{Cnb}"],
                    writes=[f"cv{Cnb}"], name=f"f{ti}sf")
        if LN1_OK:
            nbap, rsap = lnst[:, 32 + ti:33 + ti], lnst[:, ti:ti + 1]
            nbrd = ["lnb2", "lnb0"]
        else:
            nbap, rsap = cv(Cnb), cv(Crs)
            nbrd = [f"cv{Cnb}", f"cv{Crs}"]
        def fa1(eng, nbap=nbap, rsap=rsap):
            return eng.activation(abuf[:, pp: pp + E], x1_ap(qb, tt),
                                  AF.Identity, bias=nbap, scale=rsap)
        sd.step("scalar", fa1, reads=x1r + nbrd + [A_[0]],
                writes=[A_[0]], name=f"f{ti}a1", dur=0.9)
        def fb1(eng):
            return eng.tensor_mul(bbuf[:, pp: pp + E], abuf[:, pp: pp + E],
                                  ln1w_r[:, :])
        sd.step("vector", fb1, reads=[A_[0], "ln1wrx", B_[0]],
                writes=[B_[0]], name=f"f{ti}b1", dur=0.6)
        def fh1(eng):
            return eng.tensor_add(abuf[:, pp + E: pp + 2 * E],
                                  bbuf[:, pp: pp + E], ln1b_r[:, :])
        sd.step("vector", fh1, reads=[B_[0], "ln1brx", A_[1]],
                writes=[A_[1]], name=f"f{ti}h1", dur=0.6)
        def fm1(eng):
            return eng.reduce_max(cv(Cm), abuf[:, pp + E: pp + 2 * E], AX.X,
                                  apply_absolute_value=True)
        sd.step("vector", fm1, reads=[A_[1], f"cv{Cm}"], writes=[f"cv{Cm}"],
                name=f"f{ti}m1", dur=0.6)
        def fc1a(eng):
            return eng.tensor_scalar(cv(Cmp), cv(Cm), float(EPS), None, ALU.max)
        sd.step("vector", fc1a, reads=[f"cv{Cm}", f"cv{Cmp}"],
                writes=[f"cv{Cmp}"], name=f"f{ti}c1a")
        def fc1b(eng):
            return eng.reciprocal(cv(Crm), cv(Cmp))
        sd.step("vector", fc1b, reads=[f"cv{Cmp}", f"cv{Crm}"],
                writes=[f"cv{Crm}"], name=f"f{ti}c1b")
        def fc1c(eng):
            return eng.tensor_scalar(cv(Cs), cv(Crm), float(127.0),
                                     None, ALU.mult)
        sd.step("vector", fc1c, reads=[f"cv{Crm}", f"cv{Cs}"],
                writes=[f"cv{Cs}"], name=f"f{ti}c1c")
        def fi1(eng):
            return eng.reciprocal(cv(Cis), cv(Cs))
        sd.step("vector", fi1, reads=[f"cv{Cs}", f"cv{Cis}"],
                writes=[f"cv{Cis}"], name=f"f{ti}i1")
        def fq1a(eng):
            return eng.activation(ubuf[:, pp: pp + E],
                                  abuf[:, pp + E: pp + 2 * E], AF.Identity,
                                  bias=f32(0.5), scale=cv(Cs))
        sd.step("scalar", fq1a, reads=[A_[1], f"cv{Cs}", U_[0]],
                writes=[U_[0]], name=f"f{ti}q1a", dur=0.9)
        def fq1b(eng):
            return eng.activation(bbuf[:, pp: pp + E], ubuf[:, pp: pp + E],
                                  AF.Identity, bias=f32(-0.5), scale=f32(1.0))
        sd.step("scalar", fq1b, reads=[U_[0], B_[0]], writes=[B_[0]],
                name=f"f{ti}q1b", dur=0.9)
        def fq1c(eng):
            return eng.scalar_tensor_tensor(bbuf[:, pp + E: pp + 2 * E],
                                            abuf[:, pp + E: pp + 2 * E],
                                            cv(Cs), bbuf[:, pp: pp + E],
                                            ALU.mult, ALU.is_gt)
        sd.step("vector", fq1c, reads=[A_[1], f"cv{Cs}", B_[0], B_[1]],
                writes=[B_[1]], name=f"f{ti}q1c", dur=0.6)
        def fq1d(eng):
            return eng.scalar_tensor_tensor(abuf[:, pp: pp + E],
                                            bbuf[:, pp: pp + E], float(0.5),
                                            bbuf[:, pp + E: pp + 2 * E],
                                            ALU.subtract, ALU.add)
        sd.step("vector", fq1d, reads=[B_[0], B_[1], A_[0]],
                writes=[A_[0]], name=f"f{ti}q1d", dur=0.6)
        def fq1e(eng):
            return eng.tensor_scalar(abuf[:, pp + E: pp + 2 * E],
                                     abuf[:, pp: pp + E], cv(Cis),
                                     None, ALU.mult)
        sd.step("vector", fq1e, reads=[A_[0], f"cv{Cis}", A_[1]],
                writes=[A_[1]], name=f"f{ti}q1e", dur=0.6)
        gbq = par
        def ftq1(eng):
            r = None
            for k in range(EC):
                r = eng.transpose(pt[gbq][:, k * 128:(k + 1) * 128],
                                  abuf[:, pp + E + k * 128: pp + E + (k + 1) * 128],
                                  ident[:, :])
            return r
        sd.step("tensor", ftq1, reads=[A_[1], "identb", f"pt{gbq}"],
                writes=[f"pt{gbq}"], name=f"f{ti}tq", dur=0.9)
        def feq1(eng):
            return eng.tensor_copy(q1t[:, pq: pq + E], pt[gbq][:, 0:1024])
        sd.step("vector", feq1, reads=[f"pt{gbq}", QN], writes=[QN],
                name=f"f{ti}eq", dur=0.6)

    def emit_ffn_mm1(ti, qx):
        C = _tile_ctx(ti)
        pq, QN = C["pq"], C["QN"]
        bp = (qx % 2) * 2
        for nn in range(2):
            pn = bp + nn
            def fmm1(eng, qx=qx, nn=nn, pq=pq, pn=pn):
                c0 = qx * 1024 + nn * 512
                for k in range(EC):
                    eng.matmul(ps[pn][:, 0:512],
                               q1t[:, pq + k * 128: pq + (k + 1) * 128],
                               w1t_ap(k, c0, c0 + 512),
                               start=(k == 0), stop=False)
                return eng.matmul(ps[pn][:, 0:512], onesr[0:1, :],
                                  b1c[0:1, qx * 1024 + nn * 512:
                                          qx * 1024 + nn * 512 + 512],
                                  start=False, stop=True)
            sd.step("tensor", fmm1,
                    reads=[QN, "w1t", "onesrb", "b1crow", f"ps{pn}"],
                    writes=[f"ps{pn}"], name=f"f{ti}mm1_{qx}_{nn}", dur=2.3)

    def emit_ffn_gelu(ti, qx):
        C = _tile_ctx(ti)
        pp, AB = C["pp"], C["AB"]
        Q, HN, h2 = C["Q"], C["HN"], C["h2"]
        HNq, A_ = C["HNq"], C["A_"]
        par = C["par"]
        bp = (qx % 2) * 2
        gco = 16 + par * 16 + qx * 2
        def fgel(eng, qx=qx, gco=gco, bp=bp):
            r = None
            for nn in range(2):
                r = eng.activation(
                    h2(qx * 1024 + nn * 512,  qx * 1024 + nn * 512 + 512),
                    ps[bp + nn][:, 0:512], AF.Gelu, bias=f32(0.0),
                    scale=cvals[:, 12:13],
                    accum_out=cvx[:, gco + nn:gco + nn + 1])
            return r
        sd.step("scalar", fgel,
                reads=[f"ps{bp}", f"ps{bp + 1}", "cv12", HNq[qx]] +
                      [f"gco{gco + nn}" for nn in range(2)],
                writes=[HNq[qx]] + [f"gco{gco + nn}" for nn in range(2)],
                name=f"f{ti}gel{qx}", dur=1.0, table="gelu")
        dstq2 = 24 + par * 4 + qx
        def f2qq(eng, qx=qx, dstq2=dstq2):
            return eng.activation(abuf[:, pp + (qx % 2) * 1024:
                                       pp + (qx % 2) * 1024 + 1024],
                                  h2(qx * 1024, (qx + 1) * 1024),
                                  AF.Square, bias=f32(0.0), scale=f32(1.0),
                                  accum_out=cvx[:, dstq2:dstq2 + 1])
        sd.step("scalar", f2qq,
                reads=[HNq[qx], A_[qx % 2], f"sq2{dstq2}"],
                writes=[f"sq2{dstq2}", A_[qx % 2]], name=f"f{ti}2q{qx}",
                dur=0.9)
        if qx % 2 == 1:
            hf = qx // 2
            dstc = Q + 25 + hf
            def f2s(eng, gco=gco, dstc=dstc):
                return eng.reduce_sum(cvals[:, dstc:dstc + 1],
                                      cvx[:, gco - 2:gco + 2], AX.X)
            sd.step("vector", f2s,
                    reads=[f"gco{gco - 2 + nn}" for nn in range(4)] +
                          [f"cv{dstc}"],
                    writes=[f"cv{dstc}"], name=f"f{ti}2s{hf}")
            dstq = Q + 9 + hf
            def f2q(eng, dstq=dstq, dstq2=dstq2):
                return eng.tensor_add(cvals[:, dstq:dstq + 1],
                                      cvx[:, dstq2 - 1:dstq2],
                                      cvx[:, dstq2:dstq2 + 1])
            sd.step("vector", f2q,
                    reads=[f"sq2{dstq2 - 1}", f"sq2{dstq2}", f"cv{dstq}"],
                    writes=[f"cv{dstq}"], name=f"f{ti}2qs{hf}")

    def emit_ffn_tail(ti):
        C = _tile_ctx(ti)
        qb, tt, par, pp, pq = C["qb"], C["tt"], C["par"], C["pp"], C["pq"]
        AB, BB, UB, QN = C["AB"], C["BB"], C["UB"], C["QN"]
        Q, HN, h2 = C["Q"], C["HN"], C["h2"]
        HNh = C["HNh"]
        HNq, A_, B_, U_ = C["HNq"], C["A_"], C["B_"], C["U_"]
        cv = C["cv"]
        x1r = C["x1r"]
        def emit_stream_load(si):
            if si >= len(w2_streamed):
                return
            cc2 = w2_streamed[si]
            ssl2 = si % NSTSL
            dma("gpsimd", wst_ap(ssl2),
                w2sp_d[cc2 * 128:(cc2 + 1) * 128, :], f"dt{ssl2}",
                reads=[f"wst{ssl2}", f"wsp{cc2}"], writes=[f"wst{ssl2}"],
                name=f"f{ti}wl{si}")
        for s5 in range(NSTSL):
            emit_stream_load(s5)
        def f2mva(eng):
            eng.tensor_add(cvals[:, Q + 24:Q + 25], cvals[:, Q + 25:Q + 26], cvals[:, Q + 26:Q + 27])
            return eng.tensor_add(cvals[:, Q + 27:Q + 28], cvals[:, Q + 9:Q + 10],
                                  cvals[:, Q + 10:Q + 11])
        sd.step("vector", f2mva,
                reads=[f"cv{Q + 25}", f"cv{Q + 26}", f"cv{Q + 9}", f"cv{Q + 10}", f"cv{Q + 24}", f"cv{Q + 27}"],
                writes=[f"cv{Q + 24}", f"cv{Q + 27}"], name=f"f{ti}2ca")
        def f2mvb(eng):
            eng.tensor_scalar(cvals[:, Q + 28:Q + 29], cvals[:, Q + 24:Q + 25],
                              float(1.0 / HID), None, ALU.mult)
            return eng.tensor_scalar(cvals[:, Q + 29:Q + 30], cvals[:, Q + 27:Q + 28],
                                     float(1.0 / HID), None, ALU.mult)
        sd.step("vector", f2mvb, reads=[f"cv{Q + 24}", f"cv{Q + 27}", f"cv{Q + 28}", f"cv{Q + 29}"],
                writes=[f"cv{Q + 28}", f"cv{Q + 29}"], name=f"f{ti}2cb")
        def f2mvc(eng):
            return eng.tensor_mul(cvals[:, Q + 30:Q + 31], cvals[:, Q + 28:Q + 29],
                                  cvals[:, Q + 28:Q + 29])
        sd.step("vector", f2mvc, reads=[f"cv{Q + 28}", f"cv{Q + 30}"], writes=[f"cv{Q + 30}"],
                name=f"f{ti}2cc")
        def f2mvd(eng):
            return eng.tensor_sub(cvals[:, Q + 31:Q + 32], cvals[:, Q + 29:Q + 30],
                                  cvals[:, Q + 30:Q + 31])
        sd.step("vector", f2mvd, reads=[f"cv{Q + 29}", f"cv{Q + 30}", f"cv{Q + 31}"],
                writes=[f"cv{Q + 31}"], name=f"f{ti}2cd")
        def f2r1(eng):
            return eng.activation(cvals[:, Q + 33:Q + 34], cvals[:, Q + 31:Q + 32], AF.Ln,
                                  bias=f32(EPS), scale=f32(1.0))
        sd.step("scalar", f2r1, reads=[f"cv{Q + 31}", f"cv{Q + 33}"], writes=[f"cv{Q + 33}"],
                name=f"f{ti}2d", table="log")
        def f2r2(eng):
            return eng.activation(cvals[:, Q + 34:Q + 35], cvals[:, Q + 33:Q + 34], AF.Exp,
                                  bias=f32(0.0), scale=f32(-0.5))
        sd.step("scalar", f2r2, reads=[f"cv{Q + 33}", f"cv{Q + 34}"], writes=[f"cv{Q + 34}"],
                name=f"f{ti}2e", table="log")
        def f2nb(eng):
            return eng.scalar_tensor_tensor(cvals[:, Q + 35:Q + 36], cvals[:, Q + 28:Q + 29],
                                            -1.0, cvals[:, Q + 34:Q + 35],
                                            ALU.mult, ALU.mult)
        sd.step("vector", f2nb, reads=[f"cv{Q + 28}", f"cv{Q + 34}", f"cv{Q + 35}"],
                writes=[f"cv{Q + 35}"], name=f"f{ti}2f")
        for qq in range(4):
            a_, b_ = A_[qq % 2], B_[qq % 2]
            qo = pp + (qq % 2) * 1024
            def f2a(eng, qq=qq, qo=qo):
                return eng.activation(abuf[:, qo: qo + 1024],
                                      h2(qq * 1024, (qq + 1) * 1024),
                                      AF.Identity, bias=cvals[:, Q + 35:Q + 36],
                                      scale=cvals[:, Q + 34:Q + 35])
            sd.step("scalar", f2a,
                    reads=[HNq[qq], f"cv{Q + 35}", f"cv{Q + 34}", a_],
                    writes=[a_], name=f"f{ti}2g{qq}", dur=0.9)
            def f2b(eng, qq=qq, qo=qo):
                return eng.tensor_mul(bbuf[:, qo: qo + 1024],
                                      abuf[:, qo: qo + 1024],
                                      ln2w_r[:, qq * 1024:(qq + 1) * 1024])
            sd.step("vector", f2b, reads=[a_, "ln2wrx", b_], writes=[b_],
                    name=f"f{ti}2h{qq}", dur=0.6)
            def f2c(eng, qq=qq, qo=qo):
                return eng.tensor_add(h2(qq * 1024, (qq + 1) * 1024),
                                      bbuf[:, qo: qo + 1024],
                                      ln2b_r[:, qq * 1024:(qq + 1) * 1024])
            sd.step("vector", f2c, reads=[b_, "ln2brx", HNq[qq]],
                    writes=[HNq[qq]], name=f"f{ti}2i{qq}", dur=0.6)
            def fm2q(eng, qq=qq):
                return eng.reduce_max(
                    cvx[:, 56 + par * 4 + qq: 57 + par * 4 + qq],
                    h2(qq * 1024, (qq + 1) * 1024), AX.X,
                    apply_absolute_value=True)
            sd.step("vector", fm2q,
                    reads=[HNq[qq], f"m2p{par * 4 + qq}"],
                    writes=[f"m2p{par * 4 + qq}"], name=f"f{ti}m2_{qq}",
                    dur=0.6)
        def fm2(eng):
            return eng.reduce_max(cvals[:, Q + 40:Q + 41],
                                  cvx[:, 56 + par * 4: 60 + par * 4], AX.X,
                                  apply_absolute_value=True)
        sd.step("vector", fm2,
                reads=[f"m2p{par * 4 + k}" for k in range(4)] +
                      [f"cv{Q + 40}"],
                writes=[f"cv{Q + 40}"], name=f"f{ti}m2")
        def fsc2a(eng):
            return eng.tensor_scalar(cvals[:, Q + 41:Q + 42], cvals[:, Q + 40:Q + 41],
                                     float(EPS), None, ALU.max)
        sd.step("vector", fsc2a, reads=[f"cv{Q + 40}", f"cv{Q + 41}"], writes=[f"cv{Q + 41}"],
                name=f"f{ti}c2a")
        def fsc2b(eng):
            return eng.reciprocal(cvals[:, Q + 42:Q + 43], cvals[:, Q + 41:Q + 42])
        sd.step("vector", fsc2b, reads=[f"cv{Q + 41}", f"cv{Q + 42}"], writes=[f"cv{Q + 42}"],
                name=f"f{ti}c2b")
        def fsc2c(eng):
            return eng.tensor_scalar(cvals[:, Q + 43:Q + 44], cvals[:, Q + 42:Q + 43],
                                     float(127.0), None, ALU.mult)
        sd.step("vector", fsc2c, reads=[f"cv{Q + 42}", f"cv{Q + 43}"], writes=[f"cv{Q + 43}"],
                name=f"f{ti}c2c")
        def fis2a(eng):
            return eng.reciprocal(cvals[:, Q + 44:Q + 45], cvals[:, Q + 43:Q + 44])
        sd.step("vector", fis2a, reads=[f"cv{Q + 43}", f"cv{Q + 44}"], writes=[f"cv{Q + 44}"],
                name=f"f{ti}i2a")
        def fis2b(eng):
            return eng.tensor_scalar(cvals[:, Q + 45:Q + 46], cvals[:, Q + 44:Q + 45],
                                     cvals[:, 13:14], None, ALU.mult)
        sd.step("vector", fis2b, reads=[f"cv{Q + 44}", "cv13", f"cv{Q + 45}"],
                writes=[f"cv{Q + 45}"], name=f"f{ti}i2b")
        for qq in range(4):
            a_, b_, u_ = A_[qq % 2], B_[qq % 2], U_[qq % 2]
            qo = pp + (qq % 2) * 1024
            def fq2a(eng, qq=qq, qo=qo):
                return eng.activation(ubuf[:, qo: qo + 1024],
                                      h2(qq * 1024, (qq + 1) * 1024),
                                      AF.Identity, bias=f32(0.5),
                                      scale=cvals[:, Q + 43:Q + 44])
            sd.step("scalar", fq2a, reads=[HNq[qq], f"cv{Q + 43}", u_],
                    writes=[u_], name=f"f{ti}q2a{qq}", dur=0.9)
            def fq2b(eng, qq=qq, qo=qo):
                return eng.activation(abuf[:, qo: qo + 1024],
                                      ubuf[:, qo: qo + 1024], AF.Identity,
                                      bias=f32(-0.5), scale=f32(1.0))
            sd.step("scalar", fq2b, reads=[u_, a_], writes=[a_],
                    name=f"f{ti}q2b{qq}", dur=0.9)
            def fq2c(eng, qq=qq, qo=qo):
                return eng.scalar_tensor_tensor(
                    bbuf[:, qo: qo + 1024], h2(qq * 1024, (qq + 1) * 1024),
                    cvals[:, Q + 43:Q + 44], abuf[:, qo: qo + 1024],
                    ALU.mult, ALU.is_gt)
            sd.step("vector", fq2c, reads=[HNq[qq], f"cv{Q + 43}", a_, b_],
                    writes=[b_], name=f"f{ti}q2c{qq}", dur=0.6)
            def fq2d(eng, qq=qq, qo=qo):
                return eng.scalar_tensor_tensor(
                    h2(qq * 1024, (qq + 1) * 1024), abuf[:, qo: qo + 1024],
                    float(0.5), bbuf[:, qo: qo + 1024],
                    ALU.subtract, ALU.add)
            sd.step("vector", fq2d, reads=[a_, b_, HNq[qq]], writes=[HNq[qq]],
                    name=f"f{ti}q2d{qq}", dur=0.6)
        stream_pos = {cc: si for si, cc in enumerate(w2_streamed)}
        gorder = sorted(range(4),
                        key=lambda g2: (w2place[g2 * 8][0] != "stream", g2))
        cc_first = gorder[0] * 8
        for g in gorder:
            gb = g % 2
            pb = par
            def ftq2(eng, g=g, pb=pb):
                r = None
                for k in range(8):
                    r = eng.transpose(
                        pt[pb][:, k * 128:(k + 1) * 128],
                        h2(g * 1024 + k * 128,  g * 1024 + (k + 1) * 128),
                        ident[:, :])
                return r
            sd.step("tensor", ftq2, reads=[HNq[g], "identb", f"pt{pb}"],
                    writes=[f"pt{pb}"], name=f"f{ti}t2_{g}", dur=0.9)
            def feq2(eng, gb=gb, pb=pb):
                return eng.tensor_copy(q2t[:, gb * 1024:(gb + 1) * 1024],
                                       pt[pb][:, 0:1024])
            sd.step("vector", feq2, reads=[f"pt{pb}", f"q2t{gb}"],
                    writes=[f"q2t{gb}"], name=f"f{ti}e2_{g}", dur=0.6)
            for k in range(8):
                cc = g * 8 + k
                kind = w2place[cc][0]
                if kind == "stream":
                    ssl = stream_pos[cc] % NSTSL
                    rres = [f"wst{ssl}"]
                    rhs_ap = lambda c0, c1, ssl=ssl: wst_ap(ssl, c0, c1)
                else:
                    rres = ["w2t"]
                    rhs_ap = lambda c0, c1, cc=cc: w2_home_ap(cc, c0, c1)
                def fmm2(eng, g=g, k=k, gb=gb, cc=cc, rhs_ap=rhs_ap):
                    r = None
                    for nn in range(2):
                        r = eng.matmul(ps[4 + nn][:, 0:512],
                                       q2t[:, gb * 1024 + k * 128:
                                              gb * 1024 + (k + 1) * 128],
                                       rhs_ap(nn * 512, (nn + 1) * 512),
                                       start=(cc == cc_first), stop=False)
                    return r
                sd.step("tensor", fmm2,
                        reads=[f"q2t{gb}", "ps4", "ps5"] + rres,
                        writes=["ps4", "ps5"], name=f"f{ti}mm2_{cc}", dur=0.5)
                if kind == "stream":
                    emit_stream_load(stream_pos[cc] + NSTSL)
        def fbias2(eng):
            eng.matmul(ps[4][:, 0:512], onesr[0:1, :], b2c[0:1, 0:512],
                       start=False, stop=True)
            return eng.matmul(ps[5][:, 0:512], onesr[0:1, :], b2c[0:1, 512:E],
                              start=False, stop=True)
        sd.step("tensor", fbias2,
                reads=["onesrb", "b2crow", "ps4", "ps5"],
                writes=["ps4", "ps5"], name=f"f{ti}b2", dur=0.2)
        evo = 48 + C["par"] * 2
        for bb2 in range(2):
            def fev2(eng, evo=evo, bb2=bb2):
                return eng.scalar_tensor_tensor(
                    rbuf[:, bb2 * 512:(bb2 + 1) * 512],
                    ps[4 + bb2][:, 0:512],
                    cvals[:, Q + 45:Q + 46],
                    x1_ap(qb, tt, bb2 * 512, (bb2 + 1) * 512),
                    ALU.mult, ALU.add,
                    accum_out=cvx[:, evo + bb2:evo + bb2 + 1])
            sd.step("vector", fev2,
                    reads=[f"ps{4 + bb2}", f"cv{Q + 45}", f"rb{bb2}",
                           f"evo{evo + bb2}"] + x1r,
                    writes=[f"rb{bb2}", f"evo{evo + bb2}"],
                    name=f"f{ti}ev{bb2}", dur=0.6)
        def f3a(eng, evo=evo):
            return eng.tensor_add(cvals[:, Q + 36:Q + 37],
                                  cvx[:, evo:evo + 1],
                                  cvx[:, evo + 1:evo + 2])
        sd.step("vector", f3a,
                reads=[f"evo{evo}", f"evo{evo + 1}", f"cv{Q + 36}"],
                writes=[f"cv{Q + 36}"], name=f"f{ti}3a")
        def f3b(eng):
            return eng.activation(abuf[:, pp: pp + 1024], rbuf[:, :],
                                  AF.Square, bias=f32(0.0), scale=f32(1.0),
                                  accum_out=cvals[:, Q + 37:Q + 38])
        sd.step("scalar", f3b, reads=["rb0", "rb1", A_[0], f"cv{Q + 37}"],
                writes=[f"cv{Q + 37}", A_[0]], name=f"f{ti}3b", dur=0.9)
        def f3c1(eng):
            eng.tensor_scalar(cvals[:, Q + 38:Q + 39], cvals[:, Q + 36:Q + 37],
                              float(1.0 / E), None, ALU.mult)
            return eng.tensor_scalar(cvals[:, Q + 39:Q + 40], cvals[:, Q + 37:Q + 38],
                                     float(1.0 / E), None, ALU.mult)
        sd.step("vector", f3c1, reads=[f"cv{Q + 36}", f"cv{Q + 37}", f"cv{Q + 38}", f"cv{Q + 39}"],
                writes=[f"cv{Q + 38}", f"cv{Q + 39}"], name=f"f{ti}3c1")
        def f3c2(eng):
            return eng.tensor_mul(cvals[:, Q + 46:Q + 47], cvals[:, Q + 38:Q + 39],
                                  cvals[:, Q + 38:Q + 39])
        sd.step("vector", f3c2, reads=[f"cv{Q + 38}", f"cv{Q + 46}"], writes=[f"cv{Q + 46}"],
                name=f"f{ti}3c2")
        def f3c3(eng):
            return eng.tensor_sub(cvals[:, Q + 47:Q + 48], cvals[:, Q + 39:Q + 40],
                                  cvals[:, Q + 46:Q + 47])
        sd.step("vector", f3c3, reads=[f"cv{Q + 39}", f"cv{Q + 46}", f"cv{Q + 47}"],
                writes=[f"cv{Q + 47}"], name=f"f{ti}3c3")
        def f3d(eng):
            return eng.activation(cvals[:, Q + 48:Q + 49], cvals[:, Q + 47:Q + 48], AF.Ln,
                                  bias=f32(EPS), scale=f32(1.0))
        sd.step("scalar", f3d, reads=[f"cv{Q + 47}", f"cv{Q + 48}"], writes=[f"cv{Q + 48}"],
                name=f"f{ti}3d", table="log")
        def f3e(eng):
            return eng.activation(cvals[:, Q + 49:Q + 50], cvals[:, Q + 48:Q + 49], AF.Exp,
                                  bias=f32(0.0), scale=f32(-0.5))
        sd.step("scalar", f3e, reads=[f"cv{Q + 48}", f"cv{Q + 49}"], writes=[f"cv{Q + 49}"],
                name=f"f{ti}3e", table="log")
        def f3f(eng):
            return eng.tensor_scalar(obuf[:, :], rbuf[:, :], cvals[:, Q + 38:Q + 39],
                                     cvals[:, Q + 49:Q + 50], ALU.subtract, ALU.mult)
        sd.step("vector", f3f,
                reads=["rb0", "rb1", f"cv{Q + 38}", f"cv{Q + 49}", "obuf"],
                writes=["obuf"], name=f"f{ti}3f", dur=0.6)
        row0 = qb * 512 + tt * 128
        dma("sync", out_d[row0:row0 + 128, :], obuf[:, :], "do",
            reads=["obuf"], name=f"f{ti}out")

    emit_ffn_front(0)
    for qx in range(4):
        emit_ffn_mm1(0, qx)
        emit_ffn_gelu(0, qx)
    for ti in range(NTIL):
        if ti + 1 < NTIL:
            emit_ffn_front(ti + 1)
            emit_ffn_mm1(ti + 1, 0)
        emit_ffn_tail(ti)
        if ti + 1 < NTIL:
            emit_ffn_gelu(ti + 1, 0)
            for qx in range(1, 4):
                emit_ffn_mm1(ti + 1, qx)
                emit_ffn_gelu(ti + 1, qx)

    sd.reorder(0)
    sd.finalize()

    sem_keys = sorted({(st["dma_sem"] if st["dma_sem"] is not None else st["eng"])
                       for st in sd.steps})
    import contextlib
    with contextlib.ExitStack() as stk:
        sems = {}
        for k in sem_keys:
            sems[k] = stk.enter_context(nc.semaphore("sem_" + k))
        blk = stk.enter_context(nc.Block())

        @blk.sync
        def _(sync):
            sd.emit("sync", sync, sems)
            for k2 in ("do", "dbg"):
                if k2 in sd.sem_totals:
                    sync.wait_ge(sems[k2], sd.sem_totals[k2])

        @blk.gpsimd
        def _(gpsimd):
            sd.emit("gpsimd", gpsimd, sems)

        @blk.vector
        def _(vector):
            sd.emit("vector", vector, sems)

        @blk.scalar
        def _(scalar):
            sd.emit("scalar", scalar, sems)

        @blk.tensor
        def _(tensor):
            sd.emit("tensor", tensor, sems)

    return nc


def make_consts(par):
    ident = np.eye(128, dtype=np.float32).astype(NPBF)
    onesc = np.ones((128, 64), np.float32).astype(NPBF)
    onesr = np.ones((1, 128), np.float32).astype(NPBF)
    onesf = np.ones((128, 1), np.float32)
    kk = np.arange(128)[:, None]
    jj = np.arange(64)[None, :]
    dmask = (kk <= 2 * jj + par).astype(np.float32).astype(NPBF)
    return ident, onesc, onesr, onesf, dmask


def kernel(L0_attn_x, L0_attn_qkv_w, L0_attn_qkv_b, L0_attn_out_w,
           L0_attn_out_b, L0_ffn_ln1_w, L0_ffn_ln1_b, L0_ffn_w1, L0_ffn_b1,
           L0_ffn_ln2_w, L0_ffn_ln2_b, L0_ffn_w2, L0_ffn_b2):
    x = np.ascontiguousarray(np.asarray(L0_attn_x, np.float32))
    B, S, E = x.shape
    H, D = 16, 64
    qkv_w = np.asarray(L0_attn_qkv_w, np.float32)
    qw = np.concatenate([np.ascontiguousarray(qkv_w[:, :, t * 64:(t + 1) * 64])
                         .reshape(E, H * 64) for t in range(3)], axis=1)
    qkv_b = np.asarray(L0_attn_qkv_b, np.float32)
    qb = np.stack([np.ascontiguousarray(qkv_b[:, t * 64:(t + 1) * 64])
                   .reshape(H * 64) for t in range(3)], axis=0)
    out_w = np.ascontiguousarray(np.asarray(L0_attn_out_w, np.float32)
                                 .reshape(H * D, E))
    out_b = np.asarray(L0_attn_out_b, np.float32).reshape(1, E)
    w1 = np.ascontiguousarray(np.asarray(L0_ffn_w1, np.float32))
    b1 = np.asarray(L0_ffn_b1, np.float32).reshape(1, -1)
    w2 = np.ascontiguousarray(np.asarray(L0_ffn_w2, np.float32))
    b2 = np.asarray(L0_ffn_b2, np.float32).reshape(1, -1)
    ln1w = np.asarray(L0_ffn_ln1_w, np.float32).reshape(1, -1)
    ln1b = np.asarray(L0_ffn_ln1_b, np.float32).reshape(1, -1)
    ln2w = np.asarray(L0_ffn_ln2_w, np.float32).reshape(1, -1)
    ln2b = np.asarray(L0_ffn_ln2_b, np.float32).reshape(1, -1)

    nc = build_program(S=S)
    in_maps = []
    for core in range(8):
        b, par = core // 2, core % 2
        ident, onesc, onesr, onesf, dmask = make_consts(par)
        in_maps.append(dict(
            xkv=x[b], xq=np.ascontiguousarray(x[b, par::2, :]),
            qkv_w=qw, qkvb=qb, out_w=out_w, out_b=out_b,
            w1=w1, b1=b1, w2=w2, b2=b2,
            ln1w=ln1w, ln1b=ln1b, ln2w=ln2w, ln2b=ln2b,
            ident=ident, onesc=onesc, onesr=onesr, onesf=onesf, dmask=dmask,
        ))
    res = run_bass_kernel_spmd(nc, in_maps, core_ids=list(range(8)))
    out = np.empty((B, S, E), np.float32)
    for core in range(8):
        b, par = core // 2, core % 2
        out[b, par::2, :] = res.results[core]["out"]
    return out
